# version 03
# baseline (speedup 1.0000x reference)
import contextlib
import os

os.environ.setdefault("GAUGE_NTFF_IGNORE_DMA", "1")

import numpy as np

import concourse.bass as bass
import concourse.mybir as mybir
from concourse.bass_utils import run_bass_kernel_spmd

F32 = mybir.dt.float32
F32R = mybir.dt.float32r
F16 = mybir.dt.float16
AF = mybir.ActivationFunctionType
ALU = mybir.AluOpType
AXX = mybir.AxisListType.X

CFG = dict(D=1024, HID=4096, SEQ=4096, TOK=2048, NCORE=8)
TB = 256
NEG = -9.0e15
EPS = 1e-5
RG = [[0, 4], [1, 5], [2, 6], [3, 7]]
PHASES = ("pre", "attn", "post")


class Sem:
    __slots__ = ("h", "n")

    def __init__(self, h):
        self.h = h
        self.n = 0

    def inc(self, instr, k=1):
        instr.then_inc(self.h, k)
        self.n += k
        return self.n

    def wait_at(self, eng, n):
        assert 0 <= n <= self.n, (n, self.n)
        if n > 0:
            eng.wait_ge(self.h, n)

    def wait(self, eng):
        self.wait_at(eng, self.n)


def wait_toks(eng, toks):
    if not toks:
        return
    best = {}
    for sem, val in toks:
        if val > best.get(id(sem), (None, -1))[1]:
            best[id(sem)] = (sem, val)
    for sem, val in best.values():
        sem.wait_at(eng, val)


def _ap(t, rowlen, off, dims):
    return bass.AP(t, off, [[rowlen, 128]] + [[s, c] for s, c in dims])


class Build:
    def __init__(self):
        self.D = CFG["D"]
        self.HID = CFG["HID"]
        self.SEQ = CFG["SEQ"]
        self.TOK = CFG["TOK"]
        self.D3 = 3 * self.D
        self.NT = self.TOK // 128
        self.NBLK = self.TOK // TB
        self.NDC = self.D // 128
        self.NHC = self.HID // 128
        self.NKC = self.SEQ // TB
        self.NQB = self.TOK // TB
        self.DH = min(512, self.D)
        self.NDH = self.D // self.DH
        D, SEQ, D3 = self.D, self.SEQ, self.D3
        self.LNHI = 0
        self.LNLO32 = 2 * D
        self.WB32 = 4 * D
        self.PS = 2 * D
        self.HHI = 4 * D + 2048
        self.GHI = self.HHI + max(self.NHC * TB, 4 * self.PS)
        self.KVSTG = self.HHI
        self.QKB = self.GHI
        self.KVS = self.HHI
        self.KVSL = self.HHI + 2 * self.PS
        self.QH = self.GHI
        self.QB2 = self.GHI + 4 * D
        self.ATT = self.QB2
        self.RPL = max(self.GHI + self.NHC * TB, self.QB2 + 2 * self.PS,
                       self.ATT + 2048)
        self.SCO = 0
        self.LNLO16 = 0
        self.WB16 = 2 * D
        self.HLO = 2 * D + 4096
        self.GLO = self.HLO + self.NHC * TB
        f16_need = self.GLO + self.NHC * TB
        self.FAL = max(2 * SEQ, 3 * D3, (f16_need + 1) // 2)
        self.nc = bass.Bass("TRN2", target_bir_lowering=False)

    def T(self, f):
        self.blk.tensor(f)

    def V(self, f):
        self.blk.vector(f)

    def A(self, f):
        self.blk.scalar(f)

    def G(self, f):
        self.blk.gpsimd(f)

    def SY(self, f):
        self.blk.sync(f)

    def rap(self, off, dims):
        return _ap(self.rp, self.RPL, off, dims)

    def fap(self, off, dims):
        return _ap(self.fa, self.FAL, off, dims)

    def hap(self, off, dims):
        return _ap(self.fa16, 2 * self.FAL, off, dims)

    def build(self):
        nc, D, HID, D3 = self.nc, self.D, self.HID, self.D3
        TOK, NT, NBLK = self.TOK, self.NT, self.NBLK
        self.x_d = nc.dram_tensor("x", [TOK, D], F32, kind="ExternalInput").ap()
        self.aw_d = nc.dram_tensor("aw", [D, D3], F32, kind="ExternalInput").ap()
        self.w_d = {n: nc.dram_tensor(n, list(s), F32, kind="ExternalInput").ap()
                    for n, s in (("w1", (HID, D)), ("w2", (HID, D)),
                                 ("h1", (HID, HID)), ("h2", (HID, HID)),
                                 ("w3", (D, HID)))}
        self.s_d = {n: nc.dram_tensor(n + "c", [128, w], F32, kind="ExternalInput").ap()
                    for n, w in (("s1", self.NHC), ("s2", self.NHC), ("sh1", self.NHC),
                                 ("sh2", self.NHC), ("s3", self.NDC))}
        self.ident_d = nc.dram_tensor("ident", [128, 128], F32, kind="ExternalInput").ap()
        self.qpos_d = nc.dram_tensor("qpos", [128, NT], F32, kind="ExternalInput").ap()
        self.out_d = nc.dram_tensor("out", [TOK, D], F32, kind="ExternalOutput").ap()
        self.wf_d = {n: nc.dram_tensor(n + "f", list(self.w_d[n].shape), F16,
                                       kind="Internal").ap() for n in self.w_d}
        self.awhi_d = nc.dram_tensor("awhi", [D, D3], F32R, kind="Internal").ap()
        self.awlo_d = nc.dram_tensor("awlo", [D, D3], F32R, kind="Internal").ap()
        self.km, self.ka = {}, {}
        for kind in ("kh", "kl"):
            self.km[kind] = [nc.dram_tensor(f"{kind}m{b}", [D, TB], F32R,
                                            kind="Internal").ap() for b in range(NBLK)]
            self.ka[kind] = [nc.dram_tensor(f"{kind}a{b}", [2 * D, TB], F32R,
                                            kind="Internal").ap() for b in range(NBLK)]
        for kind in ("vh", "vl"):
            self.km[kind] = [nc.dram_tensor(f"{kind}m{b}", [TB, D], F32R,
                                            kind="Internal").ap() for b in range(NBLK)]
            self.ka[kind] = [nc.dram_tensor(f"{kind}a{b}", [2 * TB, D], F32R,
                                            kind="Internal").ap() for b in range(NBLK)]

        es = contextlib.ExitStack()
        with es:
            sb = lambda name, cols, dt: es.enter_context(nc.sbuf_tensor(name, [128, cols], dt))
            self.xr = sb("xr", NT * D, F32)
            self.rp = sb("rp", self.RPL, F32R)
            self.fa = sb("fa", self.FAL, F32)
            self.scr = sb("scr", D, F32)
            self.sA = sb("sA", 512, F32)
            self.sB = sb("sB", 512, F32)
            self.red = sb("red", 64, F32)
            nsc = 4 * self.NHC + self.NDC
            self.scl = sb("scl", nsc, F32)
            self.idn = sb("idn", 128, F32)
            self.qps = sb("qps", NT, F32)
            self.iot = sb("iot", 256, F32)
            self.ps = [es.enter_context(nc.psum_tensor(f"ps{i}", [128, 512], F32))
                       for i in range(8)]
            mk = lambda n: Sem(es.enter_context(nc.semaphore(n)))
            self.dx, self.dxa, self.dg = mk("dx"), mk("dxa"), mk("dg")
            self.dct = {n: mk("dct_" + n) for n in ("w1", "w2", "h1", "h2", "w3")}
            self.dt16 = [mk(f"dt16_{i}") for i in range(8)]
            self.dqw = [mk("dqw0"), mk("dqw1")]
            self.dkv = [mk("dkv0"), mk("dkv1")]
            self.sv, self.sa, self.st, self.cc = mk("sv"), mk("sa"), mk("st"), mk("cc")
            self.WR = [mk(f"wr{i}") for i in range(8)]
            self.RD = [mk(f"rd{i}") for i in range(8)]
            self.blk = es.enter_context(nc.Block())

            self.fa16 = self.fa.bitcast(F16)
            self.fa32r = self.fa.bitcast(F32R)

            self.b16_ready = [0] * 8
            self.b32_ready = [0] * 4
            self.b16_free = [None] * 8
            self.b32_free = [None] * 4
            self.b16_next = 0
            self.b32_next = 0
            self.wqk_free = [[], []]
            self.kv_free = [[], []]
            self.ffn_floor = []
            self.q_floor = []
            self.ar1_free_dg = 0
            self.ar2_free_dg_kv = 0

            self.phase0()
            self.split_aw()
            if "pre" in PHASES:
                self.ffn_call(0)
            if "attn" in PHASES:
                self.attention()
            if "post" in PHASES:
                self.ffn_call(1)
            else:
                for tb in range(NBLK):
                    self.store_out(tb)
            self.finish()
        return nc

    def phase0(self):
        D, NT = self.D, self.NT
        NHC, NDC = self.NHC, self.NDC
        dx, sv = self.dx, self.sv
        xr, scl, idn, qps, iot = self.xr, self.scl, self.idn, self.qps, self.iot
        x_d, s_d = self.x_d, self.s_d

        def sy0(sync):
            xin = bass.AP(x_d.tensor, 0, [[D, 128], [128 * D, NT], [1, D]])
            xout = bass.AP(xr, 0, [[NT * D, 128], [D, NT], [1, D]])
            dx.inc(sync.dma_start(out=xout, in_=xin), 16)
            off = 0
            for n in ("s1", "s2", "sh1", "sh2"):
                dx.inc(sync.dma_start(out=scl[:, off:off + NHC], in_=s_d[n][:, :]), 16)
                off += NHC
            dx.inc(sync.dma_start(out=scl[:, off:off + NDC], in_=s_d["s3"][:, :]), 16)
            dx.inc(sync.dma_start(out=idn[:, :], in_=self.ident_d[:, :]), 16)
            dx.inc(sync.dma_start(out=qps[:, :], in_=self.qpos_d[:, :]), 16)
        self.SY(sy0)
        self.n_small_loads = dx.n
        self.cast_done = {}

        def g0(g):
            g.iota(iot[:, :], [[1, 256]], channel_multiplier=0,
                   allow_small_or_imprecise_dtypes=True)
            for name in ("w1", "w2", "h1", "h2", "w3"):
                src, dst = self.w_d[name], self.wf_d[name]
                rows = src.shape[0]
                step = min(rows, 1024)
                for r0 in range(0, rows, step):
                    self.cast_done[name] = self.dct[name].inc(
                        g.dma_start(out=dst[r0:r0 + step, :],
                                    in_=src[r0:r0 + step, :]), 16)
        self.G(g0)

        def v0(v):
            dx.wait_at(v, self.n_small_loads)
            s3o = 4 * NHC
            v.tensor_scalar(scl[:, s3o:s3o + NDC], scl[:, s3o:s3o + NDC], 0.5, None,
                            op0=ALU.mult)
            v.drain()
            sv.inc(v.nop())
        self.V(v0)
        self.n_x_ready = self.n_small_loads

    def split_aw(self):
        D, D3 = self.D, self.D3
        dxa, dg, sv = self.dxa, self.dg, self.sv
        for r in range(D // 128):
            nl = {}

            def syl(sync, r=r, nl=nl):
                dg.wait_at(sync, self.ar1_free_dg)
                nl["n"] = dxa.inc(sync.dma_start(
                    out=self.fap(0, [(1, D3)]),
                    in_=self.aw_d[r * 128:(r + 1) * 128, :]), 16)
            self.SY(syl)

            def vsp(v, nl=nl):
                dxa.wait_at(v, nl["n"])
                src = self.fap(0, [(1, D3)])
                hi = _ap(self.fa32r, self.FAL, D3, [(1, D3)])
                d_ = self.fap(2 * D3, [(1, D3)])
                lo = _ap(self.fa32r, self.FAL, 2 * D3, [(1, D3)])
                v.tensor_copy(hi, src)
                v.drain()
                v.tensor_tensor(d_, src, hi.bitcast(F32), op=ALU.subtract)
                v.drain()
                v.tensor_copy(lo, d_)
                v.drain()
                sv.inc(v.nop())
            self.V(vsp)
            nsv = sv.n

            def gst(g, r=r, nsv=nsv):
                sv.wait_at(g, nsv)
                hi = _ap(self.fa32r, self.FAL, D3, [(1, D3)])
                lo = _ap(self.fa32r, self.FAL, 2 * D3, [(1, D3)])
                dg.inc(g.dma_start(out=self.awhi_d[r * 128:(r + 1) * 128, :], in_=hi), 16)
                dg.inc(g.dma_start(out=self.awlo_d[r * 128:(r + 1) * 128, :], in_=lo), 16)
            self.G(gst)
            self.ar1_free_dg = dg.n
        self.n_aw_ready = dg.n

    def lnorm(self, tile, lo16):
        D = self.D
        sv, sa = self.sv, self.sa
        xr, red, scr, sA, idn = self.xr, self.red, self.scr, self.sA, self.idn
        half = tile % 2
        xcol = tile * D
        nch = max(1, D // 512)
        w = min(D, 512)

        def vln(v):
            self.dx.wait_at(v, self.n_x_ready)
            for i in range(nch):
                v.bn_stats(red[:, 6 * i:6 * i + 6],
                           _ap(xr, self.NT * D, xcol + i * w, [(1, w)]))
            v.drain()
            v.bn_aggr(red[:, 12:14], red[:, 0:6 * nch])
            v.drain()
            v.tensor_scalar(red[:, 14:15], red[:, 13:14], EPS, None, op0=ALU.add)
            v.drain()
            v.reciprocal(red[:, 15:16], red[:, 14:15])
            v.drain()
            sv.inc(v.nop())
        self.V(vln)

        def aln(a):
            sv.wait(a)
            a.activation(out=red[:, 16:17], in_=red[:, 15:16], func=AF.Sqrt)
            a.drain()
            sa.inc(a.nop())
        self.A(aln)

        def vn2(v):
            sa.wait(v)
            for b in range(8):
                self.WR[b].wait(v)
            v.tensor_scalar(scr[:, 0:D], _ap(xr, self.NT * D, xcol, [(1, D)]),
                            red[:, 12:13], red[:, 16:17],
                            op0=ALU.subtract, op1=ALU.mult)
            v.drain()
            sv.inc(v.nop())
        self.V(vn2)
        n_ln = sv.n

        for gi in range(0, self.NDC, 4):
            grp = list(range(gi, min(self.NDC, gi + 4)))
            bk = 4 + (gi // 4) % 2

            def tt(t, grp=grp, bk=bk, n_ln=n_ln):
                sv.wait_at(t, n_ln)
                self.RD[bk].wait(t)
                last = None
                for j, fc in enumerate(grp):
                    last = t.transpose(_ap(self.ps[bk], 512, j * 128, [(1, 128)]),
                                       scr[:, fc * 128:(fc + 1) * 128], idn[:, :])
                self.WR[bk].inc(last)
            self.T(tt)

            def aev(a, grp=grp, bk=bk, half=half):
                self.WR[bk].wait(a)
                dst = bass.AP(self.rp, self.LNHI + grp[0] * TB + half * 128,
                              [[self.RPL, 128], [TB, len(grp)], [1, 128]])
                i = a.activation(out=dst, in_=_ap(self.ps[bk], 512, 0,
                                                  [(1, len(grp) * 128)]), func=AF.Copy)
                self.RD[bk].inc(i)
                a.drain()
                sa.inc(a.nop())
            self.A(aev)

            def vev(v, grp=grp, bk=bk, half=half):
                sa.wait(v)
                self.dg.wait_at(v, getattr(self, "n_aw_ready", 0))
                hi_f32 = bass.AP(self.rp, self.LNHI + grp[0] * TB + half * 128,
                                 [[self.RPL, 128], [TB, len(grp)], [1, 128]]).bitcast(F32)
                i = v.tensor_tensor(sA[:, 0:len(grp) * 128],
                                    _ap(self.ps[bk], 512, 0, [(1, len(grp) * 128)]),
                                    hi_f32, op=ALU.subtract)
                self.RD[bk].inc(i)
                v.drain()
                if lo16:
                    dst = bass.AP(self.fa16, self.LNLO16 + grp[0] * TB + half * 128,
                                  [[2 * self.FAL, 128], [TB, len(grp)], [1, 128]])
                else:
                    dst = bass.AP(self.rp, self.LNLO32 + grp[0] * TB + half * 128,
                                  [[self.RPL, 128], [TB, len(grp)], [1, 128]])
                v.tensor_copy(dst, sA[:, 0:len(grp) * 128])
                v.drain()
                sv.inc(v.nop())
            self.V(vev)

    def band_load(self, wname, rows0, nrows, kc):
        s = self.b16_next % 8
        self.b16_next += 1

        def adma(a, s=s):
            wait_toks(a, ([self.b16_free[s]] if self.b16_free[s] else []) + self.ffn_floor)
            self.dg.wait_at(a, getattr(self, "n_aw_ready", 0))
            self.dct[wname].wait_at(a, self.cast_done[wname])
            n = self.dt16[s].inc(a.dma_start(
                out=self.hap(self.WB16 + s * 512, [(1, nrows)]),
                in_=self.wf_d[wname][rows0:rows0 + nrows, kc * 128:(kc + 1) * 128],
                transpose=True), 16)
            self.b16_ready[s] = n
        self.A(adma)
        return s

    def band_up(self, s16, nrows):
        u = self.b32_next % 4
        self.b32_next += 1
        sa = self.sa

        def aup(a, u=u, s16=s16):
            self.dt16[s16].wait_at(a, self.b16_ready[s16])
            wait_toks(a, ([self.b32_free[u]] if self.b32_free[u] else []) + self.ffn_floor)
            a.activation(out=self.rap(self.WB32 + u * 512, [(1, nrows)]),
                         in_=self.hap(self.WB16 + s16 * 512, [(1, nrows)]),
                         func=AF.Copy)
            a.drain()
            self.b32_ready[u] = sa.inc(a.nop())
        self.A(aup)
        return u

    def ffn_call(self, call):
        for tb in range(self.NBLK):
            for tile in (2 * tb, 2 * tb + 1):
                self.lnorm(tile, lo16=True)
            self.ln_ready = self.sv.n
            self.ffn_floor = self.ffn_floor + self.wqk_free[0] + self.wqk_free[1]
            h_ready = self.stage(tb, "w1w2")
            g_ready = self.stage(tb, "h1", src_ready=h_ready)
            h3_ready = self.stage(tb, "h2", src_ready=g_ready)
            self.stage(tb, "w3", src_ready=h3_ready)
            if call == 0:
                self.qkv_kv(tb)
            else:
                self.store_out(tb)

    def stage(self, tb, name, src_ready=None):
        D = self.D
        sv, sa, st = self.sv, self.sa, self.st
        if name == "w1w2":
            KC, OC, OCG = self.NDC, self.NHC, 2
        elif name in ("h1", "h2"):
            KC, OC, OCG = self.NHC, self.NHC, 4
        else:
            KC, OC, OCG = self.NHC, self.NDC, 4

        def mv_hi(k):
            if name == "w1w2":
                return self.rap(self.LNHI + k * TB, [(1, TB)])
            base = self.HHI if name in ("h1", "w3") else self.GHI
            return self.rap(base + k * TB, [(1, TB)])

        def mv_lo(k):
            if name == "w1w2":
                return self.hap(self.LNLO16 + k * TB, [(1, TB)])
            base = self.HLO if name in ("h1", "w3") else self.GLO
            return self.hap(base + k * TB, [(1, TB)])

        wname0 = {"w1w2": "w1", "h1": "h1", "h2": "h2", "w3": "w3"}[name]
        out_ready = [0] * OC
        war_dg = 0
        if name == "w1w2":
            war_dg = max(self.ar1_free_dg, self.ar2_free_dg_kv)
        first_write = [True]

        for g0 in range(0, OC, OCG):
            banks = [0, 1, 2, 3] if (g0 // OCG) % 2 == 0 else [4, 5, 6, 7]
            ocs = list(range(g0, min(OC, g0 + OCG)))
            for k in range(KC):
                if name == "w1w2":
                    s1 = self.band_load("w1", g0 * 128, len(ocs) * 128, k)
                    s2 = self.band_load("w2", g0 * 128, len(ocs) * 128, k)
                    u1 = self.band_up(s1, len(ocs) * 128)
                    u2 = self.band_up(s2, len(ocs) * 128)
                    pairs = [(s1, u1), (s2, u2)]
                else:
                    s1 = self.band_load(wname0, g0 * 128, len(ocs) * 128, k)
                    u1 = self.band_up(s1, len(ocs) * 128)
                    pairs = [(s1, u1)]

                def tmm(t, k=k, pairs=pairs, banks=banks, ocs=ocs, KC=KC):
                    for (s16, u32) in pairs:
                        sa.wait_at(t, self.b32_ready[u32])
                        self.dt16[s16].wait_at(t, self.b16_ready[s16])
                    if src_ready is not None:
                        sv.wait_at(t, src_ready[k])
                    else:
                        sv.wait_at(t, self.ln_ready)
                    last = None
                    last_bk = None
                    for oi, oc in enumerate(ocs):
                        for wi, (s16, u32) in enumerate(pairs):
                            bk = banks[(oi * len(pairs) + wi) % len(banks)]
                            if k == 0:
                                self.RD[bk].wait(t)
                            po = _ap(self.ps[bk], 512, 0, [(1, TB)])
                            t.matmul(po, self.rap(self.WB32 + u32 * 512 + oi * 128,
                                                  [(1, 128)]),
                                     mv_hi(k), start=(k == 0), stop=False)
                            last = t.matmul(po, self.hap(self.WB16 + s16 * 512 + oi * 128,
                                                         [(1, 128)]),
                                            mv_lo(k), start=False, stop=(k == KC - 1))
                            last_bk = bk
                            if k == KC - 1:
                                self.WR[bk].inc(last)
                    if k == KC - 1:
                        tok = (self.WR[last_bk], self.WR[last_bk].n)
                    else:
                        tok = (st, st.inc(last))
                    for (s16, u32) in pairs:
                        self.b16_free[s16] = tok
                        self.b32_free[u32] = tok
                self.T(tmm)

            for oi, oc in enumerate(ocs):
                wdg = war_dg if first_write[0] else 0
                first_write[0] = False
                out_ready[oc] = self.evac(tb, name, oc, oi, banks, wdg)
        return out_ready

    def evac(self, tb, name, oc, oi, banks, war_dg):
        sv, sa, dg = self.sv, self.sa, self.dg
        NHC = self.NHC
        if name == "w1w2":
            bka = banks[(oi * 2) % 4]
            bkb = banks[(oi * 2 + 1) % 4]

            def aev(a, oc=oc, bka=bka, bkb=bkb):
                self.WR[bka].wait(a)
                self.WR[bkb].wait(a)
                sv.wait(a)
                i1 = a.activation(out=self.sA[:, 0:TB],
                                  in_=_ap(self.ps[bka], 512, 0, [(1, TB)]),
                                  func=AF.Silu, scale=self.scl[:, oc:oc + 1])
                self.RD[bka].inc(i1)
                i2 = a.activation(out=self.sB[:, 0:TB],
                                  in_=_ap(self.ps[bkb], 512, 0, [(1, TB)]),
                                  func=AF.Copy, scale=self.scl[:, NHC + oc:NHC + oc + 1])
                self.RD[bkb].inc(i2)
                a.drain()
                sa.inc(a.nop())
            self.A(aev)

            def vev(v, oc=oc, war_dg=war_dg):
                sa.wait(v)
                if war_dg:
                    dg.wait_at(v, war_dg)
                v.tensor_tensor(self.scr[:, 0:TB], self.sA[:, 0:TB], self.sB[:, 0:TB],
                                op=ALU.mult)
                v.drain()
                hid_ = self.rap(self.HHI + oc * TB, [(1, TB)])
                v.tensor_copy(hid_, self.scr[:, 0:TB])
                v.drain()
                v.tensor_tensor(self.sA[:, 0:TB], self.scr[:, 0:TB], hid_.bitcast(F32),
                                op=ALU.subtract)
                v.drain()
                lod = self.hap(self.HLO + oc * TB, [(1, TB)])
                v.tensor_copy(lod, self.sA[:, 0:TB])
                v.drain()
                sv.inc(v.nop())
            self.V(vev)
            return sv.n
        if name in ("h1", "h2"):
            bk = banks[oi % 4]
            hi_base = self.GHI if name == "h1" else self.HHI
            lo_base = self.GLO if name == "h1" else self.HLO
            scol = (2 * NHC + oc) if name == "h1" else (3 * NHC + oc)

            def aev(a, bk=bk, scol=scol):
                self.WR[bk].wait(a)
                sv.wait(a)
                i = a.activation(out=self.scr[:, 0:TB],
                                 in_=_ap(self.ps[bk], 512, 0, [(1, TB)]),
                                 func=AF.Gelu, scale=self.scl[:, scol:scol + 1])
                self.RD[bk].inc(i)
                a.drain()
                sa.inc(a.nop())
            self.A(aev)

            def vev(v, oc=oc, hi_base=hi_base, lo_base=lo_base, war_dg=war_dg):
                sa.wait(v)
                if war_dg:
                    dg.wait_at(v, war_dg)
                hid_ = self.rap(hi_base + oc * TB, [(1, TB)])
                v.tensor_copy(hid_, self.scr[:, 0:TB])
                v.drain()
                v.tensor_tensor(self.sA[:, 0:TB], self.scr[:, 0:TB], hid_.bitcast(F32),
                                op=ALU.subtract)
                v.drain()
                lod = self.hap(lo_base + oc * TB, [(1, TB)])
                v.tensor_copy(lod, self.sA[:, 0:TB])
                v.drain()
                sv.inc(v.nop())
            self.V(vev)
            return sv.n
        bk = banks[oi % 4]
        s3o = 4 * NHC

        def aev(a, oc=oc, bk=bk):
            self.WR[bk].wait(a)
            sv.wait(a)
            i = a.activation(out=self.scr[:, 0:TB], in_=_ap(self.ps[bk], 512, 0, [(1, TB)]),
                             func=AF.Copy, scale=self.scl[:, s3o + oc:s3o + oc + 1])
            self.RD[bk].inc(i)
            a.drain()
            sa.inc(a.nop())
        self.A(aev)
        tbk = 4 if bk < 4 else 0

        def ttr(t, tbk=tbk):
            sa.wait(t)
            self.RD[tbk].wait(t)
            t.transpose(_ap(self.ps[tbk], 512, 0, [(1, 128)]), self.scr[:, 0:128],
                        self.idn[:, :])
            l2 = t.transpose(_ap(self.ps[tbk], 512, 128, [(1, 128)]),
                             self.scr[:, 128:256], self.idn[:, :])
            self.WR[tbk].inc(l2)
        self.T(ttr)

        def vadd(v, oc=oc, tbk=tbk, tb=tb):
            self.WR[tbk].wait(v)
            last = None
            for h2_ in range(2):
                tile = 2 * tb + h2_
                dst = _ap(self.xr, self.NT * self.D, tile * self.D + oc * 128, [(1, 128)])
                last = v.tensor_tensor(dst, dst,
                                       _ap(self.ps[tbk], 512, h2_ * 128, [(1, 128)]),
                                       op=ALU.add)
            self.RD[tbk].inc(last)
            v.drain()
            sv.inc(v.nop())
        self.V(vadd)
        return sv.n

    def qkv_kv(self, tb):
        D, D3, NDC = self.D, self.D3, self.NDC
        sv, sa, st, dg = self.sv, self.sa, self.st, self.dg
        for tile in (2 * tb, 2 * tb + 1):
            self.lnorm(tile, lo16=False)
        n_ln2 = sv.n
        self.q_floor = (self.q_floor + [t for t in self.b16_free if t]
                        + [t for t in self.b32_free if t])
        prev_kv_dg = self.ar2_free_dg_kv

        for dc in range(NDC):
            p = dc % 2
            nld = {}

            def syb(sync, dc=dc, p=p, nld=nld):
                wait_toks(sync, self.wqk_free[p] + self.q_floor)
                dg.wait_at(sync, self.n_aw_ready)
                self.dqw[p].inc(sync.dma_start(
                    out=bass.AP(self.rp, self.QKB + p * self.PS,
                                [[self.RPL, 128], [128, NDC], [1, 128]]),
                    in_=bass.AP(self.awhi_d.tensor, D + dc * 128,
                                [[D3, 128], [128 * D3, NDC], [1, 128]])), 16)
                nld["n"] = self.dqw[p].inc(sync.dma_start(
                    out=bass.AP(self.rp, self.QKB + p * self.PS + self.D,
                                [[self.RPL, 128], [128, NDC], [1, 128]]),
                    in_=bass.AP(self.awlo_d.tensor, D + dc * 128,
                                [[D3, 128], [128 * D3, NDC], [1, 128]])), 16)
            self.SY(syb)
            bk = dc % 4

            def tkm(t, dc=dc, p=p, bk=bk, nld=nld):
                self.dqw[p].wait_at(t, nld["n"])
                sv.wait_at(t, n_ln2)
                self.RD[bk].wait(t)
                po = _ap(self.ps[bk], 512, 0, [(1, TB)])
                last = None
                for k in range(NDC):
                    whi = self.rap(self.QKB + p * self.PS + k * 128, [(1, 128)])
                    wlo = self.rap(self.QKB + p * self.PS + self.D + k * 128, [(1, 128)])
                    mhi = self.rap(self.LNHI + k * TB, [(1, TB)])
                    mlo = self.rap(self.LNLO32 + k * TB, [(1, TB)])
                    t.matmul(po, whi, mhi, start=(k == 0), stop=False)
                    t.matmul(po, whi, mlo, start=False, stop=False)
                    last = t.matmul(po, wlo, mhi, start=False, stop=(k == NDC - 1))
                self.WR[bk].inc(last)
                self.wqk_free[p] = [(self.WR[bk], self.WR[bk].n)]
            self.T(tkm)

            def akv(a, dc=dc, bk=bk, prev_kv_dg=prev_kv_dg):
                self.WR[bk].wait(a)
                dg.wait_at(a, prev_kv_dg)
                i = a.activation(out=self.rap(self.KVSTG + dc * TB, [(1, TB)]),
                                 in_=_ap(self.ps[bk], 512, 0, [(1, TB)]), func=AF.Copy)
                self.RD[bk].inc(i)
                a.drain()
                sa.inc(a.nop())
            self.A(akv)

            def vkv(v, dc=dc, bk=bk):
                sa.wait(v)
                hi = self.rap(self.KVSTG + dc * TB, [(1, TB)])
                i = v.tensor_tensor(self.sA[:, 0:TB], _ap(self.ps[bk], 512, 0, [(1, TB)]),
                                    hi.bitcast(F32), op=ALU.subtract)
                self.RD[bk].inc(i)
                v.drain()
                lo = self.rap(self.KVSTG + 2 * D + dc * TB, [(1, TB)])
                v.tensor_copy(lo, self.sA[:, 0:TB])
                v.drain()
                sv.inc(v.nop())
            self.V(vkv)

        DH, NDH = self.DH, self.NDH
        for dh in range(NDH):
            kn = {}
            for k in range(NDC):
                pk = k % 2

                def syv(sync, dh=dh, k=k, pk=pk, kn=kn):
                    wait_toks(sync, self.wqk_free[pk] + self.q_floor)
                    dg.wait_at(sync, self.n_aw_ready)
                    self.dqw[pk].inc(sync.dma_start(
                        out=self.rap(self.QKB + pk * self.PS, [(1, DH)]),
                        in_=self.awhi_d[k * 128:(k + 1) * 128,
                                        2 * D + dh * DH:2 * D + (dh + 1) * DH]), 16)
                    kn[k] = self.dqw[pk].inc(sync.dma_start(
                        out=self.rap(self.QKB + pk * self.PS + self.D, [(1, DH)]),
                        in_=self.awlo_d[k * 128:(k + 1) * 128,
                                        2 * D + dh * DH:2 * D + (dh + 1) * DH]), 16)
                self.SY(syv)

                def tvm(t, dh=dh, k=k, pk=pk, kn=kn):
                    self.dqw[pk].wait_at(t, kn[k])
                    sv.wait_at(t, n_ln2)
                    last = None
                    for tt in range(2):
                        bk = 4 + tt * NDH + dh
                        if k == 0:
                            self.RD[bk].wait(t)
                        po = _ap(self.ps[bk], 512, 0, [(1, DH)])
                        shi = self.rap(self.LNHI + k * TB + tt * 128, [(1, 128)])
                        slo = self.rap(self.LNLO32 + k * TB + tt * 128, [(1, 128)])
                        whi = self.rap(self.QKB + pk * self.PS, [(1, DH)])
                        wlo = self.rap(self.QKB + pk * self.PS + self.D, [(1, DH)])
                        t.matmul(po, shi, whi, start=(k == 0), stop=False)
                        t.matmul(po, slo, whi, start=False, stop=False)
                        last = t.matmul(po, shi, wlo, start=False, stop=(k == NDC - 1))
                        if k == NDC - 1:
                            self.WR[bk].inc(last)
                    if k == NDC - 1:
                        bk_last = 4 + NDH + dh
                        self.wqk_free[pk] = [(self.WR[bk_last], self.WR[bk_last].n)]
                    else:
                        self.wqk_free[pk] = [(st, st.inc(last))]
                self.T(tvm)

            for tt in range(2):
                bk = 4 + tt * NDH + dh

                def avv(a, tt=tt, dh=dh, bk=bk, prev_kv_dg=prev_kv_dg):
                    self.WR[bk].wait(a)
                    dg.wait_at(a, prev_kv_dg)
                    i = a.activation(out=self.rap(self.KVSTG + 4 * D + tt * D + dh * DH,
                                                  [(1, DH)]),
                                     in_=_ap(self.ps[bk], 512, 0, [(1, DH)]), func=AF.Copy)
                    self.RD[bk].inc(i)
                    a.drain()
                    sa.inc(a.nop())
                self.A(avv)

                def vvv(v, tt=tt, dh=dh, bk=bk):
                    sa.wait(v)
                    hi = self.rap(self.KVSTG + 4 * D + tt * D + dh * DH, [(1, DH)])
                    i = v.tensor_tensor(self.sA[:, 0:DH], _ap(self.ps[bk], 512, 0, [(1, DH)]),
                                        hi.bitcast(F32), op=ALU.subtract)
                    self.RD[bk].inc(i)
                    v.drain()
                    lo = self.rap(self.KVSTG + 6 * D + tt * D + dh * DH, [(1, DH)])
                    v.tensor_copy(lo, self.sA[:, 0:DH])
                    v.drain()
                    sv.inc(v.nop())
                self.V(vvv)
        n_v_done = sv.n

        def gkv(g, tb=tb, n_v_done=n_v_done):
            sv.wait_at(g, n_v_done)
            g.wait_ge(dg.h, dg.n)
            ksrc = lambda off: bass.AP(self.rp, off, [[self.RPL, 128], [TB, NDC], [1, TB]])
            kdst = lambda d: bass.AP(d.tensor, 0, [[TB, 128], [128 * TB, NDC], [1, TB]])
            dg.inc(g.dma_start(out=kdst(self.km["kh"][tb]), in_=ksrc(self.KVSTG)), 16)
            dg.inc(g.dma_start(out=kdst(self.km["kl"][tb]), in_=ksrc(self.KVSTG + 2 * D)), 16)
            vsrc = lambda off: bass.AP(self.rp, off, [[self.RPL, 128], [D, 2], [1, D]])
            vdst = lambda d: bass.AP(d.tensor, 0, [[D, 128], [128 * D, 2], [1, D]])
            dg.inc(g.dma_start(out=vdst(self.km["vh"][tb]), in_=vsrc(self.KVSTG + 4 * D)), 16)
            n4 = dg.inc(g.dma_start(out=vdst(self.km["vl"][tb]),
                                    in_=vsrc(self.KVSTG + 6 * D)), 16)
            g.wait_ge(dg.h, n4)
            for kind in ("kh", "kl", "vh", "vl"):
                self.cc.inc(g.collective_compute(
                    "AllGather", ALU.bypass, replica_groups=RG,
                    ins=[self.km[kind][tb][:, :]], outs=[self.ka[kind][tb][:, :]]))
        self.G(gkv)
        self.ar2_free_dg_kv = dg.n

    def attention(self):
        D, NDC, NKC, SEQ = self.D, self.NDC, self.NKC, self.SEQ
        DH, NDH = self.DH, self.NDH
        sv, sa, st, dg, cc = self.sv, self.sa, self.st, self.dg, self.cc
        n_ag_all = cc.n
        n_kv_stores = dg.n
        self.q_floor = (self.q_floor + [t for t in self.b16_free if t]
                        + [t for t in self.b32_free if t])
        init_kv = ([(st, st.n), (sv, sv.n), (sa, sa.n)]
                   + [(self.WR[b], self.WR[b].n) for b in range(8)])
        self.kv_free = [list(init_kv), list(init_kv)]
        inv_sqrt_d = float(1.0 / np.sqrt(float(D)))
        for qb in range(self.NQB):
            t0 = 2 * qb
            for tile in (t0, t0 + 1):
                self.lnorm(tile, lo16=False)
            n_ln2 = sv.n

            for dc in range(NDC):
                p = dc % 2
                nld = {}

                def syq(sync, dc=dc, p=p, nld=nld):
                    wait_toks(sync, self.wqk_free[p] + self.q_floor
                              + self.kv_free[0] + self.kv_free[1])
                    dg.wait_at(sync, self.n_aw_ready)
                    self.dqw[p].inc(sync.dma_start(
                        out=bass.AP(self.rp, self.QB2 + p * self.PS,
                                    [[self.RPL, 128], [128, NDC], [1, 128]]),
                        in_=bass.AP(self.awhi_d.tensor, dc * 128,
                                    [[self.D3, 128], [128 * self.D3, NDC], [1, 128]])), 16)
                    nld["n"] = self.dqw[p].inc(sync.dma_start(
                        out=bass.AP(self.rp, self.QB2 + p * self.PS + self.D,
                                    [[self.RPL, 128], [128, NDC], [1, 128]]),
                        in_=bass.AP(self.awlo_d.tensor, dc * 128,
                                    [[self.D3, 128], [128 * self.D3, NDC], [1, 128]])), 16)
                self.SY(syq)
                bk = dc % 4

                def tqm(t, p=p, bk=bk, nld=nld):
                    self.dqw[p].wait_at(t, nld["n"])
                    sv.wait_at(t, n_ln2)
                    self.RD[bk].wait(t)
                    po = _ap(self.ps[bk], 512, 0, [(1, TB)])
                    last = None
                    for k in range(NDC):
                        whi = self.rap(self.QB2 + p * self.PS + k * 128, [(1, 128)])
                        wlo = self.rap(self.QB2 + p * self.PS + self.D + k * 128, [(1, 128)])
                        mhi = self.rap(self.LNHI + k * TB, [(1, TB)])
                        mlo = self.rap(self.LNLO32 + k * TB, [(1, TB)])
                        t.matmul(po, whi, mhi, start=(k == 0), stop=False)
                        t.matmul(po, whi, mlo, start=False, stop=False)
                        last = t.matmul(po, wlo, mhi, start=False, stop=(k == NDC - 1))
                    self.WR[bk].inc(last)
                    self.wqk_free[p] = [(self.WR[bk], self.WR[bk].n)]
                self.T(tqm)

                def aqe(a, dc=dc, bk=bk):
                    self.WR[bk].wait(a)
                    i = a.activation(out=self.rap(self.QH + dc * TB, [(1, TB)]),
                                     in_=_ap(self.ps[bk], 512, 0, [(1, TB)]), func=AF.Copy)
                    self.RD[bk].inc(i)
                    a.drain()
                    sa.inc(a.nop())
                self.A(aqe)

                def vqe(v, dc=dc, bk=bk):
                    sa.wait(v)
                    hi = self.rap(self.QH + dc * TB, [(1, TB)])
                    i = v.tensor_tensor(self.sA[:, 0:TB], _ap(self.ps[bk], 512, 0, [(1, TB)]),
                                        hi.bitcast(F32), op=ALU.subtract)
                    self.RD[bk].inc(i)
                    v.drain()
                    lo = self.rap(self.QH + 2 * D + dc * TB, [(1, TB)])
                    v.tensor_copy(lo, self.sA[:, 0:TB])
                    v.drain()
                    sv.inc(v.nop())
                self.V(vqe)
            n_q_done = sv.n

            def vqs(v, t0=t0):
                for ti in range(2):
                    v.tensor_scalar(self.red[:, 32 + ti * 16:32 + ti * 16 + NKC],
                                    self.iot[:, 0:NKC], -float(TB),
                                    self.qps[:, t0 + ti:t0 + ti + 1],
                                    op0=ALU.mult, op1=ALU.add)
                v.drain()
                sv.inc(v.nop())
            self.V(vqs)
            n_qs = sv.n

            for kc in range(NKC):
                p = kc % 2
                r, sb_ = kc // self.NBLK, kc % self.NBLK
                nld = {}

                def syk(sync, p=p, r=r, sb_=sb_, nld=nld):
                    cc.wait_at(sync, n_ag_all)
                    dg.wait_at(sync, n_kv_stores)
                    wait_toks(sync, self.kv_free[p])
                    self.dkv[p].inc(sync.dma_start(
                        out=bass.AP(self.rp, self.KVS + p * self.PS,
                                    [[self.RPL, 128], [TB, NDC], [1, TB]]),
                        in_=bass.AP(self.ka["kh"][sb_].tensor, r * D * TB,
                                    [[TB, 128], [128 * TB, NDC], [1, TB]])), 16)
                    nld["n"] = self.dkv[p].inc(sync.dma_start(
                        out=bass.AP(self.rp, self.KVSL + p * self.PS,
                                    [[self.RPL, 128], [TB, NDC], [1, TB]]),
                        in_=bass.AP(self.ka["kl"][sb_].tensor, r * D * TB,
                                    [[TB, 128], [128 * TB, NDC], [1, TB]])), 16)
                self.SY(syk)

                for ti in range(2):
                    bk = ti

                    def tsc(t, kc=kc, p=p, ti=ti, bk=bk, nld=nld):
                        self.dkv[p].wait_at(t, nld["n"])
                        sv.wait_at(t, n_q_done)
                        self.RD[bk].wait(t)
                        po = _ap(self.ps[bk], 512, 0, [(1, TB)])
                        last = None
                        for k in range(NDC):
                            qhi = self.rap(self.QH + k * TB + ti * 128, [(1, 128)])
                            qlo = self.rap(self.QH + 2 * D + k * TB + ti * 128, [(1, 128)])
                            khi = self.rap(self.KVS + p * self.PS + k * TB, [(1, TB)])
                            klo = self.rap(self.KVSL + p * self.PS + k * TB, [(1, TB)])
                            t.matmul(po, qhi, khi, start=(k == 0), stop=False)
                            t.matmul(po, qhi, klo, start=False, stop=False)
                            last = t.matmul(po, qlo, khi, start=False, stop=(k == NDC - 1))
                        self.WR[bk].inc(last)
                        if ti == 1:
                            self.kv_free[p] = [(self.WR[0], self.WR[0].n),
                                               (self.WR[1], self.WR[1].n)]
                    self.T(tsc)

                    def vsc(v, kc=kc, ti=ti, bk=bk):
                        sv.wait_at(v, n_qs)
                        self.WR[bk].wait(v)
                        dg.wait_at(v, self.n_aw_ready)
                        v.tensor_scalar(self.sA[:, 0:TB], self.iot[:, 0:TB],
                                        self.red[:, 32 + ti * 16 + kc:33 + ti * 16 + kc],
                                        NEG, op0=ALU.is_lt, op1=ALU.mult)
                        v.drain()
                        i = v.tensor_scalar(self.sB[:, 0:TB],
                                            _ap(self.ps[bk], 512, 0, [(1, TB)]),
                                            inv_sqrt_d, None, op0=ALU.mult)
                        self.RD[bk].inc(i)
                        v.drain()
                        v.tensor_tensor(
                            self.fap(self.SCO + ti * SEQ + kc * TB, [(1, TB)]),
                            self.sB[:, 0:TB], self.sA[:, 0:TB], op=ALU.add)
                        v.drain()
                        sv.inc(v.nop())
                    self.V(vsc)
            n_scores = sv.n

            for ti in range(2):
                base = self.SCO + ti * SEQ
                mx, ng, sm, rc = 17 + 3 * ti, 18 + 3 * ti, 19 + 3 * ti, 23 + ti

                def vsm(v, base=base, mx=mx, ng=ng):
                    sv.wait_at(v, n_scores)
                    v.reduce_max(self.red[:, mx:mx + 1], self.fap(base, [(1, SEQ)]),
                                 axis=AXX)
                    v.drain()
                    v.tensor_scalar(self.red[:, ng:ng + 1], self.red[:, mx:mx + 1], -1.0,
                                    None, op0=ALU.mult)
                    v.drain()
                    sv.inc(v.nop())
                self.V(vsm)

                def asm(a, base=base, ng=ng, sm=sm):
                    sv.wait(a)
                    a.activation(out=self.fap(base, [(1, SEQ)]),
                                 in_=self.fap(base, [(1, SEQ)]),
                                 func=AF.Exp, bias=self.red[:, ng:ng + 1], scale=1.0,
                                 accum_out=self.red[:, sm:sm + 1])
                    a.drain()
                    sa.inc(a.nop())
                self.A(asm)

                def vrc(v, sm=sm, rc=rc):
                    sa.wait(v)
                    v.reciprocal(self.red[:, rc:rc + 1], self.red[:, sm:sm + 1])
                    v.drain()
                    sv.inc(v.nop())
                self.V(vrc)

                def apb(a, base=base, rc=rc):
                    sv.wait(a)
                    a.activation(out=self.fap(base, [(1, SEQ)]),
                                 in_=self.fap(base, [(1, SEQ)]),
                                 func=AF.Copy, scale=self.red[:, rc:rc + 1])
                    a.drain()
                    sa.inc(a.nop())
                self.A(apb)
            n_probs_sa = sa.n

            for kc in range(NKC):
                p = kc % 2
                r, sb_ = kc // self.NBLK, kc % self.NBLK
                nld = {}

                def syv(sync, p=p, r=r, sb_=sb_, nld=nld):
                    dg.wait_at(sync, n_kv_stores)
                    wait_toks(sync, self.kv_free[p])
                    self.dkv[p].inc(sync.dma_start(
                        out=bass.AP(self.rp, self.KVS + p * self.PS,
                                    [[self.RPL, 128], [D, 2], [1, D]]),
                        in_=bass.AP(self.ka["vh"][sb_].tensor, r * TB * D,
                                    [[D, 128], [128 * D, 2], [1, D]])), 16)
                    nld["n"] = self.dkv[p].inc(sync.dma_start(
                        out=bass.AP(self.rp, self.KVSL + p * self.PS,
                                    [[self.RPL, 128], [D, 2], [1, D]]),
                        in_=bass.AP(self.ka["vl"][sb_].tensor, r * TB * D,
                                    [[D, 128], [128 * D, 2], [1, D]])), 16)
                self.SY(syv)

                for ti in range(2):
                    base = self.SCO + ti * SEQ
                    tbk = 4 + (kc % 2) * 2 + ti
                    slot = (kc % 2) * 2 + ti
                    aT_hi = self.ATT + slot * 256
                    aT_lo = self.ATT + 1024 + slot * 256

                    def ttr(t, kc=kc, base=base, tbk=tbk):
                        sa.wait_at(t, n_probs_sa)
                        self.RD[tbk].wait(t)
                        t.transpose(_ap(self.ps[tbk], 512, 0, [(1, 128)]),
                                    self.fap(base + kc * TB, [(1, 128)]),
                                    self.idn[:, :])
                        l2 = t.transpose(_ap(self.ps[tbk], 512, 128, [(1, 128)]),
                                         self.fap(base + kc * TB + 128, [(1, 128)]),
                                         self.idn[:, :])
                        self.WR[tbk].inc(l2)
                    self.T(ttr)

                    def atv(a, tbk=tbk, aT_hi=aT_hi):
                        self.WR[tbk].wait(a)
                        i = a.activation(out=self.rap(aT_hi, [(1, 256)]),
                                         in_=_ap(self.ps[tbk], 512, 0, [(1, 256)]),
                                         func=AF.Copy)
                        self.RD[tbk].inc(i)
                        a.drain()
                        sa.inc(a.nop())
                    self.A(atv)

                    def vtv(v, tbk=tbk, aT_hi=aT_hi, aT_lo=aT_lo):
                        sa.wait(v)
                        hi = self.rap(aT_hi, [(1, 256)])
                        i = v.tensor_tensor(self.sA[:, 0:256],
                                            _ap(self.ps[tbk], 512, 0, [(1, 256)]),
                                            hi.bitcast(F32), op=ALU.subtract)
                        self.RD[tbk].inc(i)
                        v.drain()
                        v.tensor_copy(self.rap(aT_lo, [(1, 256)]), self.sA[:, 0:256])
                        v.drain()
                        sv.inc(v.nop())
                    self.V(vtv)
                    n_at = sv.n

                    for dh in range(NDH):
                        avbk = ti * NDH + dh

                        def tav(t, kc=kc, ti=ti, dh=dh, p=p, avbk=avbk, n_at=n_at,
                                aT_hi=aT_hi, aT_lo=aT_lo, nld=nld):
                            sv.wait_at(t, n_at)
                            self.dkv[p].wait_at(t, nld["n"])
                            if kc == 0:
                                self.RD[avbk].wait(t)
                            po = _ap(self.ps[avbk], 512, 0, [(1, DH)])
                            last = None
                            for kk in range(2):
                                ahi = self.rap(aT_hi + kk * 128, [(1, 128)])
                                alo = self.rap(aT_lo + kk * 128, [(1, 128)])
                                vhi = self.rap(self.KVS + p * self.PS + kk * D + dh * DH,
                                               [(1, DH)])
                                vlo = self.rap(self.KVSL + p * self.PS + kk * D + dh * DH,
                                               [(1, DH)])
                                first = (kc == 0 and kk == 0)
                                lastf = (kc == NKC - 1 and kk == 1)
                                t.matmul(po, ahi, vhi, start=first, stop=False)
                                t.matmul(po, ahi, vlo, start=False, stop=False)
                                last = t.matmul(po, alo, vhi, start=False, stop=lastf)
                            if kc == NKC - 1:
                                self.WR[avbk].inc(last)
                            elif dh == NDH - 1:
                                n = st.inc(last)
                                if ti == 1:
                                    self.kv_free[p] = [(st, n)]
                        self.T(tav)

            self.kv_free[(NKC - 1) % 2] = [
                (self.WR[b], self.WR[b].n) for b in range(2 * NDH)]

            for ti in range(2):
                for dh in range(NDH):
                    avbk = ti * NDH + dh

                    def vav(v, ti=ti, dh=dh, avbk=avbk, t0=t0):
                        self.WR[avbk].wait(v)
                        tile = t0 + ti
                        dst = _ap(self.xr, self.NT * D, tile * D + dh * DH, [(1, DH)])
                        last = v.tensor_tensor(dst, dst,
                                               _ap(self.ps[avbk], 512, 0, [(1, DH)]),
                                               op=ALU.add)
                        self.RD[avbk].inc(last)
                        v.drain()
                        sv.inc(v.nop())
                    self.V(vav)

    def store_out(self, tb):
        sv, dg = self.sv, self.dg
        n_ready = sv.n

        def gst(g, tb=tb, n_ready=n_ready):
            sv.wait_at(g, n_ready)
            g.wait_ge(dg.h, dg.n)
            for h2_ in range(2):
                tile = 2 * tb + h2_
                dg.inc(g.dma_start(
                    out=self.out_d[tile * 128:(tile + 1) * 128, :],
                    in_=_ap(self.xr, self.NT * self.D, tile * self.D, [(1, self.D)])), 16)
        self.G(gst)

    def finish(self):
        dg = self.dg

        def gf(g):
            g.wait_ge(dg.h, dg.n)
        self.G(gf)


def build_nc():
    return Build().build()


def kernel(L0_attn_x, L0_attn_W, L0_ffn_w1_w, L0_ffn_w1_s, L0_ffn_w2_w,
           L0_ffn_w2_s, L0_ffn_h1_w, L0_ffn_h1_s, L0_ffn_h2_w, L0_ffn_h2_s,
           L0_ffn_w3_w, L0_ffn_w3_s):
    D, SEQ, TOK = CFG["D"], CFG["SEQ"], CFG["TOK"]
    NT = TOK // 128
    x = np.ascontiguousarray(np.asarray(L0_attn_x, dtype=np.float32))
    aw = np.ascontiguousarray(np.asarray(L0_attn_W, dtype=np.float32))
    w = {n: np.ascontiguousarray(np.asarray(a, dtype=np.float32)) for n, a in
         (("w1", L0_ffn_w1_w), ("w2", L0_ffn_w2_w), ("h1", L0_ffn_h1_w),
          ("h2", L0_ffn_h2_w), ("w3", L0_ffn_w3_w))}
    s = {n: np.ascontiguousarray(np.asarray(a, dtype=np.float32).reshape(-1, 128).T)
         for n, a in (("s1", L0_ffn_w1_s), ("s2", L0_ffn_w2_s), ("sh1", L0_ffn_h1_s),
                      ("sh2", L0_ffn_h2_s), ("s3", L0_ffn_w3_s))}
    ident = np.eye(128, dtype=np.float32)
    nc = build_nc()
    in_maps = []
    for c in range(CFG["NCORE"]):
        b, h = c % 4, c // 4
        qpos = (h * TOK + np.arange(NT)[None, :] * 128
                + np.arange(128)[:, None]).astype(np.float32)
        m = {"x": np.ascontiguousarray(x[b, h * TOK:(h + 1) * TOK, :]), "aw": aw,
             "ident": ident, "qpos": np.ascontiguousarray(qpos)}
        for n in w:
            m[n] = w[n]
        for n in s:
            m[n + "c"] = s[n]
        in_maps.append(m)
    res = run_bass_kernel_spmd(nc, in_maps, core_ids=list(range(CFG["NCORE"])))
    out = np.empty((4, SEQ, D), np.float32)
    for c, r in enumerate(res.results):
        b, h = c % 4, c // 4
        out[b, h * TOK:(h + 1) * TOK, :] = r["out"]
    return out


# version 07
# speedup vs baseline: 2.4464x; 2.4464x over previous
import contextlib
import os

os.environ.setdefault("GAUGE_NTFF_IGNORE_DMA", "1")

import numpy as np

import concourse.bass as bass
import concourse.mybir as mybir
from concourse.bass_utils import run_bass_kernel_spmd

F32 = mybir.dt.float32
F32R = mybir.dt.float32r
F16 = mybir.dt.float16
AF = mybir.ActivationFunctionType
ALU = mybir.AluOpType
AXX = mybir.AxisListType.X

CFG = dict(D=1024, HID=4096, SEQ=4096, TOK=2048, NCORE=8)
TB = 256
NEG = -9.0e15
EPS = 1e-5
RG = [[0, 4], [1, 5], [2, 6], [3, 7]]
PHASES = ("pre", "attn", "post")


class Sem:
    __slots__ = ("h", "n")

    def __init__(self, h):
        self.h = h
        self.n = 0

    def inc(self, instr, k=1):
        instr.then_inc(self.h, k)
        self.n += k
        return self.n

    def wait_at(self, eng, n):
        assert 0 <= n <= self.n, (n, self.n)
        if n <= 0:
            return
        key = (type(eng).__name__, id(self))
        if WAITCACHE.get(key, 0) >= n:
            return
        WAITCACHE[key] = n
        eng.wait_ge(self.h, n)

    def wait(self, eng):
        self.wait_at(eng, self.n)


WAITCACHE = {}


def wait_toks(eng, toks):
    if not toks:
        return
    best = {}
    for sem, val in toks:
        if val > best.get(id(sem), (None, -1))[1]:
            best[id(sem)] = (sem, val)
    for sem, val in best.values():
        sem.wait_at(eng, val)


def _ap(t, rowlen, off, dims):
    return bass.AP(t, off, [[rowlen, 128]] + [[s, c] for s, c in dims])


class Build:
    def __init__(self):
        self.D = CFG["D"]
        self.HID = CFG["HID"]
        self.SEQ = CFG["SEQ"]
        self.TOK = CFG["TOK"]
        self.D3 = 3 * self.D
        self.NT = self.TOK // 128
        self.NBLK = self.TOK // TB
        self.NDC = self.D // 128
        self.NHC = self.HID // 128
        self.NKC = self.SEQ // TB
        self.NQB = self.TOK // TB
        self.DH = min(512, self.D)
        self.NDH = self.D // self.DH
        WAITCACHE.clear()
        D, SEQ, D3 = self.D, self.SEQ, self.D3
        self.LNHI = 0
        self.LNLO32 = 2 * D
        self.WB32 = 4 * D
        self.PS = 2 * D
        self.HHI = 4 * D + 2048
        self.GHI = self.HHI + max(self.NHC * TB, 4 * self.PS)
        self.KVSTG = self.HHI
        self.QKB = self.GHI
        self.KVS = self.HHI
        self.KVSL = self.HHI + 2 * self.PS
        self.QH = self.GHI
        self.QB2 = self.GHI + 4 * D
        self.ATT = self.QB2
        self.RPL = max(self.GHI + self.NHC * TB, self.QB2 + 2 * self.PS,
                       self.ATT + 2048)
        self.SCO = 0
        self.LNLO16 = 0
        self.WB16 = 2 * D
        self.HLO = 2 * D + 4096
        self.GLO = self.HLO + self.NHC * TB
        f16_need = self.GLO + self.NHC * TB
        self.FAL = max(2 * SEQ, 3 * D3, (f16_need + 1) // 2)
        self.nc = bass.Bass("TRN2", target_bir_lowering=False)

    def T(self, f):
        self.blk.tensor(f)

    def V(self, f):
        self.blk.vector(f)

    def A(self, f):
        self.blk.scalar(f)

    def G(self, f):
        self.blk.gpsimd(f)

    def SY(self, f):
        self.blk.sync(f)

    def rap(self, off, dims):
        return _ap(self.rp, self.RPL, off, dims)

    def fap(self, off, dims):
        return _ap(self.fa, self.FAL, off, dims)

    def hap(self, off, dims):
        return _ap(self.fa16, 2 * self.FAL, off, dims)

    def build(self):
        nc, D, HID, D3 = self.nc, self.D, self.HID, self.D3
        TOK, NT, NBLK = self.TOK, self.NT, self.NBLK
        self.x_d = nc.dram_tensor("x", [TOK, D], F32, kind="ExternalInput").ap()
        self.aw_d = nc.dram_tensor("aw", [D, D3], F32, kind="ExternalInput").ap()
        self.w_d = {n: nc.dram_tensor(n, list(s), F32, kind="ExternalInput").ap()
                    for n, s in (("w1", (HID, D)), ("w2", (HID, D)),
                                 ("h1", (HID, HID)), ("h2", (HID, HID)),
                                 ("w3", (D, HID)))}
        self.s_d = {n: nc.dram_tensor(n + "c", [128, w], F32, kind="ExternalInput").ap()
                    for n, w in (("s1", self.NHC), ("s2", self.NHC), ("sh1", self.NHC),
                                 ("sh2", self.NHC), ("s3", self.NDC))}
        self.ident_d = nc.dram_tensor("ident", [128, 128], F32, kind="ExternalInput").ap()
        self.qpos_d = nc.dram_tensor("qpos", [128, NT], F32, kind="ExternalInput").ap()
        self.out_d = nc.dram_tensor("out", [TOK, D], F32, kind="ExternalOutput").ap()
        self.wf_d = {n: nc.dram_tensor(n + "f", list(self.w_d[n].shape), F16,
                                       kind="Internal").ap() for n in self.w_d}
        self.awhi_d = nc.dram_tensor("awhi", [D, D3], F32R, kind="Internal").ap()
        self.awlo_d = nc.dram_tensor("awlo", [D, D3], F32R, kind="Internal").ap()
        self.km, self.ka = {}, {}
        for kind in ("kh", "kl"):
            self.km[kind] = [nc.dram_tensor(f"{kind}m{b}", [D, TB], F32R,
                                            kind="Internal").ap() for b in range(NBLK)]
            self.ka[kind] = [nc.dram_tensor(f"{kind}a{b}", [2 * D, TB], F32R,
                                            kind="Internal").ap() for b in range(NBLK)]
        for kind in ("vh", "vl"):
            self.km[kind] = [nc.dram_tensor(f"{kind}m{b}", [TB, D], F32R,
                                            kind="Internal").ap() for b in range(NBLK)]
            self.ka[kind] = [nc.dram_tensor(f"{kind}a{b}", [2 * TB, D], F32R,
                                            kind="Internal").ap() for b in range(NBLK)]

        es = contextlib.ExitStack()
        with es:
            sb = lambda name, cols, dt: es.enter_context(nc.sbuf_tensor(name, [128, cols], dt))
            self.xr = sb("xr", NT * D, F32)
            self.rp = sb("rp", self.RPL, F32R)
            self.fa = sb("fa", self.FAL, F32)
            self.scr = sb("scr", max(D, 2 * TB), F32)
            self.sA = sb("sA", 512, F32)
            self.sB = sb("sB", 512, F32)
            self.red = sb("red", 64, F32)
            nsc = 4 * self.NHC + self.NDC
            self.scl = sb("scl", nsc, F32)
            self.idn = sb("idn", 128, F32)
            self.qps = sb("qps", NT, F32)
            self.iot = sb("iot", 256, F32)
            self.ps = [es.enter_context(nc.psum_tensor(f"ps{i}", [128, 512], F32))
                       for i in range(8)]
            mk = lambda n: Sem(es.enter_context(nc.semaphore(n)))
            self.dx, self.dxa, self.dg = mk("dx"), mk("dxa"), mk("dg")
            self.dct = {n: mk("dct_" + n) for n in ("w1", "w2", "h1", "h2", "w3")}
            self.dt16 = [mk(f"dt16_{i}") for i in range(8)]
            self.dqw = [mk("dqw0"), mk("dqw1")]
            self.dkv = [mk("dkv0"), mk("dkv1")]
            self.sv, self.sa, self.st, self.cc = mk("sv"), mk("sa"), mk("st"), mk("cc")
            self.WR = [mk(f"wr{i}") for i in range(8)]
            self.RD = [mk(f"rd{i}") for i in range(8)]
            self.blk = es.enter_context(nc.Block())

            self.fa16 = self.fa.bitcast(F16)
            self.fa32r = self.fa.bitcast(F32R)

            self.b16_ready = [0] * 8
            self.b32_ready = [0] * 4
            self.b16_free = [None] * 8
            self.b32_free = [None] * 4
            self.b16_next = 0
            self.b32_next = 0
            self.wqk_free = [[], []]
            self.kv_free = [[], []]
            self.ffn_floor = []
            self.q_floor = []
            self.ar1_free_dg = 0
            self.ar2_free_dg_kv = 0
            self.evac_sv = {}
            self.evac_st = {}

            self.phase0()
            self.split_aw()
            if "pre" in PHASES:
                self.ffn_call(0)
            if "attn" in PHASES:
                self.attention()
            if "post" in PHASES:
                self.ffn_call(1)
            else:
                for tb in range(NBLK):
                    self.store_out(tb)
            self.finish()
        return nc

    def phase0(self):
        D, NT = self.D, self.NT
        NHC, NDC = self.NHC, self.NDC
        dx, sv = self.dx, self.sv
        xr, scl, idn, qps, iot = self.xr, self.scl, self.idn, self.qps, self.iot
        x_d, s_d = self.x_d, self.s_d

        def sy0(sync):
            xin = bass.AP(x_d.tensor, 0, [[D, 128], [128 * D, NT], [1, D]])
            xout = bass.AP(xr, 0, [[NT * D, 128], [D, NT], [1, D]])
            dx.inc(sync.dma_start(out=xout, in_=xin), 16)
            off = 0
            for n in ("s1", "s2", "sh1", "sh2"):
                dx.inc(sync.dma_start(out=scl[:, off:off + NHC], in_=s_d[n][:, :]), 16)
                off += NHC
            dx.inc(sync.dma_start(out=scl[:, off:off + NDC], in_=s_d["s3"][:, :]), 16)
            dx.inc(sync.dma_start(out=idn[:, :], in_=self.ident_d[:, :]), 16)
            dx.inc(sync.dma_start(out=qps[:, :], in_=self.qpos_d[:, :]), 16)
        self.G(sy0)
        self.n_small_loads = dx.n
        self.cast_done = {}

        def g0(g):
            g.iota(iot[:, :], [[1, 256]], channel_multiplier=0,
                   allow_small_or_imprecise_dtypes=True)
            for name in ("w1", "w2", "h1", "h2", "w3"):
                src, dst = self.w_d[name], self.wf_d[name]
                rows = src.shape[0]
                step = min(rows, 1024)
                for r0 in range(0, rows, step):
                    self.cast_done[name] = self.dct[name].inc(
                        g.dma_start(out=dst[r0:r0 + step, :],
                                    in_=src[r0:r0 + step, :]), 16)
        self.G(g0)

        def v0(v):
            dx.wait_at(v, self.n_small_loads)
            s3o = 4 * NHC
            v.tensor_scalar(scl[:, s3o:s3o + NDC], scl[:, s3o:s3o + NDC], 0.5, None,
                            op0=ALU.mult)
            v.drain()
            sv.inc(v.nop())
        self.V(v0)
        self.n_x_ready = self.n_small_loads

    def split_aw(self):
        D, D3 = self.D, self.D3
        dxa, dg, sv = self.dxa, self.dg, self.sv
        for r in range(D // 128):
            nl = {}

            def syl(sync, r=r, nl=nl):
                dg.wait_at(sync, self.ar1_free_dg)
                nl["n"] = dxa.inc(sync.dma_start(
                    out=self.fap(0, [(1, D3)]),
                    in_=self.aw_d[r * 128:(r + 1) * 128, :]), 16)
            self.G(syl)

            def vsp(v, nl=nl):
                dxa.wait_at(v, nl["n"])
                src = self.fap(0, [(1, D3)])
                hi = _ap(self.fa32r, self.FAL, D3, [(1, D3)])
                d_ = self.fap(2 * D3, [(1, D3)])
                lo = _ap(self.fa32r, self.FAL, 2 * D3, [(1, D3)])
                v.tensor_copy(hi, src)
                v.drain()
                v.tensor_tensor(d_, src, hi.bitcast(F32), op=ALU.subtract)
                v.drain()
                v.tensor_copy(lo, d_)
                v.drain()
                sv.inc(v.nop())
            self.V(vsp)
            nsv = sv.n

            def gst(g, r=r, nsv=nsv):
                sv.wait_at(g, nsv)
                hi = _ap(self.fa32r, self.FAL, D3, [(1, D3)])
                lo = _ap(self.fa32r, self.FAL, 2 * D3, [(1, D3)])
                dg.inc(g.dma_start(out=self.awhi_d[r * 128:(r + 1) * 128, :], in_=hi), 16)
                dg.inc(g.dma_start(out=self.awlo_d[r * 128:(r + 1) * 128, :], in_=lo), 16)
            self.G(gst)
            self.ar1_free_dg = dg.n
        self.n_aw_ready = dg.n

    def lnorm(self, tile, lo16):
        D = self.D
        sv, sa = self.sv, self.sa
        xr, red, scr, sA, idn = self.xr, self.red, self.scr, self.sA, self.idn
        half = tile % 2
        xcol = tile * D
        nch = max(1, D // 512)
        w = min(D, 512)

        def vln(v):
            self.dx.wait_at(v, self.n_x_ready)
            v.drain()
            for i in range(nch):
                v.bn_stats(red[:, 6 * i:6 * i + 6],
                           _ap(xr, self.NT * D, xcol + i * w, [(1, w)]))
            v.drain()
            v.bn_aggr(red[:, 12:14], red[:, 0:6 * nch])
            v.drain()
            v.tensor_scalar(red[:, 14:15], red[:, 13:14], EPS, None, op0=ALU.add)
            v.drain()
            sv.inc(v.reciprocal(red[:, 15:16], red[:, 14:15]))
        self.V(vln)

        def aln(a):
            sv.wait(a)
            sa.inc(a.activation(out=red[:, 16:17], in_=red[:, 15:16], func=AF.Sqrt))
        self.A(aln)

        def vn2(v):
            sa.wait(v)
            for b in range(8):
                self.WR[b].wait(v)
            v.drain()
            sv.inc(v.tensor_scalar(scr[:, 0:D], _ap(xr, self.NT * D, xcol, [(1, D)]),
                                   red[:, 12:13], red[:, 16:17],
                                   op0=ALU.subtract, op1=ALU.mult))
        self.V(vn2)
        n_ln = sv.n

        for gi in range(0, self.NDC, 4):
            grp = list(range(gi, min(self.NDC, gi + 4)))
            bk = 4 + (gi // 4) % 2

            def tt(t, grp=grp, bk=bk, n_ln=n_ln):
                sv.wait_at(t, n_ln)
                self.RD[bk].wait(t)
                last = None
                for j, fc in enumerate(grp):
                    last = t.transpose(_ap(self.ps[bk], 512, j * 128, [(1, 128)]),
                                       scr[:, fc * 128:(fc + 1) * 128], idn[:, :])
                self.WR[bk].inc(last)
            self.T(tt)

            def aev(a, grp=grp, bk=bk, half=half):
                self.WR[bk].wait(a)
                dst = bass.AP(self.rp, self.LNHI + grp[0] * TB + half * 128,
                              [[self.RPL, 128], [TB, len(grp)], [1, 128]])
                i = a.activation(out=dst, in_=_ap(self.ps[bk], 512, 0,
                                                  [(1, len(grp) * 128)]), func=AF.Copy)
                self.RD[bk].inc(i)
            self.A(aev)

            def vev(v, grp=grp, bk=bk, half=half):
                self.RD[bk].wait(v)
                self.dg.wait_at(v, getattr(self, "n_aw_ready", 0))
                v.drain()
                hi_f32 = bass.AP(self.rp, self.LNHI + grp[0] * TB + half * 128,
                                 [[self.RPL, 128], [TB, len(grp)], [1, 128]]).bitcast(F32)
                i = v.tensor_tensor(sA[:, 0:len(grp) * 128],
                                    _ap(self.ps[bk], 512, 0, [(1, len(grp) * 128)]),
                                    hi_f32, op=ALU.subtract)
                self.RD[bk].inc(i)
                v.drain()
                if lo16:
                    dst = bass.AP(self.fa16, self.LNLO16 + grp[0] * TB + half * 128,
                                  [[2 * self.FAL, 128], [TB, len(grp)], [1, 128]])
                else:
                    dst = bass.AP(self.rp, self.LNLO32 + grp[0] * TB + half * 128,
                                  [[self.RPL, 128], [TB, len(grp)], [1, 128]])
                sv.inc(v.tensor_copy(dst, sA[:, 0:len(grp) * 128]))
            self.V(vev)

    def band_load(self, wname, rows0, nrows, kc):
        s = self.b16_next % 8
        self.b16_next += 1

        def adma(sy, s=s):
            wait_toks(sy, ([self.b16_free[s]] if self.b16_free[s] else []) + self.ffn_floor)
            self.dg.wait_at(sy, getattr(self, "n_aw_ready", 0))
            self.dct[wname].wait_at(sy, self.cast_done[wname])
            n = self.dt16[s].inc(sy.dma_start(
                out=self.hap(self.WB16 + s * 512, [(1, nrows)]),
                in_=self.wf_d[wname][rows0:rows0 + nrows, kc * 128:(kc + 1) * 128],
                transpose=True), 16)
            self.b16_ready[s] = n
        self.SY(adma)
        return s

    def band_up(self, s16, nrows):
        u = self.b32_next % 4
        self.b32_next += 1
        sa = self.sa

        def aup(a, u=u, s16=s16):
            self.dt16[s16].wait_at(a, self.b16_ready[s16])
            wait_toks(a, ([self.b32_free[u]] if self.b32_free[u] else []) + self.ffn_floor)
            i = a.activation(out=self.rap(self.WB32 + u * 512, [(1, nrows)]),
                             in_=self.hap(self.WB16 + s16 * 512, [(1, nrows)]),
                             func=AF.Copy)
            self.b32_ready[u] = sa.inc(i)
        self.A(aup)
        return u

    def ffn_call(self, call):
        self.ffn_floor = (self.ffn_floor
                          + [(self.WR[b], self.WR[b].n) for b in range(8)]
                          + [(self.st, self.st.n)])
        for tb in range(self.NBLK):
            for tile in (2 * tb, 2 * tb + 1):
                self.lnorm(tile, lo16=True)
            self.ln_ready = self.sv.n
            self.ffn_floor = self.ffn_floor + self.wqk_free[0] + self.wqk_free[1]
            h_ready = self.stage(tb, "w1w2")
            g_ready = self.stage(tb, "h1", src_ready=h_ready)
            h3_ready = self.stage(tb, "h2", src_ready=g_ready)
            self.stage(tb, "w3", src_ready=h3_ready)
            if call == 0:
                self.qkv_kv(tb)
            else:
                self.store_out(tb)

    def stage(self, tb, name, src_ready=None):
        D = self.D
        sv, sa, st = self.sv, self.sa, self.st
        if name == "w1w2":
            KC, OC, OCG = self.NDC, self.NHC, 2
        elif name in ("h1", "h2"):
            KC, OC, OCG = self.NHC, self.NHC, 4
        else:
            KC, OC, OCG = self.NHC, self.NDC, 4

        def mv_hi(k):
            if name == "w1w2":
                return self.rap(self.LNHI + k * TB, [(1, TB)])
            base = self.HHI if name in ("h1", "w3") else self.GHI
            return self.rap(base + k * TB, [(1, TB)])

        def mv_lo(k):
            if name == "w1w2":
                return self.hap(self.LNLO16 + k * TB, [(1, TB)])
            base = self.HLO if name in ("h1", "w3") else self.GLO
            return self.hap(base + k * TB, [(1, TB)])

        wname0 = {"w1w2": "w1", "h1": "h1", "h2": "h2", "w3": "w3"}[name]
        out_ready = [0] * OC
        war_dg = 0
        if name == "w1w2":
            war_dg = max(self.ar1_free_dg, self.ar2_free_dg_kv)
        first_write = [True]

        def kloop(g0, banks, ocs):
            for k in range(KC):
                if name == "w1w2":
                    s1 = self.band_load("w1", g0 * 128, len(ocs) * 128, k)
                    s2 = self.band_load("w2", g0 * 128, len(ocs) * 128, k)
                    u1 = self.band_up(s1, len(ocs) * 128)
                    u2 = self.band_up(s2, len(ocs) * 128)
                    pairs = [(s1, u1), (s2, u2)]
                else:
                    s1 = self.band_load(wname0, g0 * 128, len(ocs) * 128, k)
                    u1 = self.band_up(s1, len(ocs) * 128)
                    pairs = [(s1, u1)]

                def tmm(t, k=k, pairs=pairs, banks=banks, ocs=ocs, KC=KC):
                    for (s16, u32) in pairs:
                        sa.wait_at(t, self.b32_ready[u32])
                        self.dt16[s16].wait_at(t, self.b16_ready[s16])
                    if src_ready is not None:
                        sv.wait_at(t, src_ready[k])
                    else:
                        sv.wait_at(t, self.ln_ready)
                    last = None
                    last_bk = None
                    for oi, oc in enumerate(ocs):
                        for wi, (s16, u32) in enumerate(pairs):
                            bk = banks[(oi * len(pairs) + wi) % len(banks)]
                            if k == 0:
                                self.RD[bk].wait(t)
                            po = _ap(self.ps[bk], 512, 0, [(1, TB)])
                            t.matmul(po, self.rap(self.WB32 + u32 * 512 + oi * 128,
                                                  [(1, 128)]),
                                     mv_hi(k), start=(k == 0), stop=False)
                            last = t.matmul(po, self.hap(self.WB16 + s16 * 512 + oi * 128,
                                                         [(1, 128)]),
                                            mv_lo(k), start=False, stop=(k == KC - 1))
                            last_bk = bk
                            if k == KC - 1:
                                self.WR[bk].inc(last)
                    if k == KC - 1:
                        tok = (self.WR[last_bk], self.WR[last_bk].n)
                    else:
                        tok = (st, st.inc(last))
                    for (s16, u32) in pairs:
                        self.b16_free[s16] = tok
                        self.b32_free[u32] = tok
                self.T(tmm)

        groups = []
        for g0 in range(0, OC, OCG):
            banks = [0, 1, 2, 3] if (g0 // OCG) % 2 == 0 else [4, 5, 6, 7]
            ocs = list(range(g0, min(OC, g0 + OCG)))
            groups.append((g0, banks, ocs))
        if name == "w3":
            for g in groups:
                kloop(*g)
                _, pbanks, pocs = g
                for oi, oc in enumerate(pocs):
                    wdg = war_dg if first_write[0] else 0
                    first_write[0] = False
                    out_ready[oc] = self.evac(tb, name, oc, oi, pbanks, wdg)
            return out_ready
        prev = None
        for g in groups:
            kloop(*g)
            if prev is not None:
                _, pbanks, pocs = prev
                for oi, oc in enumerate(pocs):
                    wdg = war_dg if first_write[0] else 0
                    first_write[0] = False
                    out_ready[oc] = self.evac(tb, name, oc, oi, pbanks, wdg)
            prev = g
        _, pbanks, pocs = prev
        for oi, oc in enumerate(pocs):
            wdg = war_dg if first_write[0] else 0
            first_write[0] = False
            out_ready[oc] = self.evac(tb, name, oc, oi, pbanks, wdg)
        return out_ready

    def evac(self, tb, name, oc, oi, banks, war_dg):
        sv, sa, dg = self.sv, self.sa, self.dg
        NHC = self.NHC
        if name == "w1w2":
            bka = banks[(oi * 2) % 4]
            bkb = banks[(oi * 2 + 1) % 4]

            po_ = oc % 2

            def aev(a, oc=oc, bka=bka, bkb=bkb, po_=po_):
                self.WR[bka].wait(a)
                self.WR[bkb].wait(a)
                sv.wait_at(a, self.evac_sv.get(("w", oc - 2), 0))
                i1 = a.activation(out=self.sA[:, po_ * TB:po_ * TB + TB],
                                  in_=_ap(self.ps[bka], 512, 0, [(1, TB)]),
                                  func=AF.Silu, scale=self.scl[:, oc:oc + 1])
                self.RD[bka].inc(i1)
                i2 = a.activation(out=self.sB[:, po_ * TB:po_ * TB + TB],
                                  in_=_ap(self.ps[bkb], 512, 0, [(1, TB)]),
                                  func=AF.Copy, scale=self.scl[:, NHC + oc:NHC + oc + 1])
                self.RD[bkb].inc(i2)
            self.A(aev)

            def vev(v, oc=oc, bka=bka, bkb=bkb, po_=po_, war_dg=war_dg):
                self.RD[bka].wait(v)
                self.RD[bkb].wait(v)
                if war_dg:
                    dg.wait_at(v, war_dg)
                v.drain()
                v.tensor_tensor(self.scr[:, po_ * TB:po_ * TB + TB],
                                self.sA[:, po_ * TB:po_ * TB + TB],
                                self.sB[:, po_ * TB:po_ * TB + TB], op=ALU.mult)
                v.drain()
                hid_ = self.rap(self.HHI + oc * TB, [(1, TB)])
                v.tensor_copy(hid_, self.scr[:, po_ * TB:po_ * TB + TB])
                v.drain()
                v.tensor_tensor(self.sA[:, po_ * TB:po_ * TB + TB],
                                self.scr[:, po_ * TB:po_ * TB + TB], hid_.bitcast(F32),
                                op=ALU.subtract)
                v.drain()
                lod = self.hap(self.HLO + oc * TB, [(1, TB)])
                sv.inc(v.tensor_copy(lod, self.sA[:, po_ * TB:po_ * TB + TB]))
            self.V(vev)
            self.evac_sv[("w", oc)] = sv.n
            return sv.n
        if name in ("h1", "h2"):
            bk = banks[oi % 4]
            hi_base = self.GHI if name == "h1" else self.HHI
            lo_base = self.GLO if name == "h1" else self.HLO
            scol = (2 * NHC + oc) if name == "h1" else (3 * NHC + oc)

            po_ = oc % 2

            def aev(a, oc=oc, bk=bk, scol=scol, po_=po_):
                self.WR[bk].wait(a)
                sv.wait_at(a, self.evac_sv.get((name, oc - 2), 0))
                i = a.activation(out=self.scr[:, po_ * TB:po_ * TB + TB],
                                 in_=_ap(self.ps[bk], 512, 0, [(1, TB)]),
                                 func=AF.Gelu, scale=self.scl[:, scol:scol + 1])
                self.RD[bk].inc(i)
            self.A(aev)

            def vev(v, oc=oc, bk=bk, po_=po_, hi_base=hi_base, lo_base=lo_base,
                    war_dg=war_dg):
                self.RD[bk].wait(v)
                if war_dg:
                    dg.wait_at(v, war_dg)
                v.drain()
                hid_ = self.rap(hi_base + oc * TB, [(1, TB)])
                v.tensor_copy(hid_, self.scr[:, po_ * TB:po_ * TB + TB])
                v.drain()
                v.tensor_tensor(self.sA[:, po_ * TB:po_ * TB + TB],
                                self.scr[:, po_ * TB:po_ * TB + TB], hid_.bitcast(F32),
                                op=ALU.subtract)
                v.drain()
                lod = self.hap(lo_base + oc * TB, [(1, TB)])
                sv.inc(v.tensor_copy(lod, self.sA[:, po_ * TB:po_ * TB + TB]))
            self.V(vev)
            self.evac_sv[(name, oc)] = sv.n
            return sv.n
        bk = banks[oi % 4]
        s3o = 4 * NHC

        po_ = oc % 2

        def aev(a, oc=oc, bk=bk, po_=po_):
            self.WR[bk].wait(a)
            sv.wait_at(a, self.evac_sv.get(("w3", oc - 2), 0))
            self.WR[0].wait(a)
            self.WR[4].wait(a)
            i = a.activation(out=self.scr[:, po_ * TB:po_ * TB + TB],
                             in_=_ap(self.ps[bk], 512, 0, [(1, TB)]),
                             func=AF.Copy, scale=self.scl[:, s3o + oc:s3o + oc + 1])
            self.RD[bk].inc(i)
        self.A(aev)
        tbk = 4 if bk < 4 else 0

        def ttr(t, bk=bk, tbk=tbk, po_=po_):
            self.RD[bk].wait(t)
            self.RD[tbk].wait(t)
            t.transpose(_ap(self.ps[tbk], 512, 0, [(1, 128)]),
                        self.scr[:, po_ * TB:po_ * TB + 128], self.idn[:, :])
            l2 = t.transpose(_ap(self.ps[tbk], 512, 128, [(1, 128)]),
                             self.scr[:, po_ * TB + 128:po_ * TB + 256], self.idn[:, :])
            self.WR[tbk].inc(l2)
        self.T(ttr)

        def vadd(v, oc=oc, tbk=tbk, tb=tb):
            self.WR[tbk].wait(v)
            v.tensor_tensor(_ap(self.xr, self.NT * self.D, 2 * tb * self.D + oc * 128,
                                [(1, 128)]),
                            _ap(self.xr, self.NT * self.D, 2 * tb * self.D + oc * 128,
                                [(1, 128)]),
                            _ap(self.ps[tbk], 512, 0, [(1, 128)]), op=ALU.add)
            i = v.tensor_tensor(_ap(self.xr, self.NT * self.D,
                                    (2 * tb + 1) * self.D + oc * 128, [(1, 128)]),
                                _ap(self.xr, self.NT * self.D,
                                    (2 * tb + 1) * self.D + oc * 128, [(1, 128)]),
                                _ap(self.ps[tbk], 512, 128, [(1, 128)]), op=ALU.add)
            self.RD[tbk].inc(i)
            v.drain()
            sv.inc(v.nop())
        self.V(vadd)
        self.evac_sv[("w3", oc)] = sv.n
        return sv.n

    def qkv_kv(self, tb):
        D, D3, NDC = self.D, self.D3, self.NDC
        sv, sa, st, dg = self.sv, self.sa, self.st, self.dg
        for tile in (2 * tb, 2 * tb + 1):
            self.lnorm(tile, lo16=False)
        n_ln2 = sv.n
        self.q_floor = (self.q_floor + [t for t in self.b16_free if t]
                        + [t for t in self.b32_free if t])
        prev_kv_dg = self.ar2_free_dg_kv

        for dc in range(NDC):
            p = dc % 2
            nld = {}

            def syb(sync, dc=dc, p=p, nld=nld):
                wait_toks(sync, self.wqk_free[p] + self.q_floor)
                dg.wait_at(sync, self.n_aw_ready)
                self.dqw[p].inc(sync.dma_start(
                    out=bass.AP(self.rp, self.QKB + p * self.PS,
                                [[self.RPL, 128], [128, NDC], [1, 128]]),
                    in_=bass.AP(self.awhi_d.tensor, D + dc * 128,
                                [[D3, 128], [128 * D3, NDC], [1, 128]])), 16)
                nld["n"] = self.dqw[p].inc(sync.dma_start(
                    out=bass.AP(self.rp, self.QKB + p * self.PS + self.D,
                                [[self.RPL, 128], [128, NDC], [1, 128]]),
                    in_=bass.AP(self.awlo_d.tensor, D + dc * 128,
                                [[D3, 128], [128 * D3, NDC], [1, 128]])), 16)
            self.G(syb)
            bk = dc % 4

            def tkm(t, dc=dc, p=p, bk=bk, nld=nld):
                self.dqw[p].wait_at(t, nld["n"])
                sv.wait_at(t, n_ln2)
                self.RD[bk].wait(t)
                po = _ap(self.ps[bk], 512, 0, [(1, TB)])
                last = None
                for k in range(NDC):
                    whi = self.rap(self.QKB + p * self.PS + k * 128, [(1, 128)])
                    wlo = self.rap(self.QKB + p * self.PS + self.D + k * 128, [(1, 128)])
                    mhi = self.rap(self.LNHI + k * TB, [(1, TB)])
                    mlo = self.rap(self.LNLO32 + k * TB, [(1, TB)])
                    t.matmul(po, whi, mhi, start=(k == 0), stop=False)
                    t.matmul(po, whi, mlo, start=False, stop=False)
                    last = t.matmul(po, wlo, mhi, start=False, stop=(k == NDC - 1))
                self.WR[bk].inc(last)
                self.wqk_free[p] = [(self.WR[bk], self.WR[bk].n)]
            self.T(tkm)

            def akv(a, dc=dc, bk=bk, prev_kv_dg=prev_kv_dg):
                self.WR[bk].wait(a)
                dg.wait_at(a, prev_kv_dg)
                i = a.activation(out=self.rap(self.KVSTG + dc * TB, [(1, TB)]),
                                 in_=_ap(self.ps[bk], 512, 0, [(1, TB)]), func=AF.Copy)
                self.RD[bk].inc(i)
                a.drain()
                sa.inc(a.nop())
            self.A(akv)

            def vkv(v, dc=dc, bk=bk):
                sa.wait(v)
                hi = self.rap(self.KVSTG + dc * TB, [(1, TB)])
                i = v.tensor_tensor(self.sA[:, 0:TB], _ap(self.ps[bk], 512, 0, [(1, TB)]),
                                    hi.bitcast(F32), op=ALU.subtract)
                self.RD[bk].inc(i)
                v.drain()
                lo = self.rap(self.KVSTG + 2 * D + dc * TB, [(1, TB)])
                v.tensor_copy(lo, self.sA[:, 0:TB])
                v.drain()
                sv.inc(v.nop())
            self.V(vkv)

        DH, NDH = self.DH, self.NDH
        for dh in range(NDH):
            kn = {}
            for k in range(NDC):
                pk = k % 2

                def syv(sync, dh=dh, k=k, pk=pk, kn=kn):
                    wait_toks(sync, self.wqk_free[pk] + self.q_floor)
                    dg.wait_at(sync, self.n_aw_ready)
                    self.dqw[pk].inc(sync.dma_start(
                        out=self.rap(self.QKB + pk * self.PS, [(1, DH)]),
                        in_=self.awhi_d[k * 128:(k + 1) * 128,
                                        2 * D + dh * DH:2 * D + (dh + 1) * DH]), 16)
                    kn[k] = self.dqw[pk].inc(sync.dma_start(
                        out=self.rap(self.QKB + pk * self.PS + self.D, [(1, DH)]),
                        in_=self.awlo_d[k * 128:(k + 1) * 128,
                                        2 * D + dh * DH:2 * D + (dh + 1) * DH]), 16)
                self.G(syv)

                def tvm(t, dh=dh, k=k, pk=pk, kn=kn):
                    self.dqw[pk].wait_at(t, kn[k])
                    sv.wait_at(t, n_ln2)
                    last = None
                    for tt in range(2):
                        bk = 4 + tt * NDH + dh
                        if k == 0:
                            self.RD[bk].wait(t)
                        po = _ap(self.ps[bk], 512, 0, [(1, DH)])
                        shi = self.rap(self.LNHI + k * TB + tt * 128, [(1, 128)])
                        slo = self.rap(self.LNLO32 + k * TB + tt * 128, [(1, 128)])
                        whi = self.rap(self.QKB + pk * self.PS, [(1, DH)])
                        wlo = self.rap(self.QKB + pk * self.PS + self.D, [(1, DH)])
                        t.matmul(po, shi, whi, start=(k == 0), stop=False)
                        t.matmul(po, slo, whi, start=False, stop=False)
                        last = t.matmul(po, shi, wlo, start=False, stop=(k == NDC - 1))
                        if k == NDC - 1:
                            self.WR[bk].inc(last)
                    if k == NDC - 1:
                        bk_last = 4 + NDH + dh
                        self.wqk_free[pk] = [(self.WR[bk_last], self.WR[bk_last].n)]
                    else:
                        self.wqk_free[pk] = [(st, st.inc(last))]
                self.T(tvm)

            for tt in range(2):
                bk = 4 + tt * NDH + dh

                def avv(a, tt=tt, dh=dh, bk=bk, prev_kv_dg=prev_kv_dg):
                    self.WR[bk].wait(a)
                    dg.wait_at(a, prev_kv_dg)
                    i = a.activation(out=self.rap(self.KVSTG + 4 * D + tt * D + dh * DH,
                                                  [(1, DH)]),
                                     in_=_ap(self.ps[bk], 512, 0, [(1, DH)]), func=AF.Copy)
                    self.RD[bk].inc(i)
                    a.drain()
                    sa.inc(a.nop())
                self.A(avv)

                def vvv(v, tt=tt, dh=dh, bk=bk):
                    sa.wait(v)
                    hi = self.rap(self.KVSTG + 4 * D + tt * D + dh * DH, [(1, DH)])
                    i = v.tensor_tensor(self.sA[:, 0:DH], _ap(self.ps[bk], 512, 0, [(1, DH)]),
                                        hi.bitcast(F32), op=ALU.subtract)
                    self.RD[bk].inc(i)
                    v.drain()
                    lo = self.rap(self.KVSTG + 6 * D + tt * D + dh * DH, [(1, DH)])
                    v.tensor_copy(lo, self.sA[:, 0:DH])
                    v.drain()
                    sv.inc(v.nop())
                self.V(vvv)
        n_v_done = sv.n

        def gkv(g, tb=tb, n_v_done=n_v_done):
            sv.wait_at(g, n_v_done)
            g.wait_ge(dg.h, dg.n)
            ksrc = lambda off: bass.AP(self.rp, off, [[self.RPL, 128], [TB, NDC], [1, TB]])
            kdst = lambda d: bass.AP(d.tensor, 0, [[TB, 128], [128 * TB, NDC], [1, TB]])
            dg.inc(g.dma_start(out=kdst(self.km["kh"][tb]), in_=ksrc(self.KVSTG)), 16)
            dg.inc(g.dma_start(out=kdst(self.km["kl"][tb]), in_=ksrc(self.KVSTG + 2 * D)), 16)
            vsrc = lambda off: bass.AP(self.rp, off, [[self.RPL, 128], [D, 2], [1, D]])
            vdst = lambda d: bass.AP(d.tensor, 0, [[D, 128], [128 * D, 2], [1, D]])
            dg.inc(g.dma_start(out=vdst(self.km["vh"][tb]), in_=vsrc(self.KVSTG + 4 * D)), 16)
            n4 = dg.inc(g.dma_start(out=vdst(self.km["vl"][tb]),
                                    in_=vsrc(self.KVSTG + 6 * D)), 16)
            g.wait_ge(dg.h, n4)
            for kind in ("kh", "kl", "vh", "vl"):
                self.cc.inc(g.collective_compute(
                    "AllGather", ALU.bypass, replica_groups=RG,
                    ins=[self.km[kind][tb][:, :]], outs=[self.ka[kind][tb][:, :]]))
        self.G(gkv)
        self.ar2_free_dg_kv = dg.n

    def attention(self):
        D, NDC, NKC, SEQ = self.D, self.NDC, self.NKC, self.SEQ
        DH, NDH = self.DH, self.NDH
        sv, sa, st, dg, cc = self.sv, self.sa, self.st, self.dg, self.cc
        n_ag_all = cc.n
        n_kv_stores = dg.n
        self.q_floor = (self.q_floor + [t for t in self.b16_free if t]
                        + [t for t in self.b32_free if t])
        init_kv = ([(st, st.n), (sv, sv.n), (sa, sa.n)]
                   + [(self.WR[b], self.WR[b].n) for b in range(8)])
        self.kv_free = [list(init_kv), list(init_kv)]
        inv_sqrt_d = float(1.0 / np.sqrt(float(D)))
        for qb in range(self.NQB):
            t0 = 2 * qb
            for tile in (t0, t0 + 1):
                self.lnorm(tile, lo16=False)
            n_ln2 = sv.n

            for dc in range(NDC):
                p = dc % 2
                nld = {}

                def syq(sync, dc=dc, p=p, nld=nld):
                    wait_toks(sync, self.wqk_free[p] + self.q_floor
                              + self.kv_free[0] + self.kv_free[1])
                    dg.wait_at(sync, self.n_aw_ready)
                    self.dqw[p].inc(sync.dma_start(
                        out=bass.AP(self.rp, self.QB2 + p * self.PS,
                                    [[self.RPL, 128], [128, NDC], [1, 128]]),
                        in_=bass.AP(self.awhi_d.tensor, dc * 128,
                                    [[self.D3, 128], [128 * self.D3, NDC], [1, 128]])), 16)
                    nld["n"] = self.dqw[p].inc(sync.dma_start(
                        out=bass.AP(self.rp, self.QB2 + p * self.PS + self.D,
                                    [[self.RPL, 128], [128, NDC], [1, 128]]),
                        in_=bass.AP(self.awlo_d.tensor, dc * 128,
                                    [[self.D3, 128], [128 * self.D3, NDC], [1, 128]])), 16)
                self.G(syq)
                bk = dc % 4

                def tqm(t, p=p, bk=bk, nld=nld):
                    self.dqw[p].wait_at(t, nld["n"])
                    sv.wait_at(t, n_ln2)
                    self.RD[bk].wait(t)
                    po = _ap(self.ps[bk], 512, 0, [(1, TB)])
                    last = None
                    for k in range(NDC):
                        whi = self.rap(self.QB2 + p * self.PS + k * 128, [(1, 128)])
                        wlo = self.rap(self.QB2 + p * self.PS + self.D + k * 128, [(1, 128)])
                        mhi = self.rap(self.LNHI + k * TB, [(1, TB)])
                        mlo = self.rap(self.LNLO32 + k * TB, [(1, TB)])
                        t.matmul(po, whi, mhi, start=(k == 0), stop=False)
                        t.matmul(po, whi, mlo, start=False, stop=False)
                        last = t.matmul(po, wlo, mhi, start=False, stop=(k == NDC - 1))
                    self.WR[bk].inc(last)
                    self.wqk_free[p] = [(self.WR[bk], self.WR[bk].n)]
                self.T(tqm)

                def aqe(a, dc=dc, bk=bk):
                    self.WR[bk].wait(a)
                    i = a.activation(out=self.rap(self.QH + dc * TB, [(1, TB)]),
                                     in_=_ap(self.ps[bk], 512, 0, [(1, TB)]), func=AF.Copy)
                    self.RD[bk].inc(i)
                    a.drain()
                    sa.inc(a.nop())
                self.A(aqe)

                def vqe(v, dc=dc, bk=bk):
                    sa.wait(v)
                    hi = self.rap(self.QH + dc * TB, [(1, TB)])
                    i = v.tensor_tensor(self.sA[:, 0:TB], _ap(self.ps[bk], 512, 0, [(1, TB)]),
                                        hi.bitcast(F32), op=ALU.subtract)
                    self.RD[bk].inc(i)
                    v.drain()
                    lo = self.rap(self.QH + 2 * D + dc * TB, [(1, TB)])
                    v.tensor_copy(lo, self.sA[:, 0:TB])
                    v.drain()
                    sv.inc(v.nop())
                self.V(vqe)
            n_q_done = sv.n

            def vqs(v, t0=t0):
                for ti in range(2):
                    v.tensor_scalar(self.red[:, 32 + ti * 16:32 + ti * 16 + NKC],
                                    self.iot[:, 0:NKC], -float(TB),
                                    self.qps[:, t0 + ti:t0 + ti + 1],
                                    op0=ALU.mult, op1=ALU.add)
                v.drain()
                sv.inc(v.nop())
            self.V(vqs)
            n_qs = sv.n

            for kc in range(NKC):
                p = kc % 2
                r, sb_ = kc // self.NBLK, kc % self.NBLK
                nld = {}

                def syk(sync, p=p, r=r, sb_=sb_, nld=nld):
                    cc.wait_at(sync, n_ag_all)
                    dg.wait_at(sync, n_kv_stores)
                    wait_toks(sync, self.kv_free[p])
                    self.dkv[p].inc(sync.dma_start(
                        out=bass.AP(self.rp, self.KVS + p * self.PS,
                                    [[self.RPL, 128], [TB, NDC], [1, TB]]),
                        in_=bass.AP(self.ka["kh"][sb_].tensor, r * D * TB,
                                    [[TB, 128], [128 * TB, NDC], [1, TB]])), 16)
                    nld["n"] = self.dkv[p].inc(sync.dma_start(
                        out=bass.AP(self.rp, self.KVSL + p * self.PS,
                                    [[self.RPL, 128], [TB, NDC], [1, TB]]),
                        in_=bass.AP(self.ka["kl"][sb_].tensor, r * D * TB,
                                    [[TB, 128], [128 * TB, NDC], [1, TB]])), 16)
                self.G(syk)

                for ti in range(2):
                    bk = ti

                    def tsc(t, kc=kc, p=p, ti=ti, bk=bk, nld=nld):
                        self.dkv[p].wait_at(t, nld["n"])
                        sv.wait_at(t, n_q_done)
                        self.RD[bk].wait(t)
                        po = _ap(self.ps[bk], 512, 0, [(1, TB)])
                        last = None
                        for k in range(NDC):
                            qhi = self.rap(self.QH + k * TB + ti * 128, [(1, 128)])
                            qlo = self.rap(self.QH + 2 * D + k * TB + ti * 128, [(1, 128)])
                            khi = self.rap(self.KVS + p * self.PS + k * TB, [(1, TB)])
                            klo = self.rap(self.KVSL + p * self.PS + k * TB, [(1, TB)])
                            t.matmul(po, qhi, khi, start=(k == 0), stop=False)
                            t.matmul(po, qhi, klo, start=False, stop=False)
                            last = t.matmul(po, qlo, khi, start=False, stop=(k == NDC - 1))
                        self.WR[bk].inc(last)
                        if ti == 1:
                            self.kv_free[p] = [(self.WR[0], self.WR[0].n),
                                               (self.WR[1], self.WR[1].n)]
                    self.T(tsc)

                    def vsc(v, kc=kc, ti=ti, bk=bk):
                        sv.wait_at(v, n_qs)
                        self.WR[bk].wait(v)
                        dg.wait_at(v, self.n_aw_ready)
                        v.tensor_scalar(self.sA[:, 0:TB], self.iot[:, 0:TB],
                                        self.red[:, 32 + ti * 16 + kc:33 + ti * 16 + kc],
                                        NEG, op0=ALU.is_lt, op1=ALU.mult)
                        v.drain()
                        i = v.tensor_scalar(self.sB[:, 0:TB],
                                            _ap(self.ps[bk], 512, 0, [(1, TB)]),
                                            inv_sqrt_d, None, op0=ALU.mult)
                        self.RD[bk].inc(i)
                        v.drain()
                        v.tensor_tensor(
                            self.fap(self.SCO + ti * SEQ + kc * TB, [(1, TB)]),
                            self.sB[:, 0:TB], self.sA[:, 0:TB], op=ALU.add)
                        v.drain()
                        sv.inc(v.nop())
                    self.V(vsc)
            n_scores = sv.n

            for ti in range(2):
                base = self.SCO + ti * SEQ
                mx, ng, sm, rc = 17 + 3 * ti, 18 + 3 * ti, 19 + 3 * ti, 23 + ti

                def vsm(v, base=base, mx=mx, ng=ng):
                    sv.wait_at(v, n_scores)
                    v.reduce_max(self.red[:, mx:mx + 1], self.fap(base, [(1, SEQ)]),
                                 axis=AXX)
                    v.drain()
                    v.tensor_scalar(self.red[:, ng:ng + 1], self.red[:, mx:mx + 1], -1.0,
                                    None, op0=ALU.mult)
                    v.drain()
                    sv.inc(v.nop())
                self.V(vsm)

                def asm(a, base=base, ng=ng, sm=sm):
                    sv.wait(a)
                    a.activation(out=self.fap(base, [(1, SEQ)]),
                                 in_=self.fap(base, [(1, SEQ)]),
                                 func=AF.Exp, bias=self.red[:, ng:ng + 1], scale=1.0,
                                 accum_out=self.red[:, sm:sm + 1])
                    a.drain()
                    sa.inc(a.nop())
                self.A(asm)

                def vrc(v, sm=sm, rc=rc):
                    sa.wait(v)
                    v.reciprocal(self.red[:, rc:rc + 1], self.red[:, sm:sm + 1])
                    v.drain()
                    sv.inc(v.nop())
                self.V(vrc)

                def apb(a, base=base, rc=rc):
                    sv.wait(a)
                    a.activation(out=self.fap(base, [(1, SEQ)]),
                                 in_=self.fap(base, [(1, SEQ)]),
                                 func=AF.Copy, scale=self.red[:, rc:rc + 1])
                    a.drain()
                    sa.inc(a.nop())
                self.A(apb)
            n_probs_sa = sa.n

            for kc in range(NKC):
                p = kc % 2
                r, sb_ = kc // self.NBLK, kc % self.NBLK
                nld = {}

                def syv(sync, p=p, r=r, sb_=sb_, nld=nld):
                    dg.wait_at(sync, n_kv_stores)
                    wait_toks(sync, self.kv_free[p])
                    self.dkv[p].inc(sync.dma_start(
                        out=bass.AP(self.rp, self.KVS + p * self.PS,
                                    [[self.RPL, 128], [D, 2], [1, D]]),
                        in_=bass.AP(self.ka["vh"][sb_].tensor, r * TB * D,
                                    [[D, 128], [128 * D, 2], [1, D]])), 16)
                    nld["n"] = self.dkv[p].inc(sync.dma_start(
                        out=bass.AP(self.rp, self.KVSL + p * self.PS,
                                    [[self.RPL, 128], [D, 2], [1, D]]),
                        in_=bass.AP(self.ka["vl"][sb_].tensor, r * TB * D,
                                    [[D, 128], [128 * D, 2], [1, D]])), 16)
                self.G(syv)

                for ti in range(2):
                    base = self.SCO + ti * SEQ
                    tbk = 4 + (kc % 2) * 2 + ti
                    slot = (kc % 2) * 2 + ti
                    aT_hi = self.ATT + slot * 256
                    aT_lo = self.ATT + 1024 + slot * 256

                    def ttr(t, kc=kc, base=base, tbk=tbk):
                        sa.wait_at(t, n_probs_sa)
                        self.RD[tbk].wait(t)
                        t.transpose(_ap(self.ps[tbk], 512, 0, [(1, 128)]),
                                    self.fap(base + kc * TB, [(1, 128)]),
                                    self.idn[:, :])
                        l2 = t.transpose(_ap(self.ps[tbk], 512, 128, [(1, 128)]),
                                         self.fap(base + kc * TB + 128, [(1, 128)]),
                                         self.idn[:, :])
                        self.WR[tbk].inc(l2)
                    self.T(ttr)

                    def atv(a, tbk=tbk, aT_hi=aT_hi):
                        self.WR[tbk].wait(a)
                        i = a.activation(out=self.rap(aT_hi, [(1, 256)]),
                                         in_=_ap(self.ps[tbk], 512, 0, [(1, 256)]),
                                         func=AF.Copy)
                        self.RD[tbk].inc(i)
                        a.drain()
                        sa.inc(a.nop())
                    self.A(atv)

                    def vtv(v, tbk=tbk, aT_hi=aT_hi, aT_lo=aT_lo):
                        sa.wait(v)
                        hi = self.rap(aT_hi, [(1, 256)])
                        i = v.tensor_tensor(self.sA[:, 0:256],
                                            _ap(self.ps[tbk], 512, 0, [(1, 256)]),
                                            hi.bitcast(F32), op=ALU.subtract)
                        self.RD[tbk].inc(i)
                        v.drain()
                        v.tensor_copy(self.rap(aT_lo, [(1, 256)]), self.sA[:, 0:256])
                        v.drain()
                        sv.inc(v.nop())
                    self.V(vtv)
                    n_at = sv.n

                    for dh in range(NDH):
                        avbk = ti * NDH + dh

                        def tav(t, kc=kc, ti=ti, dh=dh, p=p, avbk=avbk, n_at=n_at,
                                aT_hi=aT_hi, aT_lo=aT_lo, nld=nld):
                            sv.wait_at(t, n_at)
                            self.dkv[p].wait_at(t, nld["n"])
                            if kc == 0:
                                self.RD[avbk].wait(t)
                            po = _ap(self.ps[avbk], 512, 0, [(1, DH)])
                            last = None
                            for kk in range(2):
                                ahi = self.rap(aT_hi + kk * 128, [(1, 128)])
                                alo = self.rap(aT_lo + kk * 128, [(1, 128)])
                                vhi = self.rap(self.KVS + p * self.PS + kk * D + dh * DH,
                                               [(1, DH)])
                                vlo = self.rap(self.KVSL + p * self.PS + kk * D + dh * DH,
                                               [(1, DH)])
                                first = (kc == 0 and kk == 0)
                                lastf = (kc == NKC - 1 and kk == 1)
                                t.matmul(po, ahi, vhi, start=first, stop=False)
                                t.matmul(po, ahi, vlo, start=False, stop=False)
                                last = t.matmul(po, alo, vhi, start=False, stop=lastf)
                            if kc == NKC - 1:
                                self.WR[avbk].inc(last)
                            elif dh == NDH - 1:
                                n = st.inc(last)
                                if ti == 1:
                                    self.kv_free[p] = [(st, n)]
                        self.T(tav)

            self.kv_free[(NKC - 1) % 2] = [
                (self.WR[b], self.WR[b].n) for b in range(2 * NDH)]

            for ti in range(2):
                for dh in range(NDH):
                    avbk = ti * NDH + dh

                    def vav(v, ti=ti, dh=dh, avbk=avbk, t0=t0):
                        self.WR[avbk].wait(v)
                        tile = t0 + ti
                        dst = _ap(self.xr, self.NT * D, tile * D + dh * DH, [(1, DH)])
                        last = v.tensor_tensor(dst, dst,
                                               _ap(self.ps[avbk], 512, 0, [(1, DH)]),
                                               op=ALU.add)
                        self.RD[avbk].inc(last)
                        v.drain()
                        sv.inc(v.nop())
                    self.V(vav)

    def store_out(self, tb):
        sv, dg = self.sv, self.dg
        n_ready = sv.n

        def gst(g, tb=tb, n_ready=n_ready):
            sv.wait_at(g, n_ready)
            g.wait_ge(dg.h, dg.n)
            for h2_ in range(2):
                tile = 2 * tb + h2_
                dg.inc(g.dma_start(
                    out=self.out_d[tile * 128:(tile + 1) * 128, :],
                    in_=_ap(self.xr, self.NT * self.D, tile * self.D, [(1, self.D)])), 16)
        self.G(gst)

    def finish(self):
        dg = self.dg

        def gf(g):
            g.wait_ge(dg.h, dg.n)
        self.G(gf)


def build_nc():
    return Build().build()


def kernel(L0_attn_x, L0_attn_W, L0_ffn_w1_w, L0_ffn_w1_s, L0_ffn_w2_w,
           L0_ffn_w2_s, L0_ffn_h1_w, L0_ffn_h1_s, L0_ffn_h2_w, L0_ffn_h2_s,
           L0_ffn_w3_w, L0_ffn_w3_s):
    D, SEQ, TOK = CFG["D"], CFG["SEQ"], CFG["TOK"]
    NT = TOK // 128
    x = np.ascontiguousarray(np.asarray(L0_attn_x, dtype=np.float32))
    aw = np.ascontiguousarray(np.asarray(L0_attn_W, dtype=np.float32))
    w = {n: np.ascontiguousarray(np.asarray(a, dtype=np.float32)) for n, a in
         (("w1", L0_ffn_w1_w), ("w2", L0_ffn_w2_w), ("h1", L0_ffn_h1_w),
          ("h2", L0_ffn_h2_w), ("w3", L0_ffn_w3_w))}
    s = {n: np.ascontiguousarray(np.asarray(a, dtype=np.float32).reshape(-1, 128).T)
         for n, a in (("s1", L0_ffn_w1_s), ("s2", L0_ffn_w2_s), ("sh1", L0_ffn_h1_s),
                      ("sh2", L0_ffn_h2_s), ("s3", L0_ffn_w3_s))}
    ident = np.eye(128, dtype=np.float32)
    nc = build_nc()
    in_maps = []
    for c in range(CFG["NCORE"]):
        b, h = c % 4, c // 4
        qpos = (h * TOK + np.arange(NT)[None, :] * 128
                + np.arange(128)[:, None]).astype(np.float32)
        m = {"x": np.ascontiguousarray(x[b, h * TOK:(h + 1) * TOK, :]), "aw": aw,
             "ident": ident, "qpos": np.ascontiguousarray(qpos)}
        for n in w:
            m[n] = w[n]
        for n in s:
            m[n + "c"] = s[n]
        in_maps.append(m)
    res = run_bass_kernel_spmd(nc, in_maps, core_ids=list(range(CFG["NCORE"])))
    out = np.empty((4, SEQ, D), np.float32)
    for c, r in enumerate(res.results):
        b, h = c % 4, c // 4
        out[b, h * TOK:(h + 1) * TOK, :] = r["out"]
    return out


# version 08
# speedup vs baseline: 3.1126x; 1.2723x over previous
import contextlib
import os

os.environ.setdefault("GAUGE_NTFF_IGNORE_DMA", "1")

import numpy as np

import concourse.bass as bass
import concourse.mybir as mybir
from concourse.bass_utils import run_bass_kernel_spmd

F32 = mybir.dt.float32
F32R = mybir.dt.float32r
F16 = mybir.dt.float16
AF = mybir.ActivationFunctionType
ALU = mybir.AluOpType
AXX = mybir.AxisListType.X

CFG = dict(D=1024, HID=4096, SEQ=4096, TOK=2048, NCORE=8)
TB = 256
NEG = -9.0e15
EPS = 1e-5
RG = [[0, 4], [1, 5], [2, 6], [3, 7]]
PHASES = ("pre", "attn", "post")


class Sem:
    __slots__ = ("h", "n")

    def __init__(self, h):
        self.h = h
        self.n = 0

    def inc(self, instr, k=1):
        instr.then_inc(self.h, k)
        self.n += k
        return self.n

    def wait_at(self, eng, n):
        assert 0 <= n <= self.n, (n, self.n)
        if n <= 0:
            return
        key = (type(eng).__name__, id(self))
        if WAITCACHE.get(key, 0) >= n:
            return
        WAITCACHE[key] = n
        eng.wait_ge(self.h, n)

    def wait(self, eng):
        self.wait_at(eng, self.n)


WAITCACHE = {}


def wait_toks(eng, toks):
    if not toks:
        return
    best = {}
    for sem, val in toks:
        if val > best.get(id(sem), (None, -1))[1]:
            best[id(sem)] = (sem, val)
    for sem, val in best.values():
        sem.wait_at(eng, val)


def _ap(t, rowlen, off, dims):
    return bass.AP(t, off, [[rowlen, 128]] + [[s, c] for s, c in dims])


class Build:
    def __init__(self):
        self.D = CFG["D"]
        self.HID = CFG["HID"]
        self.SEQ = CFG["SEQ"]
        self.TOK = CFG["TOK"]
        self.D3 = 3 * self.D
        self.NT = self.TOK // 128
        self.NBLK = self.TOK // TB
        self.NDC = self.D // 128
        self.NHC = self.HID // 128
        self.NKC = self.SEQ // TB
        self.NQB = self.TOK // TB
        self.DH = min(512, self.D)
        self.NDH = self.D // self.DH
        WAITCACHE.clear()
        D, SEQ, D3 = self.D, self.SEQ, self.D3
        self.LNHI = 0
        self.LNLO32 = 2 * D
        self.WB32 = 4 * D
        self.PS = 2 * D
        self.HHI = 4 * D + 2048
        self.GHI = self.HHI + max(self.NHC * TB, 4 * self.PS)
        self.KVSTG = self.HHI
        self.QKB = self.GHI
        self.KVS = self.HHI
        self.KVSL = self.HHI + 2 * self.PS
        self.QH = self.GHI
        self.QB2 = self.GHI + 4 * D
        self.ATT = self.QB2
        self.RPL = max(self.GHI + self.NHC * TB, self.QB2 + 2 * self.PS,
                       self.ATT + 2048)
        self.SCO = 0
        self.LNLO16 = 0
        self.WB16 = 2 * D
        self.HLO = 2 * D + 4096
        self.GLO = self.HLO + self.NHC * TB
        f16_need = self.GLO + self.NHC * TB
        self.FAL = max(2 * SEQ, 3 * D3, (f16_need + 1) // 2)
        self.nc = bass.Bass("TRN2", target_bir_lowering=False)

    def T(self, f):
        self.blk.tensor(f)

    def V(self, f):
        self.blk.vector(f)

    def A(self, f):
        self.blk.scalar(f)

    def G(self, f):
        self.blk.gpsimd(f)

    def SY(self, f):
        self.blk.sync(f)

    def rap(self, off, dims):
        return _ap(self.rp, self.RPL, off, dims)

    def fap(self, off, dims):
        return _ap(self.fa, self.FAL, off, dims)

    def hap(self, off, dims):
        return _ap(self.fa16, 2 * self.FAL, off, dims)

    def build(self):
        nc, D, HID, D3 = self.nc, self.D, self.HID, self.D3
        TOK, NT, NBLK = self.TOK, self.NT, self.NBLK
        self.x_d = nc.dram_tensor("x", [TOK, D], F32, kind="ExternalInput").ap()
        self.aw_d = nc.dram_tensor("aw", [D, D3], F32, kind="ExternalInput").ap()
        self.w_d = {n: nc.dram_tensor(n, list(s), F32, kind="ExternalInput").ap()
                    for n, s in (("w1", (HID, D)), ("w2", (HID, D)),
                                 ("h1", (HID, HID)), ("h2", (HID, HID)),
                                 ("w3", (D, HID)))}
        self.s_d = {n: nc.dram_tensor(n + "c", [128, w], F32, kind="ExternalInput").ap()
                    for n, w in (("s1", self.NHC), ("s2", self.NHC), ("sh1", self.NHC),
                                 ("sh2", self.NHC), ("s3", self.NDC))}
        self.ident_d = nc.dram_tensor("ident", [128, 128], F32, kind="ExternalInput").ap()
        self.qpos_d = nc.dram_tensor("qpos", [128, NT], F32, kind="ExternalInput").ap()
        self.out_d = nc.dram_tensor("out", [TOK, D], F32, kind="ExternalOutput").ap()
        self.wf_d = {n: nc.dram_tensor(n + "f", list(self.w_d[n].shape), F16,
                                       kind="Internal").ap() for n in self.w_d}
        self.awhi_d = nc.dram_tensor("awhi", [D, D3], F32R, kind="Internal").ap()
        self.awlo_d = nc.dram_tensor("awlo", [D, D3], F32R, kind="Internal").ap()
        self.km, self.ka = {}, {}
        for kind in ("kh", "kl"):
            self.km[kind] = [nc.dram_tensor(f"{kind}m{b}", [D, TB], F32R,
                                            kind="Internal").ap() for b in range(NBLK)]
            self.ka[kind] = [nc.dram_tensor(f"{kind}a{b}", [2 * D, TB], F32R,
                                            kind="Internal").ap() for b in range(NBLK)]
        for kind in ("vh", "vl"):
            self.km[kind] = [nc.dram_tensor(f"{kind}m{b}", [TB, D], F32R,
                                            kind="Internal").ap() for b in range(NBLK)]
            self.ka[kind] = [nc.dram_tensor(f"{kind}a{b}", [2 * TB, D], F32R,
                                            kind="Internal").ap() for b in range(NBLK)]

        es = contextlib.ExitStack()
        with es:
            sb = lambda name, cols, dt: es.enter_context(nc.sbuf_tensor(name, [128, cols], dt))
            self.xr = sb("xr", NT * D, F32)
            self.rp = sb("rp", self.RPL, F32R)
            self.fa = sb("fa", self.FAL, F32)
            self.scr = sb("scr", max(D, 2 * TB), F32)
            self.sA = sb("sA", 512, F32)
            self.sB = sb("sB", 512, F32)
            self.red = sb("red", 64, F32)
            nsc = 4 * self.NHC + self.NDC
            self.scl = sb("scl", nsc, F32)
            self.idn = sb("idn", 128, F32)
            self.qps = sb("qps", NT, F32)
            self.iot = sb("iot", 256, F32)
            self.ps = [es.enter_context(nc.psum_tensor(f"ps{i}", [128, 512], F32))
                       for i in range(8)]
            mk = lambda n: Sem(es.enter_context(nc.semaphore(n)))
            self.dx, self.dxa, self.dg = mk("dx"), mk("dxa"), mk("dg")
            self.dct = {n: mk("dct_" + n) for n in ("w1", "w2", "h1", "h2", "w3")}
            self.dt16 = [mk(f"dt16_{i}") for i in range(8)]
            self.dqw = [mk("dqw0"), mk("dqw1")]
            self.dkv = [mk("dkv0"), mk("dkv1")]
            self.sv, self.sa, self.st, self.cc = mk("sv"), mk("sa"), mk("st"), mk("cc")
            self.WR = [mk(f"wr{i}") for i in range(8)]
            self.RD = [mk(f"rd{i}") for i in range(8)]
            self.blk = es.enter_context(nc.Block())

            self.fa16 = self.fa.bitcast(F16)
            self.fa32r = self.fa.bitcast(F32R)

            self.b16_ready = [0] * 8
            self.b32_ready = [None] * 4
            self.b16_free = [None] * 8
            self.b32_free = [None] * 4
            self.b16_next = 0
            self.b32_next = 0
            self.wqk_free = [[], []]
            self.kv_free = [[], []]
            self.ffn_floor = []
            self.q_floor = []
            self.ar1_free_dg = 0
            self.ar2_free_dg_kv = 0
            self.evac_sv = {}
            self.evac_st = {}

            self.phase0()
            self.split_aw()
            if "pre" in PHASES:
                self.ffn_call(0)
            if "attn" in PHASES:
                self.attention()
            if "post" in PHASES:
                self.ffn_call(1)
            else:
                for tb in range(NBLK):
                    self.store_out(tb)
            self.finish()
        return nc

    def phase0(self):
        D, NT = self.D, self.NT
        NHC, NDC = self.NHC, self.NDC
        dx, sv = self.dx, self.sv
        xr, scl, idn, qps, iot = self.xr, self.scl, self.idn, self.qps, self.iot
        x_d, s_d = self.x_d, self.s_d

        def sy0(sync):
            xin = bass.AP(x_d.tensor, 0, [[D, 128], [128 * D, NT], [1, D]])
            xout = bass.AP(xr, 0, [[NT * D, 128], [D, NT], [1, D]])
            dx.inc(sync.dma_start(out=xout, in_=xin), 16)
            off = 0
            for n in ("s1", "s2", "sh1", "sh2"):
                dx.inc(sync.dma_start(out=scl[:, off:off + NHC], in_=s_d[n][:, :]), 16)
                off += NHC
            dx.inc(sync.dma_start(out=scl[:, off:off + NDC], in_=s_d["s3"][:, :]), 16)
            dx.inc(sync.dma_start(out=idn[:, :], in_=self.ident_d[:, :]), 16)
            dx.inc(sync.dma_start(out=qps[:, :], in_=self.qpos_d[:, :]), 16)
        self.G(sy0)
        self.n_small_loads = dx.n
        self.cast_done = {}

        def g0(g):
            g.iota(iot[:, :], [[1, 256]], channel_multiplier=0,
                   allow_small_or_imprecise_dtypes=True)
            for name in ("w1", "w2", "h1", "h2", "w3"):
                src, dst = self.w_d[name], self.wf_d[name]
                rows = src.shape[0]
                step = min(rows, 1024)
                for r0 in range(0, rows, step):
                    self.cast_done[name] = self.dct[name].inc(
                        g.dma_start(out=dst[r0:r0 + step, :],
                                    in_=src[r0:r0 + step, :]), 16)
        self.G(g0)

        def v0(v):
            dx.wait_at(v, self.n_small_loads)
            s3o = 4 * NHC
            v.tensor_scalar(scl[:, s3o:s3o + NDC], scl[:, s3o:s3o + NDC], 0.5, None,
                            op0=ALU.mult)
            v.drain()
            sv.inc(v.nop())
        self.V(v0)
        self.n_x_ready = self.n_small_loads

    def split_aw(self):
        D, D3 = self.D, self.D3
        dxa, dg, sv = self.dxa, self.dg, self.sv
        for r in range(D // 128):
            nl = {}

            def syl(sync, r=r, nl=nl):
                dg.wait_at(sync, self.ar1_free_dg)
                nl["n"] = dxa.inc(sync.dma_start(
                    out=self.fap(0, [(1, D3)]),
                    in_=self.aw_d[r * 128:(r + 1) * 128, :]), 16)
            self.G(syl)

            def vsp(v, nl=nl):
                dxa.wait_at(v, nl["n"])
                src = self.fap(0, [(1, D3)])
                hi = _ap(self.fa32r, self.FAL, D3, [(1, D3)])
                d_ = self.fap(2 * D3, [(1, D3)])
                lo = _ap(self.fa32r, self.FAL, 2 * D3, [(1, D3)])
                v.tensor_copy(hi, src)
                v.drain()
                v.tensor_tensor(d_, src, hi.bitcast(F32), op=ALU.subtract)
                v.drain()
                v.tensor_copy(lo, d_)
                v.drain()
                sv.inc(v.nop())
            self.V(vsp)
            nsv = sv.n

            def gst(g, r=r, nsv=nsv):
                sv.wait_at(g, nsv)
                hi = _ap(self.fa32r, self.FAL, D3, [(1, D3)])
                lo = _ap(self.fa32r, self.FAL, 2 * D3, [(1, D3)])
                dg.inc(g.dma_start(out=self.awhi_d[r * 128:(r + 1) * 128, :], in_=hi), 16)
                dg.inc(g.dma_start(out=self.awlo_d[r * 128:(r + 1) * 128, :], in_=lo), 16)
            self.G(gst)
            self.ar1_free_dg = dg.n
        self.n_aw_ready = dg.n

    def lnorm(self, tile, lo16):
        D = self.D
        sv, sa = self.sv, self.sa
        xr, red, scr, sA, idn = self.xr, self.red, self.scr, self.sA, self.idn
        half = tile % 2
        xcol = tile * D
        nch = max(1, D // 512)
        w = min(D, 512)

        def vln(v):
            self.dx.wait_at(v, self.n_x_ready)
            v.drain()
            for i in range(nch):
                v.bn_stats(red[:, 6 * i:6 * i + 6],
                           _ap(xr, self.NT * D, xcol + i * w, [(1, w)]))
            v.drain()
            v.bn_aggr(red[:, 12:14], red[:, 0:6 * nch])
            v.drain()
            v.tensor_scalar(red[:, 14:15], red[:, 13:14], EPS, None, op0=ALU.add)
            v.drain()
            sv.inc(v.reciprocal(red[:, 15:16], red[:, 14:15]))
        self.V(vln)

        def aln(a):
            sv.wait(a)
            sa.inc(a.activation(out=red[:, 16:17], in_=red[:, 15:16], func=AF.Sqrt))
        self.A(aln)

        def vn2(v):
            sa.wait(v)
            for b in range(8):
                self.WR[b].wait(v)
            v.drain()
            sv.inc(v.tensor_scalar(scr[:, 0:D], _ap(xr, self.NT * D, xcol, [(1, D)]),
                                   red[:, 12:13], red[:, 16:17],
                                   op0=ALU.subtract, op1=ALU.mult))
        self.V(vn2)
        n_ln = sv.n

        for gi in range(0, self.NDC, 4):
            grp = list(range(gi, min(self.NDC, gi + 4)))
            bk = 4 + (gi // 4) % 2

            def tt(t, grp=grp, bk=bk, n_ln=n_ln):
                sv.wait_at(t, n_ln)
                self.RD[bk].wait(t)
                last = None
                for j, fc in enumerate(grp):
                    last = t.transpose(_ap(self.ps[bk], 512, j * 128, [(1, 128)]),
                                       scr[:, fc * 128:(fc + 1) * 128], idn[:, :])
                self.WR[bk].inc(last)
            self.T(tt)

            def vev(v, grp=grp, bk=bk, half=half):
                self.WR[bk].wait(v)
                self.dg.wait_at(v, getattr(self, "n_aw_ready", 0))
                v.drain()
                dst = bass.AP(self.rp, self.LNHI + grp[0] * TB + half * 128,
                              [[self.RPL, 128], [TB, len(grp)], [1, 128]])
                i = v.tensor_copy(dst, _ap(self.ps[bk], 512, 0,
                                           [(1, len(grp) * 128)]))
                self.RD[bk].inc(i)
                v.drain()
                hi_f32 = bass.AP(self.rp, self.LNHI + grp[0] * TB + half * 128,
                                 [[self.RPL, 128], [TB, len(grp)], [1, 128]]).bitcast(F32)
                i = v.tensor_tensor(sA[:, 0:len(grp) * 128],
                                    _ap(self.ps[bk], 512, 0, [(1, len(grp) * 128)]),
                                    hi_f32, op=ALU.subtract)
                self.RD[bk].inc(i)
                v.drain()
                if lo16:
                    dst = bass.AP(self.fa16, self.LNLO16 + grp[0] * TB + half * 128,
                                  [[2 * self.FAL, 128], [TB, len(grp)], [1, 128]])
                else:
                    dst = bass.AP(self.rp, self.LNLO32 + grp[0] * TB + half * 128,
                                  [[self.RPL, 128], [TB, len(grp)], [1, 128]])
                sv.inc(v.tensor_copy(dst, sA[:, 0:len(grp) * 128]))
            self.V(vev)

    def band_load(self, wname, rows0, nrows, kc):
        s = self.b16_next % 8
        self.b16_next += 1

        def adma(sy, s=s):
            wait_toks(sy, ([self.b16_free[s]] if self.b16_free[s] else []) + self.ffn_floor)
            self.dg.wait_at(sy, getattr(self, "n_aw_ready", 0))
            self.dct[wname].wait_at(sy, self.cast_done[wname])
            n = self.dt16[s].inc(sy.dma_start(
                out=self.hap(self.WB16 + s * 512, [(1, nrows)]),
                in_=self.wf_d[wname][rows0:rows0 + nrows, kc * 128:(kc + 1) * 128],
                transpose=True), 16)
            self.b16_ready[s] = n
        self.SY(adma)
        return s

    def band_up(self, s16, nrows):
        u = self.b32_next % 4
        self.b32_next += 1
        sv = self.sv

        def vup(v, u=u, s16=s16):
            self.dt16[s16].wait_at(v, self.b16_ready[s16])
            wait_toks(v, ([self.b32_free[u]] if self.b32_free[u] else []) + self.ffn_floor)
            i = v.tensor_copy(self.rap(self.WB32 + u * 512, [(1, nrows)]),
                              self.hap(self.WB16 + s16 * 512, [(1, nrows)]))
            self.b32_ready[u] = (sv, sv.inc(i))
        self.V(vup)
        return u

    def ffn_call(self, call):
        self.ffn_floor = (self.ffn_floor
                          + [(self.WR[b], self.WR[b].n) for b in range(8)]
                          + [(self.st, self.st.n)])
        for tb in range(self.NBLK):
            for tile in (2 * tb, 2 * tb + 1):
                self.lnorm(tile, lo16=True)
            self.ln_ready = self.sv.n
            self.ffn_floor = self.ffn_floor + self.wqk_free[0] + self.wqk_free[1]
            h_ready = self.stage(tb, "w1w2")
            g_ready = self.stage(tb, "h1", src_ready=h_ready)
            h3_ready = self.stage(tb, "h2", src_ready=g_ready)
            self.stage(tb, "w3", src_ready=h3_ready)
            if call == 0:
                self.qkv_kv(tb)
            else:
                self.store_out(tb)

    def stage(self, tb, name, src_ready=None):
        D = self.D
        sv, sa, st = self.sv, self.sa, self.st
        if name == "w1w2":
            KC, OC, OCG = self.NDC, self.NHC, 2
        elif name in ("h1", "h2"):
            KC, OC, OCG = self.NHC, self.NHC, 4
        else:
            KC, OC, OCG = self.NHC, self.NDC, 4

        def mv_hi(k):
            if name == "w1w2":
                return self.rap(self.LNHI + k * TB, [(1, TB)])
            base = self.HHI if name in ("h1", "w3") else self.GHI
            return self.rap(base + k * TB, [(1, TB)])

        def mv_lo(k):
            if name == "w1w2":
                return self.hap(self.LNLO16 + k * TB, [(1, TB)])
            base = self.HLO if name in ("h1", "w3") else self.GLO
            return self.hap(base + k * TB, [(1, TB)])

        wname0 = {"w1w2": "w1", "h1": "h1", "h2": "h2", "w3": "w3"}[name]
        out_ready = [0] * OC
        war_dg = 0
        if name == "w1w2":
            war_dg = max(self.ar1_free_dg, self.ar2_free_dg_kv)
        first_write = [True]

        def kloop(g0, banks, ocs):
            for k in range(KC):
                if name == "w1w2":
                    s1 = self.band_load("w1", g0 * 128, len(ocs) * 128, k)
                    s2 = self.band_load("w2", g0 * 128, len(ocs) * 128, k)
                    u1 = self.band_up(s1, len(ocs) * 128)
                    u2 = self.band_up(s2, len(ocs) * 128)
                    pairs = [(s1, u1), (s2, u2)]
                else:
                    s1 = self.band_load(wname0, g0 * 128, len(ocs) * 128, k)
                    u1 = self.band_up(s1, len(ocs) * 128)
                    pairs = [(s1, u1)]

                def tmm(t, k=k, pairs=pairs, banks=banks, ocs=ocs, KC=KC):
                    for (s16, u32) in pairs:
                        if self.b32_ready[u32]:
                            wait_toks(t, [self.b32_ready[u32]])
                        self.dt16[s16].wait_at(t, self.b16_ready[s16])
                    if src_ready is not None:
                        sv.wait_at(t, src_ready[k])
                    else:
                        sv.wait_at(t, self.ln_ready)
                    last = None
                    last_bk = None
                    for oi, oc in enumerate(ocs):
                        for wi, (s16, u32) in enumerate(pairs):
                            bk = banks[(oi * len(pairs) + wi) % len(banks)]
                            if k == 0:
                                self.RD[bk].wait(t)
                            po = _ap(self.ps[bk], 512, 0, [(1, TB)])
                            t.matmul(po, self.rap(self.WB32 + u32 * 512 + oi * 128,
                                                  [(1, 128)]),
                                     mv_hi(k), start=(k == 0), stop=False)
                            last = t.matmul(po, self.hap(self.WB16 + s16 * 512 + oi * 128,
                                                         [(1, 128)]),
                                            mv_lo(k), start=False, stop=(k == KC - 1))
                            last_bk = bk
                            if k == KC - 1:
                                self.WR[bk].inc(last)
                    if k == KC - 1:
                        tok = (self.WR[last_bk], self.WR[last_bk].n)
                    else:
                        tok = (st, st.inc(last))
                    for (s16, u32) in pairs:
                        self.b16_free[s16] = tok
                        self.b32_free[u32] = tok
                self.T(tmm)

        groups = []
        for g0 in range(0, OC, OCG):
            banks = [0, 1, 2, 3] if (g0 // OCG) % 2 == 0 else [4, 5, 6, 7]
            ocs = list(range(g0, min(OC, g0 + OCG)))
            groups.append((g0, banks, ocs))
        if name == "w3":
            for g in groups:
                kloop(*g)
                _, pbanks, pocs = g
                for oi, oc in enumerate(pocs):
                    wdg = war_dg if first_write[0] else 0
                    first_write[0] = False
                    out_ready[oc] = self.evac(tb, name, oc, oi, pbanks, wdg)
            return out_ready
        prev = None
        for g in groups:
            kloop(*g)
            if prev is not None:
                _, pbanks, pocs = prev
                for oi, oc in enumerate(pocs):
                    wdg = war_dg if first_write[0] else 0
                    first_write[0] = False
                    out_ready[oc] = self.evac(tb, name, oc, oi, pbanks, wdg)
            prev = g
        _, pbanks, pocs = prev
        for oi, oc in enumerate(pocs):
            wdg = war_dg if first_write[0] else 0
            first_write[0] = False
            out_ready[oc] = self.evac(tb, name, oc, oi, pbanks, wdg)
        return out_ready

    def evac(self, tb, name, oc, oi, banks, war_dg):
        sv, sa, dg = self.sv, self.sa, self.dg
        NHC = self.NHC
        if name == "w1w2":
            bka = banks[(oi * 2) % 4]
            bkb = banks[(oi * 2 + 1) % 4]

            po_ = oc % 2

            def aev(a, oc=oc, bka=bka, bkb=bkb, po_=po_):
                self.WR[bka].wait(a)
                sv.wait_at(a, self.evac_sv.get(("w", oc - 2), 0))
                i1 = a.activation(out=self.sA[:, po_ * TB:po_ * TB + TB],
                                  in_=_ap(self.ps[bka], 512, 0, [(1, TB)]),
                                  func=AF.Silu, scale=self.scl[:, oc:oc + 1])
                self.RD[bka].inc(i1)
            self.A(aev)

            def vev(v, oc=oc, bka=bka, bkb=bkb, po_=po_, war_dg=war_dg):
                self.RD[bka].wait(v)
                self.WR[bkb].wait(v)
                if war_dg:
                    dg.wait_at(v, war_dg)
                v.drain()
                i2 = v.tensor_scalar(self.sB[:, po_ * TB:po_ * TB + TB],
                                     _ap(self.ps[bkb], 512, 0, [(1, TB)]),
                                     self.scl[:, NHC + oc:NHC + oc + 1], None,
                                     op0=ALU.mult)
                self.RD[bkb].inc(i2)
                v.drain()
                v.tensor_tensor(self.scr[:, po_ * TB:po_ * TB + TB],
                                self.sA[:, po_ * TB:po_ * TB + TB],
                                self.sB[:, po_ * TB:po_ * TB + TB], op=ALU.mult)
                v.drain()
                hid_ = self.rap(self.HHI + oc * TB, [(1, TB)])
                v.tensor_copy(hid_, self.scr[:, po_ * TB:po_ * TB + TB])
                v.drain()
                v.tensor_tensor(self.sA[:, po_ * TB:po_ * TB + TB],
                                self.scr[:, po_ * TB:po_ * TB + TB], hid_.bitcast(F32),
                                op=ALU.subtract)
                v.drain()
                lod = self.hap(self.HLO + oc * TB, [(1, TB)])
                sv.inc(v.tensor_copy(lod, self.sA[:, po_ * TB:po_ * TB + TB]))
            self.V(vev)
            self.evac_sv[("w", oc)] = sv.n
            return sv.n
        if name in ("h1", "h2"):
            bk = banks[oi % 4]
            hi_base = self.GHI if name == "h1" else self.HHI
            lo_base = self.GLO if name == "h1" else self.HLO
            scol = (2 * NHC + oc) if name == "h1" else (3 * NHC + oc)

            po_ = oc % 2

            def aev(a, oc=oc, bk=bk, scol=scol, po_=po_):
                self.WR[bk].wait(a)
                sv.wait_at(a, self.evac_sv.get((name, oc - 2), 0))
                i = a.activation(out=self.scr[:, po_ * TB:po_ * TB + TB],
                                 in_=_ap(self.ps[bk], 512, 0, [(1, TB)]),
                                 func=AF.Gelu, scale=self.scl[:, scol:scol + 1])
                self.RD[bk].inc(i)
            self.A(aev)

            def vev(v, oc=oc, bk=bk, po_=po_, hi_base=hi_base, lo_base=lo_base,
                    war_dg=war_dg):
                self.RD[bk].wait(v)
                if war_dg:
                    dg.wait_at(v, war_dg)
                v.drain()
                hid_ = self.rap(hi_base + oc * TB, [(1, TB)])
                v.tensor_copy(hid_, self.scr[:, po_ * TB:po_ * TB + TB])
                v.drain()
                v.tensor_tensor(self.sA[:, po_ * TB:po_ * TB + TB],
                                self.scr[:, po_ * TB:po_ * TB + TB], hid_.bitcast(F32),
                                op=ALU.subtract)
                v.drain()
                lod = self.hap(lo_base + oc * TB, [(1, TB)])
                sv.inc(v.tensor_copy(lod, self.sA[:, po_ * TB:po_ * TB + TB]))
            self.V(vev)
            self.evac_sv[(name, oc)] = sv.n
            return sv.n
        bk = banks[oi % 4]
        s3o = 4 * NHC

        po_ = oc % 2

        def wev(v, oc=oc, bk=bk, po_=po_):
            self.WR[bk].wait(v)
            sv.wait_at(v, self.evac_sv.get(("w3", oc - 2), 0))
            self.WR[0].wait(v)
            self.WR[4].wait(v)
            v.drain()
            i = v.tensor_scalar(self.scr[:, po_ * TB:po_ * TB + TB],
                                _ap(self.ps[bk], 512, 0, [(1, TB)]),
                                self.scl[:, s3o + oc:s3o + oc + 1], None, op0=ALU.mult)
            self.RD[bk].inc(i)
        self.V(wev)
        tbk = 4 if bk < 4 else 0

        def ttr(t, bk=bk, tbk=tbk, po_=po_):
            self.RD[bk].wait(t)
            self.RD[tbk].wait(t)
            t.transpose(_ap(self.ps[tbk], 512, 0, [(1, 128)]),
                        self.scr[:, po_ * TB:po_ * TB + 128], self.idn[:, :])
            l2 = t.transpose(_ap(self.ps[tbk], 512, 128, [(1, 128)]),
                             self.scr[:, po_ * TB + 128:po_ * TB + 256], self.idn[:, :])
            self.WR[tbk].inc(l2)
        self.T(ttr)

        def vadd(v, oc=oc, tbk=tbk, tb=tb):
            self.WR[tbk].wait(v)
            v.tensor_tensor(_ap(self.xr, self.NT * self.D, 2 * tb * self.D + oc * 128,
                                [(1, 128)]),
                            _ap(self.xr, self.NT * self.D, 2 * tb * self.D + oc * 128,
                                [(1, 128)]),
                            _ap(self.ps[tbk], 512, 0, [(1, 128)]), op=ALU.add)
            i = v.tensor_tensor(_ap(self.xr, self.NT * self.D,
                                    (2 * tb + 1) * self.D + oc * 128, [(1, 128)]),
                                _ap(self.xr, self.NT * self.D,
                                    (2 * tb + 1) * self.D + oc * 128, [(1, 128)]),
                                _ap(self.ps[tbk], 512, 128, [(1, 128)]), op=ALU.add)
            self.RD[tbk].inc(i)
            v.drain()
            sv.inc(v.nop())
        self.V(vadd)
        self.evac_sv[("w3", oc)] = sv.n
        return sv.n

    def qkv_kv(self, tb):
        D, D3, NDC = self.D, self.D3, self.NDC
        sv, sa, st, dg = self.sv, self.sa, self.st, self.dg
        for tile in (2 * tb, 2 * tb + 1):
            self.lnorm(tile, lo16=False)
        n_ln2 = sv.n
        self.q_floor = (self.q_floor + [t for t in self.b16_free if t]
                        + [t for t in self.b32_free if t])
        prev_kv_dg = self.ar2_free_dg_kv

        for dc in range(NDC):
            p = dc % 2
            nld = {}

            def syb(sync, dc=dc, p=p, nld=nld):
                wait_toks(sync, self.wqk_free[p] + self.q_floor)
                dg.wait_at(sync, self.n_aw_ready)
                self.dqw[p].inc(sync.dma_start(
                    out=bass.AP(self.rp, self.QKB + p * self.PS,
                                [[self.RPL, 128], [128, NDC], [1, 128]]),
                    in_=bass.AP(self.awhi_d.tensor, D + dc * 128,
                                [[D3, 128], [128 * D3, NDC], [1, 128]])), 16)
                nld["n"] = self.dqw[p].inc(sync.dma_start(
                    out=bass.AP(self.rp, self.QKB + p * self.PS + self.D,
                                [[self.RPL, 128], [128, NDC], [1, 128]]),
                    in_=bass.AP(self.awlo_d.tensor, D + dc * 128,
                                [[D3, 128], [128 * D3, NDC], [1, 128]])), 16)
            self.G(syb)
            bk = dc % 4

            def tkm(t, dc=dc, p=p, bk=bk, nld=nld):
                self.dqw[p].wait_at(t, nld["n"])
                sv.wait_at(t, n_ln2)
                self.RD[bk].wait(t)
                po = _ap(self.ps[bk], 512, 0, [(1, TB)])
                last = None
                for k in range(NDC):
                    whi = self.rap(self.QKB + p * self.PS + k * 128, [(1, 128)])
                    wlo = self.rap(self.QKB + p * self.PS + self.D + k * 128, [(1, 128)])
                    mhi = self.rap(self.LNHI + k * TB, [(1, TB)])
                    mlo = self.rap(self.LNLO32 + k * TB, [(1, TB)])
                    t.matmul(po, whi, mhi, start=(k == 0), stop=False)
                    t.matmul(po, whi, mlo, start=False, stop=False)
                    last = t.matmul(po, wlo, mhi, start=False, stop=(k == NDC - 1))
                self.WR[bk].inc(last)
                self.wqk_free[p] = [(self.WR[bk], self.WR[bk].n)]
            self.T(tkm)

            def vkv(v, dc=dc, bk=bk, prev_kv_dg=prev_kv_dg):
                self.WR[bk].wait(v)
                dg.wait_at(v, prev_kv_dg)
                hi = self.rap(self.KVSTG + dc * TB, [(1, TB)])
                i0 = v.tensor_copy(hi, _ap(self.ps[bk], 512, 0, [(1, TB)]))
                self.RD[bk].inc(i0)
                v.drain()
                i = v.tensor_tensor(self.sA[:, 0:TB], _ap(self.ps[bk], 512, 0, [(1, TB)]),
                                    hi.bitcast(F32), op=ALU.subtract)
                self.RD[bk].inc(i)
                v.drain()
                lo = self.rap(self.KVSTG + 2 * D + dc * TB, [(1, TB)])
                v.tensor_copy(lo, self.sA[:, 0:TB])
                v.drain()
                sv.inc(v.nop())
            self.V(vkv)

        DH, NDH = self.DH, self.NDH
        for dh in range(NDH):
            kn = {}
            for k in range(NDC):
                pk = k % 2

                def syv(sync, dh=dh, k=k, pk=pk, kn=kn):
                    wait_toks(sync, self.wqk_free[pk] + self.q_floor)
                    dg.wait_at(sync, self.n_aw_ready)
                    self.dqw[pk].inc(sync.dma_start(
                        out=self.rap(self.QKB + pk * self.PS, [(1, DH)]),
                        in_=self.awhi_d[k * 128:(k + 1) * 128,
                                        2 * D + dh * DH:2 * D + (dh + 1) * DH]), 16)
                    kn[k] = self.dqw[pk].inc(sync.dma_start(
                        out=self.rap(self.QKB + pk * self.PS + self.D, [(1, DH)]),
                        in_=self.awlo_d[k * 128:(k + 1) * 128,
                                        2 * D + dh * DH:2 * D + (dh + 1) * DH]), 16)
                self.G(syv)

                def tvm(t, dh=dh, k=k, pk=pk, kn=kn):
                    self.dqw[pk].wait_at(t, kn[k])
                    sv.wait_at(t, n_ln2)
                    last = None
                    for tt in range(2):
                        bk = 4 + tt * NDH + dh
                        if k == 0:
                            self.RD[bk].wait(t)
                        po = _ap(self.ps[bk], 512, 0, [(1, DH)])
                        shi = self.rap(self.LNHI + k * TB + tt * 128, [(1, 128)])
                        slo = self.rap(self.LNLO32 + k * TB + tt * 128, [(1, 128)])
                        whi = self.rap(self.QKB + pk * self.PS, [(1, DH)])
                        wlo = self.rap(self.QKB + pk * self.PS + self.D, [(1, DH)])
                        t.matmul(po, shi, whi, start=(k == 0), stop=False)
                        t.matmul(po, slo, whi, start=False, stop=False)
                        last = t.matmul(po, shi, wlo, start=False, stop=(k == NDC - 1))
                        if k == NDC - 1:
                            self.WR[bk].inc(last)
                    if k == NDC - 1:
                        bk_last = 4 + NDH + dh
                        self.wqk_free[pk] = [(self.WR[bk_last], self.WR[bk_last].n)]
                    else:
                        self.wqk_free[pk] = [(st, st.inc(last))]
                self.T(tvm)

            for tt in range(2):
                bk = 4 + tt * NDH + dh

                def vvv(v, tt=tt, dh=dh, bk=bk, prev_kv_dg=prev_kv_dg):
                    self.WR[bk].wait(v)
                    dg.wait_at(v, prev_kv_dg)
                    hi = self.rap(self.KVSTG + 4 * D + tt * D + dh * DH, [(1, DH)])
                    i0 = v.tensor_copy(hi, _ap(self.ps[bk], 512, 0, [(1, DH)]))
                    self.RD[bk].inc(i0)
                    v.drain()
                    i = v.tensor_tensor(self.sA[:, 0:DH], _ap(self.ps[bk], 512, 0, [(1, DH)]),
                                        hi.bitcast(F32), op=ALU.subtract)
                    self.RD[bk].inc(i)
                    v.drain()
                    lo = self.rap(self.KVSTG + 6 * D + tt * D + dh * DH, [(1, DH)])
                    v.tensor_copy(lo, self.sA[:, 0:DH])
                    v.drain()
                    sv.inc(v.nop())
                self.V(vvv)
        n_v_done = sv.n

        def gkv(g, tb=tb, n_v_done=n_v_done):
            sv.wait_at(g, n_v_done)
            g.wait_ge(dg.h, dg.n)
            ksrc = lambda off: bass.AP(self.rp, off, [[self.RPL, 128], [TB, NDC], [1, TB]])
            kdst = lambda d: bass.AP(d.tensor, 0, [[TB, 128], [128 * TB, NDC], [1, TB]])
            dg.inc(g.dma_start(out=kdst(self.km["kh"][tb]), in_=ksrc(self.KVSTG)), 16)
            dg.inc(g.dma_start(out=kdst(self.km["kl"][tb]), in_=ksrc(self.KVSTG + 2 * D)), 16)
            vsrc = lambda off: bass.AP(self.rp, off, [[self.RPL, 128], [D, 2], [1, D]])
            vdst = lambda d: bass.AP(d.tensor, 0, [[D, 128], [128 * D, 2], [1, D]])
            dg.inc(g.dma_start(out=vdst(self.km["vh"][tb]), in_=vsrc(self.KVSTG + 4 * D)), 16)
            n4 = dg.inc(g.dma_start(out=vdst(self.km["vl"][tb]),
                                    in_=vsrc(self.KVSTG + 6 * D)), 16)
            g.wait_ge(dg.h, n4)
            for kind in ("kh", "kl", "vh", "vl"):
                self.cc.inc(g.collective_compute(
                    "AllGather", ALU.bypass, replica_groups=RG,
                    ins=[self.km[kind][tb][:, :]], outs=[self.ka[kind][tb][:, :]]))
        self.G(gkv)
        self.ar2_free_dg_kv = dg.n

    def attention(self):
        D, NDC, NKC, SEQ = self.D, self.NDC, self.NKC, self.SEQ
        DH, NDH = self.DH, self.NDH
        sv, sa, st, dg, cc = self.sv, self.sa, self.st, self.dg, self.cc
        n_ag_all = cc.n
        n_kv_stores = dg.n
        self.q_floor = (self.q_floor + [t for t in self.b16_free if t]
                        + [t for t in self.b32_free if t])
        init_kv = ([(st, st.n), (sv, sv.n), (sa, sa.n)]
                   + [(self.WR[b], self.WR[b].n) for b in range(8)])
        self.kv_free = [list(init_kv), list(init_kv)]
        inv_sqrt_d = float(1.0 / np.sqrt(float(D)))
        for qb in range(self.NQB):
            t0 = 2 * qb
            for tile in (t0, t0 + 1):
                self.lnorm(tile, lo16=False)
            n_ln2 = sv.n

            for dc in range(NDC):
                p = dc % 2
                nld = {}

                def syq(sync, dc=dc, p=p, nld=nld):
                    wait_toks(sync, self.wqk_free[p] + self.q_floor
                              + self.kv_free[0] + self.kv_free[1])
                    dg.wait_at(sync, self.n_aw_ready)
                    self.dqw[p].inc(sync.dma_start(
                        out=bass.AP(self.rp, self.QB2 + p * self.PS,
                                    [[self.RPL, 128], [128, NDC], [1, 128]]),
                        in_=bass.AP(self.awhi_d.tensor, dc * 128,
                                    [[self.D3, 128], [128 * self.D3, NDC], [1, 128]])), 16)
                    nld["n"] = self.dqw[p].inc(sync.dma_start(
                        out=bass.AP(self.rp, self.QB2 + p * self.PS + self.D,
                                    [[self.RPL, 128], [128, NDC], [1, 128]]),
                        in_=bass.AP(self.awlo_d.tensor, dc * 128,
                                    [[self.D3, 128], [128 * self.D3, NDC], [1, 128]])), 16)
                self.G(syq)
                bk = dc % 4

                def tqm(t, p=p, bk=bk, nld=nld):
                    self.dqw[p].wait_at(t, nld["n"])
                    sv.wait_at(t, n_ln2)
                    self.RD[bk].wait(t)
                    po = _ap(self.ps[bk], 512, 0, [(1, TB)])
                    last = None
                    for k in range(NDC):
                        whi = self.rap(self.QB2 + p * self.PS + k * 128, [(1, 128)])
                        wlo = self.rap(self.QB2 + p * self.PS + self.D + k * 128, [(1, 128)])
                        mhi = self.rap(self.LNHI + k * TB, [(1, TB)])
                        mlo = self.rap(self.LNLO32 + k * TB, [(1, TB)])
                        t.matmul(po, whi, mhi, start=(k == 0), stop=False)
                        t.matmul(po, whi, mlo, start=False, stop=False)
                        last = t.matmul(po, wlo, mhi, start=False, stop=(k == NDC - 1))
                    self.WR[bk].inc(last)
                    self.wqk_free[p] = [(self.WR[bk], self.WR[bk].n)]
                self.T(tqm)

                def vqe(v, dc=dc, bk=bk):
                    self.WR[bk].wait(v)
                    hi = self.rap(self.QH + dc * TB, [(1, TB)])
                    i0 = v.tensor_copy(hi, _ap(self.ps[bk], 512, 0, [(1, TB)]))
                    self.RD[bk].inc(i0)
                    v.drain()
                    i = v.tensor_tensor(self.sA[:, 0:TB], _ap(self.ps[bk], 512, 0, [(1, TB)]),
                                        hi.bitcast(F32), op=ALU.subtract)
                    self.RD[bk].inc(i)
                    v.drain()
                    lo = self.rap(self.QH + 2 * D + dc * TB, [(1, TB)])
                    v.tensor_copy(lo, self.sA[:, 0:TB])
                    v.drain()
                    sv.inc(v.nop())
                self.V(vqe)
            n_q_done = sv.n

            def vqs(v, t0=t0):
                for ti in range(2):
                    v.tensor_scalar(self.red[:, 32 + ti * 16:32 + ti * 16 + NKC],
                                    self.iot[:, 0:NKC], -float(TB),
                                    self.qps[:, t0 + ti:t0 + ti + 1],
                                    op0=ALU.mult, op1=ALU.add)
                v.drain()
                sv.inc(v.nop())
            self.V(vqs)
            n_qs = sv.n

            for kc in range(NKC):
                p = kc % 2
                r, sb_ = kc // self.NBLK, kc % self.NBLK
                nld = {}

                def syk(sync, p=p, r=r, sb_=sb_, nld=nld):
                    cc.wait_at(sync, n_ag_all)
                    dg.wait_at(sync, n_kv_stores)
                    wait_toks(sync, self.kv_free[p])
                    self.dkv[p].inc(sync.dma_start(
                        out=bass.AP(self.rp, self.KVS + p * self.PS,
                                    [[self.RPL, 128], [TB, NDC], [1, TB]]),
                        in_=bass.AP(self.ka["kh"][sb_].tensor, r * D * TB,
                                    [[TB, 128], [128 * TB, NDC], [1, TB]])), 16)
                    nld["n"] = self.dkv[p].inc(sync.dma_start(
                        out=bass.AP(self.rp, self.KVSL + p * self.PS,
                                    [[self.RPL, 128], [TB, NDC], [1, TB]]),
                        in_=bass.AP(self.ka["kl"][sb_].tensor, r * D * TB,
                                    [[TB, 128], [128 * TB, NDC], [1, TB]])), 16)
                self.G(syk)

                for ti in range(2):
                    bk = ti

                    def tsc(t, kc=kc, p=p, ti=ti, bk=bk, nld=nld):
                        self.dkv[p].wait_at(t, nld["n"])
                        sv.wait_at(t, n_q_done)
                        self.RD[bk].wait(t)
                        po = _ap(self.ps[bk], 512, 0, [(1, TB)])
                        last = None
                        for k in range(NDC):
                            qhi = self.rap(self.QH + k * TB + ti * 128, [(1, 128)])
                            qlo = self.rap(self.QH + 2 * D + k * TB + ti * 128, [(1, 128)])
                            khi = self.rap(self.KVS + p * self.PS + k * TB, [(1, TB)])
                            klo = self.rap(self.KVSL + p * self.PS + k * TB, [(1, TB)])
                            t.matmul(po, qhi, khi, start=(k == 0), stop=False)
                            t.matmul(po, qhi, klo, start=False, stop=False)
                            last = t.matmul(po, qlo, khi, start=False, stop=(k == NDC - 1))
                        self.WR[bk].inc(last)
                        if ti == 1:
                            self.kv_free[p] = [(self.WR[0], self.WR[0].n),
                                               (self.WR[1], self.WR[1].n)]
                    self.T(tsc)

                    def vsc(v, kc=kc, ti=ti, bk=bk):
                        sv.wait_at(v, n_qs)
                        self.WR[bk].wait(v)
                        dg.wait_at(v, self.n_aw_ready)
                        v.tensor_scalar(self.sA[:, 0:TB], self.iot[:, 0:TB],
                                        self.red[:, 32 + ti * 16 + kc:33 + ti * 16 + kc],
                                        NEG, op0=ALU.is_lt, op1=ALU.mult)
                        v.drain()
                        i = v.tensor_scalar(self.sB[:, 0:TB],
                                            _ap(self.ps[bk], 512, 0, [(1, TB)]),
                                            inv_sqrt_d, None, op0=ALU.mult)
                        self.RD[bk].inc(i)
                        v.drain()
                        v.tensor_tensor(
                            self.fap(self.SCO + ti * SEQ + kc * TB, [(1, TB)]),
                            self.sB[:, 0:TB], self.sA[:, 0:TB], op=ALU.add)
                        v.drain()
                        sv.inc(v.nop())
                    self.V(vsc)
            n_scores = sv.n

            for ti in range(2):
                base = self.SCO + ti * SEQ
                mx, ng, sm, rc = 17 + 3 * ti, 18 + 3 * ti, 19 + 3 * ti, 23 + ti

                def vsm(v, base=base, mx=mx, ng=ng):
                    sv.wait_at(v, n_scores)
                    v.reduce_max(self.red[:, mx:mx + 1], self.fap(base, [(1, SEQ)]),
                                 axis=AXX)
                    v.drain()
                    v.tensor_scalar(self.red[:, ng:ng + 1], self.red[:, mx:mx + 1], -1.0,
                                    None, op0=ALU.mult)
                    v.drain()
                    sv.inc(v.nop())
                self.V(vsm)

                def asm(a, base=base, ng=ng, sm=sm):
                    sv.wait(a)
                    a.activation(out=self.fap(base, [(1, SEQ)]),
                                 in_=self.fap(base, [(1, SEQ)]),
                                 func=AF.Exp, bias=self.red[:, ng:ng + 1], scale=1.0,
                                 accum_out=self.red[:, sm:sm + 1])
                    a.drain()
                    sa.inc(a.nop())
                self.A(asm)

                def vrc(v, sm=sm, rc=rc):
                    sa.wait(v)
                    v.reciprocal(self.red[:, rc:rc + 1], self.red[:, sm:sm + 1])
                    v.drain()
                    sv.inc(v.nop())
                self.V(vrc)

                def vpb(v, base=base, rc=rc):
                    v.tensor_scalar(self.fap(base, [(1, SEQ)]),
                                    self.fap(base, [(1, SEQ)]),
                                    self.red[:, rc:rc + 1], None, op0=ALU.mult)
                    v.drain()
                    sv.inc(v.nop())
                self.V(vpb)
            n_probs_sv = sv.n

            for kc in range(NKC):
                p = kc % 2
                r, sb_ = kc // self.NBLK, kc % self.NBLK
                nld = {}

                def syv(sync, p=p, r=r, sb_=sb_, nld=nld):
                    dg.wait_at(sync, n_kv_stores)
                    wait_toks(sync, self.kv_free[p])
                    self.dkv[p].inc(sync.dma_start(
                        out=bass.AP(self.rp, self.KVS + p * self.PS,
                                    [[self.RPL, 128], [D, 2], [1, D]]),
                        in_=bass.AP(self.ka["vh"][sb_].tensor, r * TB * D,
                                    [[D, 128], [128 * D, 2], [1, D]])), 16)
                    nld["n"] = self.dkv[p].inc(sync.dma_start(
                        out=bass.AP(self.rp, self.KVSL + p * self.PS,
                                    [[self.RPL, 128], [D, 2], [1, D]]),
                        in_=bass.AP(self.ka["vl"][sb_].tensor, r * TB * D,
                                    [[D, 128], [128 * D, 2], [1, D]])), 16)
                self.G(syv)

                for ti in range(2):
                    base = self.SCO + ti * SEQ
                    tbk = 4 + (kc % 2) * 2 + ti
                    slot = (kc % 2) * 2 + ti
                    aT_hi = self.ATT + slot * 256
                    aT_lo = self.ATT + 1024 + slot * 256

                    def ttr(t, kc=kc, base=base, tbk=tbk):
                        sv.wait_at(t, n_probs_sv)
                        self.RD[tbk].wait(t)
                        t.transpose(_ap(self.ps[tbk], 512, 0, [(1, 128)]),
                                    self.fap(base + kc * TB, [(1, 128)]),
                                    self.idn[:, :])
                        l2 = t.transpose(_ap(self.ps[tbk], 512, 128, [(1, 128)]),
                                         self.fap(base + kc * TB + 128, [(1, 128)]),
                                         self.idn[:, :])
                        self.WR[tbk].inc(l2)
                    self.T(ttr)

                    def vtv(v, tbk=tbk, aT_hi=aT_hi, aT_lo=aT_lo):
                        self.WR[tbk].wait(v)
                        hi = self.rap(aT_hi, [(1, 256)])
                        i0 = v.tensor_copy(hi, _ap(self.ps[tbk], 512, 0, [(1, 256)]))
                        self.RD[tbk].inc(i0)
                        v.drain()
                        i = v.tensor_tensor(self.sA[:, 0:256],
                                            _ap(self.ps[tbk], 512, 0, [(1, 256)]),
                                            hi.bitcast(F32), op=ALU.subtract)
                        self.RD[tbk].inc(i)
                        v.drain()
                        v.tensor_copy(self.rap(aT_lo, [(1, 256)]), self.sA[:, 0:256])
                        v.drain()
                        sv.inc(v.nop())
                    self.V(vtv)
                    n_at = sv.n

                    for dh in range(NDH):
                        avbk = ti * NDH + dh

                        def tav(t, kc=kc, ti=ti, dh=dh, p=p, avbk=avbk, n_at=n_at,
                                aT_hi=aT_hi, aT_lo=aT_lo, nld=nld):
                            sv.wait_at(t, n_at)
                            self.dkv[p].wait_at(t, nld["n"])
                            if kc == 0:
                                self.RD[avbk].wait(t)
                            po = _ap(self.ps[avbk], 512, 0, [(1, DH)])
                            last = None
                            for kk in range(2):
                                ahi = self.rap(aT_hi + kk * 128, [(1, 128)])
                                alo = self.rap(aT_lo + kk * 128, [(1, 128)])
                                vhi = self.rap(self.KVS + p * self.PS + kk * D + dh * DH,
                                               [(1, DH)])
                                vlo = self.rap(self.KVSL + p * self.PS + kk * D + dh * DH,
                                               [(1, DH)])
                                first = (kc == 0 and kk == 0)
                                lastf = (kc == NKC - 1 and kk == 1)
                                t.matmul(po, ahi, vhi, start=first, stop=False)
                                t.matmul(po, ahi, vlo, start=False, stop=False)
                                last = t.matmul(po, alo, vhi, start=False, stop=lastf)
                            if kc == NKC - 1:
                                self.WR[avbk].inc(last)
                            elif dh == NDH - 1:
                                n = st.inc(last)
                                if ti == 1:
                                    self.kv_free[p] = [(st, n)]
                        self.T(tav)

            self.kv_free[(NKC - 1) % 2] = [
                (self.WR[b], self.WR[b].n) for b in range(2 * NDH)]

            for ti in range(2):
                for dh in range(NDH):
                    avbk = ti * NDH + dh

                    def vav(v, ti=ti, dh=dh, avbk=avbk, t0=t0):
                        self.WR[avbk].wait(v)
                        tile = t0 + ti
                        dst = _ap(self.xr, self.NT * D, tile * D + dh * DH, [(1, DH)])
                        last = v.tensor_tensor(dst, dst,
                                               _ap(self.ps[avbk], 512, 0, [(1, DH)]),
                                               op=ALU.add)
                        self.RD[avbk].inc(last)
                        v.drain()
                        sv.inc(v.nop())
                    self.V(vav)

    def store_out(self, tb):
        sv, dg = self.sv, self.dg
        n_ready = sv.n

        def gst(g, tb=tb, n_ready=n_ready):
            sv.wait_at(g, n_ready)
            g.wait_ge(dg.h, dg.n)
            for h2_ in range(2):
                tile = 2 * tb + h2_
                dg.inc(g.dma_start(
                    out=self.out_d[tile * 128:(tile + 1) * 128, :],
                    in_=_ap(self.xr, self.NT * self.D, tile * self.D, [(1, self.D)])), 16)
        self.G(gst)

    def finish(self):
        dg = self.dg

        def gf(g):
            g.wait_ge(dg.h, dg.n)
        self.G(gf)


def build_nc():
    return Build().build()


def kernel(L0_attn_x, L0_attn_W, L0_ffn_w1_w, L0_ffn_w1_s, L0_ffn_w2_w,
           L0_ffn_w2_s, L0_ffn_h1_w, L0_ffn_h1_s, L0_ffn_h2_w, L0_ffn_h2_s,
           L0_ffn_w3_w, L0_ffn_w3_s):
    D, SEQ, TOK = CFG["D"], CFG["SEQ"], CFG["TOK"]
    NT = TOK // 128
    x = np.ascontiguousarray(np.asarray(L0_attn_x, dtype=np.float32))
    aw = np.ascontiguousarray(np.asarray(L0_attn_W, dtype=np.float32))
    w = {n: np.ascontiguousarray(np.asarray(a, dtype=np.float32)) for n, a in
         (("w1", L0_ffn_w1_w), ("w2", L0_ffn_w2_w), ("h1", L0_ffn_h1_w),
          ("h2", L0_ffn_h2_w), ("w3", L0_ffn_w3_w))}
    s = {n: np.ascontiguousarray(np.asarray(a, dtype=np.float32).reshape(-1, 128).T)
         for n, a in (("s1", L0_ffn_w1_s), ("s2", L0_ffn_w2_s), ("sh1", L0_ffn_h1_s),
                      ("sh2", L0_ffn_h2_s), ("s3", L0_ffn_w3_s))}
    ident = np.eye(128, dtype=np.float32)
    nc = build_nc()
    in_maps = []
    for c in range(CFG["NCORE"]):
        b, h = c % 4, c // 4
        qpos = (h * TOK + np.arange(NT)[None, :] * 128
                + np.arange(128)[:, None]).astype(np.float32)
        m = {"x": np.ascontiguousarray(x[b, h * TOK:(h + 1) * TOK, :]), "aw": aw,
             "ident": ident, "qpos": np.ascontiguousarray(qpos)}
        for n in w:
            m[n] = w[n]
        for n in s:
            m[n + "c"] = s[n]
        in_maps.append(m)
    res = run_bass_kernel_spmd(nc, in_maps, core_ids=list(range(CFG["NCORE"])))
    out = np.empty((4, SEQ, D), np.float32)
    for c, r in enumerate(res.results):
        b, h = c % 4, c // 4
        out[b, h * TOK:(h + 1) * TOK, :] = r["out"]
    return out


# version 10
# speedup vs baseline: 3.5055x; 1.1262x over previous
import contextlib
import os

os.environ.setdefault("GAUGE_NTFF_IGNORE_DMA", "1")

import numpy as np

import concourse.bass as bass
import concourse.mybir as mybir
from concourse.bass_utils import run_bass_kernel_spmd

F32 = mybir.dt.float32
F32R = mybir.dt.float32r
F16 = mybir.dt.float16
AF = mybir.ActivationFunctionType
ALU = mybir.AluOpType
AXX = mybir.AxisListType.X

CFG = dict(D=1024, HID=4096, SEQ=4096, TOK=2048, NCORE=8)
TB = 256
NEG = -9.0e15
EPS = 1e-5
RG = [[0, 4], [1, 5], [2, 6], [3, 7]]
PHASES = ("pre", "attn", "post")


class Sem:
    __slots__ = ("h", "n")

    def __init__(self, h):
        self.h = h
        self.n = 0

    def inc(self, instr, k=1):
        instr.then_inc(self.h, k)
        self.n += k
        return self.n

    def wait_at(self, eng, n):
        assert 0 <= n <= self.n, (n, self.n)
        if n <= 0:
            return
        key = (type(eng).__name__, id(self))
        if WAITCACHE.get(key, 0) >= n:
            return
        WAITCACHE[key] = n
        eng.wait_ge(self.h, n)

    def wait(self, eng):
        self.wait_at(eng, self.n)


WAITCACHE = {}


def wait_toks(eng, toks):
    if not toks:
        return
    best = {}
    for sem, val in toks:
        if val > best.get(id(sem), (None, -1))[1]:
            best[id(sem)] = (sem, val)
    for sem, val in best.values():
        sem.wait_at(eng, val)


def _ap(t, rowlen, off, dims):
    return bass.AP(t, off, [[rowlen, 128]] + [[s, c] for s, c in dims])


class Build:
    def __init__(self):
        self.D = CFG["D"]
        self.HID = CFG["HID"]
        self.SEQ = CFG["SEQ"]
        self.TOK = CFG["TOK"]
        self.D3 = 3 * self.D
        self.NT = self.TOK // 128
        self.NBLK = self.TOK // TB
        self.NDC = self.D // 128
        self.NHC = self.HID // 128
        self.NKC = self.SEQ // TB
        self.NQB = self.TOK // TB
        self.DH = min(512, self.D)
        self.NDH = self.D // self.DH
        WAITCACHE.clear()
        D, SEQ, D3 = self.D, self.SEQ, self.D3
        self.LNHI = 0
        self.LNLO32 = 2 * D
        self.WB32 = 4 * D
        self.PS = 2 * D
        self.HHI = 4 * D + 2048
        self.GHI = self.HHI + max(self.NHC * TB, 4 * self.PS)
        self.KVSTG = self.HHI
        self.QKB = self.GHI
        self.KVS = self.HHI
        self.KVSL = self.HHI + 2 * self.PS
        self.QH = self.GHI
        self.QB2 = self.GHI + 4 * D
        self.ATT = self.QB2
        self.RPL = max(self.GHI + self.NHC * TB, self.QB2 + 2 * self.PS,
                       self.ATT + 2048)
        self.SCO = 0
        self.LNLO16 = 0
        self.WB16 = 2 * D
        self.HLO = 2 * D + 3584
        self.GLO = self.HLO + self.NHC * TB
        f16_need = self.GLO + self.NHC * TB
        self.FAL = max(2 * SEQ, 3 * D3, (f16_need + 1) // 2)
        self.nc = bass.Bass("TRN2", target_bir_lowering=False)

    def T(self, f):
        self.blk.tensor(f)

    def V(self, f):
        self.blk.vector(f)

    def A(self, f):
        self.blk.scalar(f)

    def G(self, f):
        self.blk.gpsimd(f)

    def SY(self, f):
        self.blk.sync(f)

    def rap(self, off, dims):
        return _ap(self.rp, self.RPL, off, dims)

    def fap(self, off, dims):
        return _ap(self.fa, self.FAL, off, dims)

    def hap(self, off, dims):
        return _ap(self.fa16, 2 * self.FAL, off, dims)

    def build(self):
        nc, D, HID, D3 = self.nc, self.D, self.HID, self.D3
        TOK, NT, NBLK = self.TOK, self.NT, self.NBLK
        self.x_d = nc.dram_tensor("x", [TOK, D], F32, kind="ExternalInput").ap()
        self.aw_d = nc.dram_tensor("aw", [D, D3], F32, kind="ExternalInput").ap()
        self.w_d = {n: nc.dram_tensor(n, list(s), F32, kind="ExternalInput").ap()
                    for n, s in (("w1", (HID, D)), ("w2", (HID, D)),
                                 ("h1", (HID, HID)), ("h2", (HID, HID)),
                                 ("w3", (D, HID)))}
        self.s_d = {n: nc.dram_tensor(n + "c", [128, w], F32, kind="ExternalInput").ap()
                    for n, w in (("s1", self.NHC), ("s2", self.NHC), ("sh1", self.NHC),
                                 ("sh2", self.NHC), ("s3", self.NDC))}
        self.ident_d = nc.dram_tensor("ident", [128, 128], F32, kind="ExternalInput").ap()
        self.qpos_d = nc.dram_tensor("qpos", [128, NT], F32, kind="ExternalInput").ap()
        self.out_d = nc.dram_tensor("out", [TOK, D], F32, kind="ExternalOutput").ap()
        self.wf_d = {n: nc.dram_tensor(n + "f", list(self.w_d[n].shape), F16,
                                       kind="Internal").ap() for n in self.w_d}
        self.wT_d = {n: nc.dram_tensor(n + "T", [self.w_d[n].shape[1],
                                                 self.w_d[n].shape[0]], F16,
                                       kind="Internal").ap() for n in self.w_d}
        self.idn16_d = nc.dram_tensor("idn16", [128, 128], F16,
                                      kind="ExternalInput").ap()
        self.awhi_d = nc.dram_tensor("awhi", [D, D3], F32R, kind="Internal").ap()
        self.awlo_d = nc.dram_tensor("awlo", [D, D3], F32R, kind="Internal").ap()
        self.km, self.ka = {}, {}
        for kind in ("kh", "kl"):
            self.km[kind] = [nc.dram_tensor(f"{kind}m{b}", [D, TB], F32R,
                                            kind="Internal").ap() for b in range(NBLK)]
            self.ka[kind] = [nc.dram_tensor(f"{kind}a{b}", [2 * D, TB], F32R,
                                            kind="Internal").ap() for b in range(NBLK)]
        for kind in ("vh", "vl"):
            self.km[kind] = [nc.dram_tensor(f"{kind}m{b}", [TB, D], F32R,
                                            kind="Internal").ap() for b in range(NBLK)]
            self.ka[kind] = [nc.dram_tensor(f"{kind}a{b}", [2 * TB, D], F32R,
                                            kind="Internal").ap() for b in range(NBLK)]

        es = contextlib.ExitStack()
        with es:
            sb = lambda name, cols, dt: es.enter_context(nc.sbuf_tensor(name, [128, cols], dt))
            self.xr = sb("xr", NT * D, F32)
            self.rp = sb("rp", self.RPL, F32R)
            self.fa = sb("fa", self.FAL, F32)
            self.scr = sb("scr", max(D, 2 * TB), F32)
            self.sA = sb("sA", 512, F32)
            self.sB = sb("sB", 512, F32)
            self.red = sb("red", 64, F32)
            nsc = 4 * self.NHC + self.NDC
            self.scl = sb("scl", nsc, F32)
            self.idn = sb("idn", 128, F32)
            self.idn16 = sb("idn16s", 128, F16)
            self.pst = sb("pst", 1024, F16)
            self.qps = sb("qps", NT, F32)
            self.iot = sb("iot", 256, F32)
            self.ps = [es.enter_context(nc.psum_tensor(f"ps{i}", [128, 512], F32))
                       for i in range(8)]
            mk = lambda n: Sem(es.enter_context(nc.semaphore(n)))
            self.dx, self.dxa, self.dg = mk("dx"), mk("dxa"), mk("dg")
            self.dct = {n: mk("dct_" + n) for n in ("w1", "w2", "h1", "h2", "w3")}
            self.dt16 = [mk(f"dt16_{i}") for i in range(8)]
            self.dqw = [mk("dqw0"), mk("dqw1")]
            self.dps = [mk("dps0"), mk("dps1")]
            self.dkv = [mk("dkv0"), mk("dkv1")]
            self.sv, self.sa, self.st, self.cc = mk("sv"), mk("sa"), mk("st"), mk("cc")
            self.WR = [mk(f"wr{i}") for i in range(8)]
            self.RD = [mk(f"rd{i}") for i in range(8)]
            self.blk = es.enter_context(nc.Block())

            self.fa16 = self.fa.bitcast(F16)
            self.fa32r = self.fa.bitcast(F32R)

            self.b16_ready = [0] * 8
            self.b32_ready = [None] * 4
            self.b16_free = [None] * 8
            self.b32_free = [None] * 4
            self.b16_next = 0
            self.b32_next = 0
            self.wqk_free = [[], []]
            self.kv_free = [[], []]
            self.ffn_floor = []
            self.q_floor = []
            self.ar1_free_dg = 0
            self.ar2_free_dg_kv = 0
            self.evac_sv = {}
            self.evac_st = {}
            self.pst_free = [None, None]
            self.wT_tok = {}

            self.phase0()
            self.split_aw()
            self.wt_prepass()
            if "pre" in PHASES:
                self.ffn_call(0)
            if "attn" in PHASES:
                self.attention()
            if "post" in PHASES:
                self.ffn_call(1)
            else:
                for tb in range(NBLK):
                    self.store_out(tb)
            self.finish()
        return nc

    def phase0(self):
        D, NT = self.D, self.NT
        NHC, NDC = self.NHC, self.NDC
        dx, sv = self.dx, self.sv
        xr, scl, idn, qps, iot = self.xr, self.scl, self.idn, self.qps, self.iot
        x_d, s_d = self.x_d, self.s_d

        def sy0(sync):
            xin = bass.AP(x_d.tensor, 0, [[D, 128], [128 * D, NT], [1, D]])
            xout = bass.AP(xr, 0, [[NT * D, 128], [D, NT], [1, D]])
            dx.inc(sync.dma_start(out=xout, in_=xin), 16)
            off = 0
            for n in ("s1", "s2", "sh1", "sh2"):
                dx.inc(sync.dma_start(out=scl[:, off:off + NHC], in_=s_d[n][:, :]), 16)
                off += NHC
            dx.inc(sync.dma_start(out=scl[:, off:off + NDC], in_=s_d["s3"][:, :]), 16)
            dx.inc(sync.dma_start(out=idn[:, :], in_=self.ident_d[:, :]), 16)
            dx.inc(sync.dma_start(out=self.idn16[:, :], in_=self.idn16_d[:, :]), 16)
            dx.inc(sync.dma_start(out=qps[:, :], in_=self.qpos_d[:, :]), 16)
        self.G(sy0)
        self.n_small_loads = dx.n
        self.cast_done = {}

        def g0(g):
            g.iota(iot[:, :], [[1, 256]], channel_multiplier=0,
                   allow_small_or_imprecise_dtypes=True)
            for name in ("w1", "w2", "h1", "h2", "w3"):
                src, dst = self.w_d[name], self.wf_d[name]
                rows = src.shape[0]
                step = min(rows, 1024)
                for r0 in range(0, rows, step):
                    self.cast_done[name] = self.dct[name].inc(
                        g.dma_start(out=dst[r0:r0 + step, :],
                                    in_=src[r0:r0 + step, :]), 16)
        self.G(g0)

        def v0(v):
            dx.wait_at(v, self.n_small_loads)
            s3o = 4 * NHC
            v.tensor_scalar(scl[:, s3o:s3o + NDC], scl[:, s3o:s3o + NDC], 0.5, None,
                            op0=ALU.mult)
            v.drain()
            sv.inc(v.nop())
        self.V(v0)
        self.n_x_ready = self.n_small_loads

    def split_aw(self):
        D, D3 = self.D, self.D3
        dxa, dg, sv = self.dxa, self.dg, self.sv
        for r in range(D // 128):
            nl = {}

            def syl(sync, r=r, nl=nl):
                dg.wait_at(sync, self.ar1_free_dg)
                nl["n"] = dxa.inc(sync.dma_start(
                    out=self.fap(0, [(1, D3)]),
                    in_=self.aw_d[r * 128:(r + 1) * 128, :]), 16)
            self.G(syl)

            def vsp(v, nl=nl):
                dxa.wait_at(v, nl["n"])
                src = self.fap(0, [(1, D3)])
                hi = _ap(self.fa32r, self.FAL, D3, [(1, D3)])
                d_ = self.fap(2 * D3, [(1, D3)])
                lo = _ap(self.fa32r, self.FAL, 2 * D3, [(1, D3)])
                v.tensor_copy(hi, src)
                v.drain()
                v.tensor_tensor(d_, src, hi.bitcast(F32), op=ALU.subtract)
                v.drain()
                v.tensor_copy(lo, d_)
                v.drain()
                sv.inc(v.nop())
            self.V(vsp)
            nsv = sv.n

            def gst(g, r=r, nsv=nsv):
                sv.wait_at(g, nsv)
                hi = _ap(self.fa32r, self.FAL, D3, [(1, D3)])
                lo = _ap(self.fa32r, self.FAL, 2 * D3, [(1, D3)])
                dg.inc(g.dma_start(out=self.awhi_d[r * 128:(r + 1) * 128, :], in_=hi), 16)
                dg.inc(g.dma_start(out=self.awlo_d[r * 128:(r + 1) * 128, :], in_=lo), 16)
            self.G(gst)
            self.ar1_free_dg = dg.n
        self.n_aw_ready = dg.n

    def wt_prepass(self):
        sv, st, dg = self.sv, self.st, self.dg
        sbi = 0
        for name in ("w1", "w2", "h1", "h2", "w3"):
            R, C = self.wf_d[name].shape
            for cg in range(0, C, 512):
                ncb = min(4, (C - cg) // 128)
                for rg in range(0, R, 512):
                    nrb = min(4, (R - rg) // 128)
                    bko = 4 * (sbi % 2)
                    sbi += 1
                    for ri in range(nrb):
                        s = self.b16_next % 7
                        self.b16_next += 1

                        def pld(sy, name=name, s=s, cg=cg, rg=rg, ri=ri, ncb=ncb):
                            wait_toks(sy, [self.b16_free[s]] if self.b16_free[s] else [])
                            dg.wait_at(sy, self.n_aw_ready)
                            self.dct[name].wait_at(sy, self.cast_done[name])
                            self.b16_ready[s] = self.dt16[s].inc(sy.dma_start(
                                out=self.hap(self.WB16 + s * 512, [(1, ncb * 128)]),
                                in_=self.wf_d[name][rg + ri * 128:rg + (ri + 1) * 128,
                                                    cg:cg + ncb * 128]), 16)
                        self.SY(pld)

                        def ptr(t, s=s, ri=ri, ncb=ncb, nrb=nrb, bko=bko):
                            self.dt16[s].wait_at(t, self.b16_ready[s])
                            last = None
                            for cb in range(ncb):
                                bk = bko + cb
                                if ri == 0:
                                    self.RD[bk].wait(t)
                                last = t.transpose(
                                    _ap(self.ps[bk].bitcast(F16), 1024, ri * 128,
                                        [(1, 128)]),
                                    self.hap(self.WB16 + s * 512 + cb * 128, [(1, 128)]),
                                    self.idn16[:, :])
                                if ri == nrb - 1:
                                    self.WR[bk].inc(last)
                            if ri < nrb - 1:
                                self.b16_free[s] = (st, st.inc(last))
                            else:
                                self.b16_free[s] = (self.WR[bko + ncb - 1],
                                                    self.WR[bko + ncb - 1].n)
                        self.T(ptr)

                    for cb in range(ncb):
                        bk = bko + cb
                        ps_ = cb % 2
                        nst = {}

                        def pcv(v, bk=bk, ps_=ps_, nrb=nrb, nst=nst):
                            self.WR[bk].wait(v)
                            if self.pst_free[ps_]:
                                wait_toks(v, [self.pst_free[ps_]])
                            v.drain()
                            i = v.tensor_copy(self.pst[:, ps_ * 512:ps_ * 512 + nrb * 128],
                                              _ap(self.ps[bk].bitcast(F16), 1024, 0,
                                                  [(1, nrb * 128)]))
                            self.RD[bk].inc(i)
                            v.drain()
                            nst["v"] = sv.inc(v.nop())
                        self.V(pcv)

                        def pwr(g, name=name, ps_=ps_, cg=cg, rg=rg, cb=cb, nrb=nrb,
                                nst=nst):
                            sv.wait_at(g, nst["v"])
                            Rr = self.wf_d[name].shape[0]
                            n = self.dps[ps_].inc(g.dma_start(
                                out=bass.AP(self.wT_d[name].tensor,
                                            (cg + cb * 128) * Rr + rg,
                                            [[Rr, 128], [1, nrb * 128]]),
                                in_=self.pst[:, ps_ * 512:ps_ * 512 + nrb * 128]), 16)
                            self.pst_free[ps_] = (self.dps[ps_], n)
                        self.G(pwr)
            self.wT_tok[name] = [(self.dps[0], self.dps[0].n),
                                 (self.dps[1], self.dps[1].n)]

    def lnorm(self, tile, lo16):
        D = self.D
        sv, sa = self.sv, self.sa
        xr, red, scr, sA, idn = self.xr, self.red, self.scr, self.sA, self.idn
        half = tile % 2
        xcol = tile * D
        nch = max(1, D // 512)
        w = min(D, 512)

        def vln(v):
            self.dx.wait_at(v, self.n_x_ready)
            v.drain()
            for i in range(nch):
                v.bn_stats(red[:, 6 * i:6 * i + 6],
                           _ap(xr, self.NT * D, xcol + i * w, [(1, w)]))
            v.drain()
            v.bn_aggr(red[:, 12:14], red[:, 0:6 * nch])
            v.drain()
            v.tensor_scalar(red[:, 14:15], red[:, 13:14], EPS, None, op0=ALU.add)
            v.drain()
            sv.inc(v.reciprocal(red[:, 15:16], red[:, 14:15]))
        self.V(vln)

        def aln(a):
            sv.wait(a)
            sa.inc(a.activation(out=red[:, 16:17], in_=red[:, 15:16], func=AF.Sqrt))
        self.A(aln)

        def vn2(v):
            sa.wait(v)
            for b in range(8):
                self.WR[b].wait(v)
            v.drain()
            sv.inc(v.tensor_scalar(scr[:, 0:D], _ap(xr, self.NT * D, xcol, [(1, D)]),
                                   red[:, 12:13], red[:, 16:17],
                                   op0=ALU.subtract, op1=ALU.mult))
        self.V(vn2)
        n_ln = sv.n

        for gi in range(0, self.NDC, 4):
            grp = list(range(gi, min(self.NDC, gi + 4)))
            bk = 4 + (gi // 4) % 2

            def tt(t, grp=grp, bk=bk, n_ln=n_ln):
                sv.wait_at(t, n_ln)
                self.RD[bk].wait(t)
                last = None
                for j, fc in enumerate(grp):
                    last = t.transpose(_ap(self.ps[bk], 512, j * 128, [(1, 128)]),
                                       scr[:, fc * 128:(fc + 1) * 128], idn[:, :])
                self.WR[bk].inc(last)
            self.T(tt)

            def vev(v, grp=grp, bk=bk, half=half):
                self.WR[bk].wait(v)
                self.dg.wait_at(v, getattr(self, "n_aw_ready", 0))
                v.drain()
                dst = bass.AP(self.rp, self.LNHI + grp[0] * TB + half * 128,
                              [[self.RPL, 128], [TB, len(grp)], [1, 128]])
                i = v.tensor_copy(dst, _ap(self.ps[bk], 512, 0,
                                           [(1, len(grp) * 128)]))
                self.RD[bk].inc(i)
                v.drain()
                hi_f32 = bass.AP(self.rp, self.LNHI + grp[0] * TB + half * 128,
                                 [[self.RPL, 128], [TB, len(grp)], [1, 128]]).bitcast(F32)
                i = v.tensor_tensor(sA[:, 0:len(grp) * 128],
                                    _ap(self.ps[bk], 512, 0, [(1, len(grp) * 128)]),
                                    hi_f32, op=ALU.subtract)
                self.RD[bk].inc(i)
                v.drain()
                if lo16:
                    dst = bass.AP(self.fa16, self.LNLO16 + grp[0] * TB + half * 128,
                                  [[2 * self.FAL, 128], [TB, len(grp)], [1, 128]])
                else:
                    dst = bass.AP(self.rp, self.LNLO32 + grp[0] * TB + half * 128,
                                  [[self.RPL, 128], [TB, len(grp)], [1, 128]])
                sv.inc(v.tensor_copy(dst, sA[:, 0:len(grp) * 128]))
            self.V(vev)

    def band_load(self, wname, rows0, nrows, kc):
        s = self.b16_next % 7
        self.b16_next += 1

        def adma(sy, s=s):
            wait_toks(sy, ([self.b16_free[s]] if self.b16_free[s] else [])
                      + self.ffn_floor + self.wT_tok[wname])
            self.dg.wait_at(sy, getattr(self, "n_aw_ready", 0))
            R = self.wf_d[wname].shape[0]
            n = self.dt16[s].inc(sy.dma_start(
                out=self.hap(self.WB16 + s * 512, [(1, nrows)]),
                in_=bass.AP(self.wT_d[wname].tensor, kc * 128 * R + rows0,
                            [[R, 128], [1, nrows]])), 16)
            self.b16_ready[s] = n
        self.SY(adma)
        return s

    def band_up(self, s16, nrows):
        u = self.b32_next % 4
        self.b32_next += 1
        sv = self.sv

        def vup(v, u=u, s16=s16):
            self.dt16[s16].wait_at(v, self.b16_ready[s16])
            wait_toks(v, ([self.b32_free[u]] if self.b32_free[u] else []) + self.ffn_floor)
            i = v.tensor_copy(self.rap(self.WB32 + u * 512, [(1, nrows)]),
                              self.hap(self.WB16 + s16 * 512, [(1, nrows)]))
            self.b32_ready[u] = (sv, sv.inc(i))
        self.V(vup)
        return u

    def ffn_call(self, call):
        self.ffn_floor = (self.ffn_floor
                          + [(self.WR[b], self.WR[b].n) for b in range(8)]
                          + [(self.st, self.st.n)])
        for tb in range(self.NBLK):
            for tile in (2 * tb, 2 * tb + 1):
                self.lnorm(tile, lo16=True)
            self.ln_ready = self.sv.n
            self.ffn_floor = self.ffn_floor + self.wqk_free[0] + self.wqk_free[1]
            h_ready = self.stage(tb, "w1w2")
            g_ready = self.stage(tb, "h1", src_ready=h_ready)
            h3_ready = self.stage(tb, "h2", src_ready=g_ready)
            self.stage(tb, "w3", src_ready=h3_ready)
            if call == 0:
                self.qkv_kv(tb)
            else:
                self.store_out(tb)

    def stage(self, tb, name, src_ready=None):
        D = self.D
        sv, sa, st = self.sv, self.sa, self.st
        if name == "w1w2":
            KC, OC, OCG = self.NDC, self.NHC, 2
        elif name in ("h1", "h2"):
            KC, OC, OCG = self.NHC, self.NHC, 4
        else:
            KC, OC, OCG = self.NHC, self.NDC, 4

        def mv_hi(k):
            if name == "w1w2":
                return self.rap(self.LNHI + k * TB, [(1, TB)])
            base = self.HHI if name in ("h1", "w3") else self.GHI
            return self.rap(base + k * TB, [(1, TB)])

        def mv_lo(k):
            if name == "w1w2":
                return self.hap(self.LNLO16 + k * TB, [(1, TB)])
            base = self.HLO if name in ("h1", "w3") else self.GLO
            return self.hap(base + k * TB, [(1, TB)])

        wname0 = {"w1w2": "w1", "h1": "h1", "h2": "h2", "w3": "w3"}[name]
        out_ready = [0] * OC
        war_dg = 0
        if name == "w1w2":
            war_dg = max(self.ar1_free_dg, self.ar2_free_dg_kv)
        first_write = [True]

        def kloop(g0, banks, ocs):
            for k in range(KC):
                if name == "w1w2":
                    s1 = self.band_load("w1", g0 * 128, len(ocs) * 128, k)
                    s2 = self.band_load("w2", g0 * 128, len(ocs) * 128, k)
                    u1 = self.band_up(s1, len(ocs) * 128)
                    u2 = self.band_up(s2, len(ocs) * 128)
                    pairs = [(s1, u1), (s2, u2)]
                else:
                    s1 = self.band_load(wname0, g0 * 128, len(ocs) * 128, k)
                    u1 = self.band_up(s1, len(ocs) * 128)
                    pairs = [(s1, u1)]

                def tmm(t, k=k, pairs=pairs, banks=banks, ocs=ocs, KC=KC):
                    for (s16, u32) in pairs:
                        if self.b32_ready[u32]:
                            wait_toks(t, [self.b32_ready[u32]])
                        self.dt16[s16].wait_at(t, self.b16_ready[s16])
                    if src_ready is not None:
                        sv.wait_at(t, src_ready[k])
                    else:
                        sv.wait_at(t, self.ln_ready)
                    last = None
                    last_bk = None
                    for oi, oc in enumerate(ocs):
                        for wi, (s16, u32) in enumerate(pairs):
                            bk = banks[(oi * len(pairs) + wi) % len(banks)]
                            if k == 0:
                                self.RD[bk].wait(t)
                            po = _ap(self.ps[bk], 512, 0, [(1, TB)])
                            t.matmul(po, self.rap(self.WB32 + u32 * 512 + oi * 128,
                                                  [(1, 128)]),
                                     mv_hi(k), start=(k == 0), stop=False)
                            last = t.matmul(po, self.hap(self.WB16 + s16 * 512 + oi * 128,
                                                         [(1, 128)]),
                                            mv_lo(k), start=False, stop=(k == KC - 1))
                            last_bk = bk
                            if k == KC - 1:
                                self.WR[bk].inc(last)
                    if k == KC - 1:
                        tok = (self.WR[last_bk], self.WR[last_bk].n)
                    else:
                        tok = (st, st.inc(last))
                    for (s16, u32) in pairs:
                        self.b16_free[s16] = tok
                        self.b32_free[u32] = tok
                self.T(tmm)

        groups = []
        for g0 in range(0, OC, OCG):
            banks = [0, 1, 2, 3] if (g0 // OCG) % 2 == 0 else [4, 5, 6, 7]
            ocs = list(range(g0, min(OC, g0 + OCG)))
            groups.append((g0, banks, ocs))
        if name == "w3":
            for g in groups:
                kloop(*g)
                _, pbanks, pocs = g
                for oi, oc in enumerate(pocs):
                    wdg = war_dg if first_write[0] else 0
                    first_write[0] = False
                    out_ready[oc] = self.evac(tb, name, oc, oi, pbanks, wdg)
            return out_ready
        prev = None
        for g in groups:
            kloop(*g)
            if prev is not None:
                _, pbanks, pocs = prev
                for oi, oc in enumerate(pocs):
                    wdg = war_dg if first_write[0] else 0
                    first_write[0] = False
                    out_ready[oc] = self.evac(tb, name, oc, oi, pbanks, wdg)
            prev = g
        _, pbanks, pocs = prev
        for oi, oc in enumerate(pocs):
            wdg = war_dg if first_write[0] else 0
            first_write[0] = False
            out_ready[oc] = self.evac(tb, name, oc, oi, pbanks, wdg)
        return out_ready

    def evac(self, tb, name, oc, oi, banks, war_dg):
        sv, sa, dg = self.sv, self.sa, self.dg
        NHC = self.NHC
        if name == "w1w2":
            bka = banks[(oi * 2) % 4]
            bkb = banks[(oi * 2 + 1) % 4]

            po_ = oc % 2

            def aev(a, oc=oc, bka=bka, bkb=bkb, po_=po_):
                self.WR[bka].wait(a)
                sv.wait_at(a, self.evac_sv.get(("w", oc - 2), 0))
                i1 = a.activation(out=self.sA[:, po_ * TB:po_ * TB + TB],
                                  in_=_ap(self.ps[bka], 512, 0, [(1, TB)]),
                                  func=AF.Silu, scale=self.scl[:, oc:oc + 1])
                self.RD[bka].inc(i1)
            self.A(aev)

            def vev(v, oc=oc, bka=bka, bkb=bkb, po_=po_, war_dg=war_dg):
                self.RD[bka].wait(v)
                self.WR[bkb].wait(v)
                if war_dg:
                    dg.wait_at(v, war_dg)
                v.drain()
                i2 = v.tensor_scalar(self.sB[:, po_ * TB:po_ * TB + TB],
                                     _ap(self.ps[bkb], 512, 0, [(1, TB)]),
                                     self.scl[:, NHC + oc:NHC + oc + 1], None,
                                     op0=ALU.mult)
                self.RD[bkb].inc(i2)
                v.drain()
                v.tensor_tensor(self.scr[:, po_ * TB:po_ * TB + TB],
                                self.sA[:, po_ * TB:po_ * TB + TB],
                                self.sB[:, po_ * TB:po_ * TB + TB], op=ALU.mult)
                v.drain()
                hid_ = self.rap(self.HHI + oc * TB, [(1, TB)])
                v.tensor_copy(hid_, self.scr[:, po_ * TB:po_ * TB + TB])
                v.drain()
                v.tensor_tensor(self.sA[:, po_ * TB:po_ * TB + TB],
                                self.scr[:, po_ * TB:po_ * TB + TB], hid_.bitcast(F32),
                                op=ALU.subtract)
                v.drain()
                lod = self.hap(self.HLO + oc * TB, [(1, TB)])
                sv.inc(v.tensor_copy(lod, self.sA[:, po_ * TB:po_ * TB + TB]))
            self.V(vev)
            self.evac_sv[("w", oc)] = sv.n
            return sv.n
        if name in ("h1", "h2"):
            bk = banks[oi % 4]
            hi_base = self.GHI if name == "h1" else self.HHI
            lo_base = self.GLO if name == "h1" else self.HLO
            scol = (2 * NHC + oc) if name == "h1" else (3 * NHC + oc)

            po_ = oc % 2

            def aev(a, oc=oc, bk=bk, scol=scol, po_=po_):
                self.WR[bk].wait(a)
                sv.wait_at(a, self.evac_sv.get((name, oc - 2), 0))
                i = a.activation(out=self.scr[:, po_ * TB:po_ * TB + TB],
                                 in_=_ap(self.ps[bk], 512, 0, [(1, TB)]),
                                 func=AF.Gelu, scale=self.scl[:, scol:scol + 1])
                self.RD[bk].inc(i)
            self.A(aev)

            def vev(v, oc=oc, bk=bk, po_=po_, hi_base=hi_base, lo_base=lo_base,
                    war_dg=war_dg):
                self.RD[bk].wait(v)
                if war_dg:
                    dg.wait_at(v, war_dg)
                v.drain()
                hid_ = self.rap(hi_base + oc * TB, [(1, TB)])
                v.tensor_copy(hid_, self.scr[:, po_ * TB:po_ * TB + TB])
                v.drain()
                v.tensor_tensor(self.sA[:, po_ * TB:po_ * TB + TB],
                                self.scr[:, po_ * TB:po_ * TB + TB], hid_.bitcast(F32),
                                op=ALU.subtract)
                v.drain()
                lod = self.hap(lo_base + oc * TB, [(1, TB)])
                sv.inc(v.tensor_copy(lod, self.sA[:, po_ * TB:po_ * TB + TB]))
            self.V(vev)
            self.evac_sv[(name, oc)] = sv.n
            return sv.n
        bk = banks[oi % 4]
        s3o = 4 * NHC

        po_ = oc % 2

        def wev(v, oc=oc, bk=bk, po_=po_):
            self.WR[bk].wait(v)
            sv.wait_at(v, self.evac_sv.get(("w3", oc - 2), 0))
            self.WR[0].wait(v)
            self.WR[4].wait(v)
            v.drain()
            i = v.tensor_scalar(self.scr[:, po_ * TB:po_ * TB + TB],
                                _ap(self.ps[bk], 512, 0, [(1, TB)]),
                                self.scl[:, s3o + oc:s3o + oc + 1], None, op0=ALU.mult)
            self.RD[bk].inc(i)
        self.V(wev)
        tbk = 4 if bk < 4 else 0

        def ttr(t, bk=bk, tbk=tbk, po_=po_):
            self.RD[bk].wait(t)
            self.RD[tbk].wait(t)
            t.transpose(_ap(self.ps[tbk], 512, 0, [(1, 128)]),
                        self.scr[:, po_ * TB:po_ * TB + 128], self.idn[:, :])
            l2 = t.transpose(_ap(self.ps[tbk], 512, 128, [(1, 128)]),
                             self.scr[:, po_ * TB + 128:po_ * TB + 256], self.idn[:, :])
            self.WR[tbk].inc(l2)
        self.T(ttr)

        def vadd(v, oc=oc, tbk=tbk, tb=tb):
            self.WR[tbk].wait(v)
            v.tensor_tensor(_ap(self.xr, self.NT * self.D, 2 * tb * self.D + oc * 128,
                                [(1, 128)]),
                            _ap(self.xr, self.NT * self.D, 2 * tb * self.D + oc * 128,
                                [(1, 128)]),
                            _ap(self.ps[tbk], 512, 0, [(1, 128)]), op=ALU.add)
            i = v.tensor_tensor(_ap(self.xr, self.NT * self.D,
                                    (2 * tb + 1) * self.D + oc * 128, [(1, 128)]),
                                _ap(self.xr, self.NT * self.D,
                                    (2 * tb + 1) * self.D + oc * 128, [(1, 128)]),
                                _ap(self.ps[tbk], 512, 128, [(1, 128)]), op=ALU.add)
            self.RD[tbk].inc(i)
            v.drain()
            sv.inc(v.nop())
        self.V(vadd)
        self.evac_sv[("w3", oc)] = sv.n
        return sv.n

    def qkv_kv(self, tb):
        D, D3, NDC = self.D, self.D3, self.NDC
        sv, sa, st, dg = self.sv, self.sa, self.st, self.dg
        for tile in (2 * tb, 2 * tb + 1):
            self.lnorm(tile, lo16=False)
        n_ln2 = sv.n
        self.q_floor = (self.q_floor + [t for t in self.b16_free if t]
                        + [t for t in self.b32_free if t])
        prev_kv_dg = self.ar2_free_dg_kv

        for dc in range(NDC):
            p = dc % 2
            nld = {}

            def syb(sync, dc=dc, p=p, nld=nld):
                wait_toks(sync, self.wqk_free[p] + self.q_floor)
                dg.wait_at(sync, self.n_aw_ready)
                self.dqw[p].inc(sync.dma_start(
                    out=bass.AP(self.rp, self.QKB + p * self.PS,
                                [[self.RPL, 128], [128, NDC], [1, 128]]),
                    in_=bass.AP(self.awhi_d.tensor, D + dc * 128,
                                [[D3, 128], [128 * D3, NDC], [1, 128]])), 16)
                nld["n"] = self.dqw[p].inc(sync.dma_start(
                    out=bass.AP(self.rp, self.QKB + p * self.PS + self.D,
                                [[self.RPL, 128], [128, NDC], [1, 128]]),
                    in_=bass.AP(self.awlo_d.tensor, D + dc * 128,
                                [[D3, 128], [128 * D3, NDC], [1, 128]])), 16)
            self.G(syb)
            bk = dc % 4

            def tkm(t, dc=dc, p=p, bk=bk, nld=nld):
                self.dqw[p].wait_at(t, nld["n"])
                sv.wait_at(t, n_ln2)
                self.RD[bk].wait(t)
                po = _ap(self.ps[bk], 512, 0, [(1, TB)])
                last = None
                for k in range(NDC):
                    whi = self.rap(self.QKB + p * self.PS + k * 128, [(1, 128)])
                    wlo = self.rap(self.QKB + p * self.PS + self.D + k * 128, [(1, 128)])
                    mhi = self.rap(self.LNHI + k * TB, [(1, TB)])
                    mlo = self.rap(self.LNLO32 + k * TB, [(1, TB)])
                    t.matmul(po, whi, mhi, start=(k == 0), stop=False)
                    t.matmul(po, whi, mlo, start=False, stop=False)
                    last = t.matmul(po, wlo, mhi, start=False, stop=(k == NDC - 1))
                self.WR[bk].inc(last)
                self.wqk_free[p] = [(self.WR[bk], self.WR[bk].n)]
            self.T(tkm)

            def vkv(v, dc=dc, bk=bk, prev_kv_dg=prev_kv_dg):
                self.WR[bk].wait(v)
                dg.wait_at(v, prev_kv_dg)
                hi = self.rap(self.KVSTG + dc * TB, [(1, TB)])
                i0 = v.tensor_copy(hi, _ap(self.ps[bk], 512, 0, [(1, TB)]))
                self.RD[bk].inc(i0)
                v.drain()
                i = v.tensor_tensor(self.sA[:, 0:TB], _ap(self.ps[bk], 512, 0, [(1, TB)]),
                                    hi.bitcast(F32), op=ALU.subtract)
                self.RD[bk].inc(i)
                v.drain()
                lo = self.rap(self.KVSTG + 2 * D + dc * TB, [(1, TB)])
                v.tensor_copy(lo, self.sA[:, 0:TB])
                v.drain()
                sv.inc(v.nop())
            self.V(vkv)

        DH, NDH = self.DH, self.NDH
        for dh in range(NDH):
            kn = {}
            for k in range(NDC):
                pk = k % 2

                def syv(sync, dh=dh, k=k, pk=pk, kn=kn):
                    wait_toks(sync, self.wqk_free[pk] + self.q_floor)
                    dg.wait_at(sync, self.n_aw_ready)
                    self.dqw[pk].inc(sync.dma_start(
                        out=self.rap(self.QKB + pk * self.PS, [(1, DH)]),
                        in_=self.awhi_d[k * 128:(k + 1) * 128,
                                        2 * D + dh * DH:2 * D + (dh + 1) * DH]), 16)
                    kn[k] = self.dqw[pk].inc(sync.dma_start(
                        out=self.rap(self.QKB + pk * self.PS + self.D, [(1, DH)]),
                        in_=self.awlo_d[k * 128:(k + 1) * 128,
                                        2 * D + dh * DH:2 * D + (dh + 1) * DH]), 16)
                self.G(syv)

                def tvm(t, dh=dh, k=k, pk=pk, kn=kn):
                    self.dqw[pk].wait_at(t, kn[k])
                    sv.wait_at(t, n_ln2)
                    last = None
                    for tt in range(2):
                        bk = 4 + tt * NDH + dh
                        if k == 0:
                            self.RD[bk].wait(t)
                        po = _ap(self.ps[bk], 512, 0, [(1, DH)])
                        shi = self.rap(self.LNHI + k * TB + tt * 128, [(1, 128)])
                        slo = self.rap(self.LNLO32 + k * TB + tt * 128, [(1, 128)])
                        whi = self.rap(self.QKB + pk * self.PS, [(1, DH)])
                        wlo = self.rap(self.QKB + pk * self.PS + self.D, [(1, DH)])
                        t.matmul(po, shi, whi, start=(k == 0), stop=False)
                        t.matmul(po, slo, whi, start=False, stop=False)
                        last = t.matmul(po, shi, wlo, start=False, stop=(k == NDC - 1))
                        if k == NDC - 1:
                            self.WR[bk].inc(last)
                    if k == NDC - 1:
                        bk_last = 4 + NDH + dh
                        self.wqk_free[pk] = [(self.WR[bk_last], self.WR[bk_last].n)]
                    else:
                        self.wqk_free[pk] = [(st, st.inc(last))]
                self.T(tvm)

            for tt in range(2):
                bk = 4 + tt * NDH + dh

                def vvv(v, tt=tt, dh=dh, bk=bk, prev_kv_dg=prev_kv_dg):
                    self.WR[bk].wait(v)
                    dg.wait_at(v, prev_kv_dg)
                    hi = self.rap(self.KVSTG + 4 * D + tt * D + dh * DH, [(1, DH)])
                    i0 = v.tensor_copy(hi, _ap(self.ps[bk], 512, 0, [(1, DH)]))
                    self.RD[bk].inc(i0)
                    v.drain()
                    i = v.tensor_tensor(self.sA[:, 0:DH], _ap(self.ps[bk], 512, 0, [(1, DH)]),
                                        hi.bitcast(F32), op=ALU.subtract)
                    self.RD[bk].inc(i)
                    v.drain()
                    lo = self.rap(self.KVSTG + 6 * D + tt * D + dh * DH, [(1, DH)])
                    v.tensor_copy(lo, self.sA[:, 0:DH])
                    v.drain()
                    sv.inc(v.nop())
                self.V(vvv)
        n_v_done = sv.n

        def gkv(g, tb=tb, n_v_done=n_v_done):
            sv.wait_at(g, n_v_done)
            g.wait_ge(dg.h, dg.n)
            ksrc = lambda off: bass.AP(self.rp, off, [[self.RPL, 128], [TB, NDC], [1, TB]])
            kdst = lambda d: bass.AP(d.tensor, 0, [[TB, 128], [128 * TB, NDC], [1, TB]])
            dg.inc(g.dma_start(out=kdst(self.km["kh"][tb]), in_=ksrc(self.KVSTG)), 16)
            dg.inc(g.dma_start(out=kdst(self.km["kl"][tb]), in_=ksrc(self.KVSTG + 2 * D)), 16)
            vsrc = lambda off: bass.AP(self.rp, off, [[self.RPL, 128], [D, 2], [1, D]])
            vdst = lambda d: bass.AP(d.tensor, 0, [[D, 128], [128 * D, 2], [1, D]])
            dg.inc(g.dma_start(out=vdst(self.km["vh"][tb]), in_=vsrc(self.KVSTG + 4 * D)), 16)
            n4 = dg.inc(g.dma_start(out=vdst(self.km["vl"][tb]),
                                    in_=vsrc(self.KVSTG + 6 * D)), 16)
            g.wait_ge(dg.h, n4)
            for kind in ("kh", "kl", "vh", "vl"):
                self.cc.inc(g.collective_compute(
                    "AllGather", ALU.bypass, replica_groups=RG,
                    ins=[self.km[kind][tb][:, :]], outs=[self.ka[kind][tb][:, :]]))
        self.G(gkv)
        self.ar2_free_dg_kv = dg.n

    def attention(self):
        D, NDC, NKC, SEQ = self.D, self.NDC, self.NKC, self.SEQ
        DH, NDH = self.DH, self.NDH
        sv, sa, st, dg, cc = self.sv, self.sa, self.st, self.dg, self.cc
        n_ag_all = cc.n
        n_kv_stores = dg.n
        self.q_floor = (self.q_floor + [t for t in self.b16_free if t]
                        + [t for t in self.b32_free if t])
        init_kv = ([(st, st.n), (sv, sv.n), (sa, sa.n)]
                   + [(self.WR[b], self.WR[b].n) for b in range(8)])
        self.kv_free = [list(init_kv), list(init_kv)]
        inv_sqrt_d = float(1.0 / np.sqrt(float(D)))
        for qb in range(self.NQB):
            t0 = 2 * qb
            for tile in (t0, t0 + 1):
                self.lnorm(tile, lo16=False)
            n_ln2 = sv.n

            for dc in range(NDC):
                p = dc % 2
                nld = {}

                def syq(sync, dc=dc, p=p, nld=nld):
                    wait_toks(sync, self.wqk_free[p] + self.q_floor
                              + self.kv_free[0] + self.kv_free[1])
                    dg.wait_at(sync, self.n_aw_ready)
                    self.dqw[p].inc(sync.dma_start(
                        out=bass.AP(self.rp, self.QB2 + p * self.PS,
                                    [[self.RPL, 128], [128, NDC], [1, 128]]),
                        in_=bass.AP(self.awhi_d.tensor, dc * 128,
                                    [[self.D3, 128], [128 * self.D3, NDC], [1, 128]])), 16)
                    nld["n"] = self.dqw[p].inc(sync.dma_start(
                        out=bass.AP(self.rp, self.QB2 + p * self.PS + self.D,
                                    [[self.RPL, 128], [128, NDC], [1, 128]]),
                        in_=bass.AP(self.awlo_d.tensor, dc * 128,
                                    [[self.D3, 128], [128 * self.D3, NDC], [1, 128]])), 16)
                self.G(syq)
                bk = dc % 4

                def tqm(t, p=p, bk=bk, nld=nld):
                    self.dqw[p].wait_at(t, nld["n"])
                    sv.wait_at(t, n_ln2)
                    self.RD[bk].wait(t)
                    po = _ap(self.ps[bk], 512, 0, [(1, TB)])
                    last = None
                    for k in range(NDC):
                        whi = self.rap(self.QB2 + p * self.PS + k * 128, [(1, 128)])
                        wlo = self.rap(self.QB2 + p * self.PS + self.D + k * 128, [(1, 128)])
                        mhi = self.rap(self.LNHI + k * TB, [(1, TB)])
                        mlo = self.rap(self.LNLO32 + k * TB, [(1, TB)])
                        t.matmul(po, whi, mhi, start=(k == 0), stop=False)
                        t.matmul(po, whi, mlo, start=False, stop=False)
                        last = t.matmul(po, wlo, mhi, start=False, stop=(k == NDC - 1))
                    self.WR[bk].inc(last)
                    self.wqk_free[p] = [(self.WR[bk], self.WR[bk].n)]
                self.T(tqm)

                def vqe(v, dc=dc, bk=bk):
                    self.WR[bk].wait(v)
                    hi = self.rap(self.QH + dc * TB, [(1, TB)])
                    i0 = v.tensor_copy(hi, _ap(self.ps[bk], 512, 0, [(1, TB)]))
                    self.RD[bk].inc(i0)
                    v.drain()
                    i = v.tensor_tensor(self.sA[:, 0:TB], _ap(self.ps[bk], 512, 0, [(1, TB)]),
                                        hi.bitcast(F32), op=ALU.subtract)
                    self.RD[bk].inc(i)
                    v.drain()
                    lo = self.rap(self.QH + 2 * D + dc * TB, [(1, TB)])
                    v.tensor_copy(lo, self.sA[:, 0:TB])
                    v.drain()
                    sv.inc(v.nop())
                self.V(vqe)
            n_q_done = sv.n

            def vqs(v, t0=t0):
                for ti in range(2):
                    v.tensor_scalar(self.red[:, 32 + ti * 16:32 + ti * 16 + NKC],
                                    self.iot[:, 0:NKC], -float(TB),
                                    self.qps[:, t0 + ti:t0 + ti + 1],
                                    op0=ALU.mult, op1=ALU.add)
                v.drain()
                sv.inc(v.nop())
            self.V(vqs)
            n_qs = sv.n

            for kc in range(NKC):
                p = kc % 2
                r, sb_ = kc // self.NBLK, kc % self.NBLK
                nld = {}

                def syk(sync, p=p, r=r, sb_=sb_, nld=nld):
                    cc.wait_at(sync, n_ag_all)
                    dg.wait_at(sync, n_kv_stores)
                    wait_toks(sync, self.kv_free[p])
                    self.dkv[p].inc(sync.dma_start(
                        out=bass.AP(self.rp, self.KVS + p * self.PS,
                                    [[self.RPL, 128], [TB, NDC], [1, TB]]),
                        in_=bass.AP(self.ka["kh"][sb_].tensor, r * D * TB,
                                    [[TB, 128], [128 * TB, NDC], [1, TB]])), 16)
                    nld["n"] = self.dkv[p].inc(sync.dma_start(
                        out=bass.AP(self.rp, self.KVSL + p * self.PS,
                                    [[self.RPL, 128], [TB, NDC], [1, TB]]),
                        in_=bass.AP(self.ka["kl"][sb_].tensor, r * D * TB,
                                    [[TB, 128], [128 * TB, NDC], [1, TB]])), 16)
                self.G(syk)

                for ti in range(2):
                    bk = ti

                    def tsc(t, kc=kc, p=p, ti=ti, bk=bk, nld=nld):
                        self.dkv[p].wait_at(t, nld["n"])
                        sv.wait_at(t, n_q_done)
                        self.RD[bk].wait(t)
                        po = _ap(self.ps[bk], 512, 0, [(1, TB)])
                        last = None
                        for k in range(NDC):
                            qhi = self.rap(self.QH + k * TB + ti * 128, [(1, 128)])
                            qlo = self.rap(self.QH + 2 * D + k * TB + ti * 128, [(1, 128)])
                            khi = self.rap(self.KVS + p * self.PS + k * TB, [(1, TB)])
                            klo = self.rap(self.KVSL + p * self.PS + k * TB, [(1, TB)])
                            t.matmul(po, qhi, khi, start=(k == 0), stop=False)
                            t.matmul(po, qhi, klo, start=False, stop=False)
                            last = t.matmul(po, qlo, khi, start=False, stop=(k == NDC - 1))
                        self.WR[bk].inc(last)
                        if ti == 1:
                            self.kv_free[p] = [(self.WR[0], self.WR[0].n),
                                               (self.WR[1], self.WR[1].n)]
                    self.T(tsc)

                    def vsc(v, kc=kc, ti=ti, bk=bk):
                        sv.wait_at(v, n_qs)
                        self.WR[bk].wait(v)
                        dg.wait_at(v, self.n_aw_ready)
                        v.tensor_scalar(self.sA[:, 0:TB], self.iot[:, 0:TB],
                                        self.red[:, 32 + ti * 16 + kc:33 + ti * 16 + kc],
                                        NEG, op0=ALU.is_lt, op1=ALU.mult)
                        v.drain()
                        i = v.tensor_scalar(self.sB[:, 0:TB],
                                            _ap(self.ps[bk], 512, 0, [(1, TB)]),
                                            inv_sqrt_d, None, op0=ALU.mult)
                        self.RD[bk].inc(i)
                        v.drain()
                        v.tensor_tensor(
                            self.fap(self.SCO + ti * SEQ + kc * TB, [(1, TB)]),
                            self.sB[:, 0:TB], self.sA[:, 0:TB], op=ALU.add)
                        v.drain()
                        sv.inc(v.nop())
                    self.V(vsc)
            n_scores = sv.n

            for ti in range(2):
                base = self.SCO + ti * SEQ
                mx, ng, sm, rc = 17 + 3 * ti, 18 + 3 * ti, 19 + 3 * ti, 23 + ti

                def vsm(v, base=base, mx=mx, ng=ng):
                    sv.wait_at(v, n_scores)
                    v.reduce_max(self.red[:, mx:mx + 1], self.fap(base, [(1, SEQ)]),
                                 axis=AXX)
                    v.drain()
                    v.tensor_scalar(self.red[:, ng:ng + 1], self.red[:, mx:mx + 1], -1.0,
                                    None, op0=ALU.mult)
                    v.drain()
                    sv.inc(v.nop())
                self.V(vsm)

                def asm(a, base=base, ng=ng, sm=sm):
                    sv.wait(a)
                    a.activation(out=self.fap(base, [(1, SEQ)]),
                                 in_=self.fap(base, [(1, SEQ)]),
                                 func=AF.Exp, bias=self.red[:, ng:ng + 1], scale=1.0,
                                 accum_out=self.red[:, sm:sm + 1])
                    a.drain()
                    sa.inc(a.nop())
                self.A(asm)

                def vrc(v, sm=sm, rc=rc):
                    sa.wait(v)
                    v.reciprocal(self.red[:, rc:rc + 1], self.red[:, sm:sm + 1])
                    v.drain()
                    sv.inc(v.nop())
                self.V(vrc)

                def vpb(v, base=base, rc=rc):
                    v.tensor_scalar(self.fap(base, [(1, SEQ)]),
                                    self.fap(base, [(1, SEQ)]),
                                    self.red[:, rc:rc + 1], None, op0=ALU.mult)
                    v.drain()
                    sv.inc(v.nop())
                self.V(vpb)
            n_probs_sv = sv.n

            for kc in range(NKC):
                p = kc % 2
                r, sb_ = kc // self.NBLK, kc % self.NBLK
                nld = {}

                def syv(sync, p=p, r=r, sb_=sb_, nld=nld):
                    dg.wait_at(sync, n_kv_stores)
                    wait_toks(sync, self.kv_free[p])
                    self.dkv[p].inc(sync.dma_start(
                        out=bass.AP(self.rp, self.KVS + p * self.PS,
                                    [[self.RPL, 128], [D, 2], [1, D]]),
                        in_=bass.AP(self.ka["vh"][sb_].tensor, r * TB * D,
                                    [[D, 128], [128 * D, 2], [1, D]])), 16)
                    nld["n"] = self.dkv[p].inc(sync.dma_start(
                        out=bass.AP(self.rp, self.KVSL + p * self.PS,
                                    [[self.RPL, 128], [D, 2], [1, D]]),
                        in_=bass.AP(self.ka["vl"][sb_].tensor, r * TB * D,
                                    [[D, 128], [128 * D, 2], [1, D]])), 16)
                self.G(syv)

                for ti in range(2):
                    base = self.SCO + ti * SEQ
                    tbk = 4 + (kc % 2) * 2 + ti
                    slot = (kc % 2) * 2 + ti
                    aT_hi = self.ATT + slot * 256
                    aT_lo = self.ATT + 1024 + slot * 256

                    def ttr(t, kc=kc, base=base, tbk=tbk):
                        sv.wait_at(t, n_probs_sv)
                        self.RD[tbk].wait(t)
                        t.transpose(_ap(self.ps[tbk], 512, 0, [(1, 128)]),
                                    self.fap(base + kc * TB, [(1, 128)]),
                                    self.idn[:, :])
                        l2 = t.transpose(_ap(self.ps[tbk], 512, 128, [(1, 128)]),
                                         self.fap(base + kc * TB + 128, [(1, 128)]),
                                         self.idn[:, :])
                        self.WR[tbk].inc(l2)
                    self.T(ttr)

                    def vtv(v, tbk=tbk, aT_hi=aT_hi, aT_lo=aT_lo):
                        self.WR[tbk].wait(v)
                        hi = self.rap(aT_hi, [(1, 256)])
                        i0 = v.tensor_copy(hi, _ap(self.ps[tbk], 512, 0, [(1, 256)]))
                        self.RD[tbk].inc(i0)
                        v.drain()
                        i = v.tensor_tensor(self.sA[:, 0:256],
                                            _ap(self.ps[tbk], 512, 0, [(1, 256)]),
                                            hi.bitcast(F32), op=ALU.subtract)
                        self.RD[tbk].inc(i)
                        v.drain()
                        v.tensor_copy(self.rap(aT_lo, [(1, 256)]), self.sA[:, 0:256])
                        v.drain()
                        sv.inc(v.nop())
                    self.V(vtv)
                    n_at = sv.n

                    for dh in range(NDH):
                        avbk = ti * NDH + dh

                        def tav(t, kc=kc, ti=ti, dh=dh, p=p, avbk=avbk, n_at=n_at,
                                aT_hi=aT_hi, aT_lo=aT_lo, nld=nld):
                            sv.wait_at(t, n_at)
                            self.dkv[p].wait_at(t, nld["n"])
                            if kc == 0:
                                self.RD[avbk].wait(t)
                            po = _ap(self.ps[avbk], 512, 0, [(1, DH)])
                            last = None
                            for kk in range(2):
                                ahi = self.rap(aT_hi + kk * 128, [(1, 128)])
                                alo = self.rap(aT_lo + kk * 128, [(1, 128)])
                                vhi = self.rap(self.KVS + p * self.PS + kk * D + dh * DH,
                                               [(1, DH)])
                                vlo = self.rap(self.KVSL + p * self.PS + kk * D + dh * DH,
                                               [(1, DH)])
                                first = (kc == 0 and kk == 0)
                                lastf = (kc == NKC - 1 and kk == 1)
                                t.matmul(po, ahi, vhi, start=first, stop=False)
                                t.matmul(po, ahi, vlo, start=False, stop=False)
                                last = t.matmul(po, alo, vhi, start=False, stop=lastf)
                            if kc == NKC - 1:
                                self.WR[avbk].inc(last)
                            elif dh == NDH - 1:
                                n = st.inc(last)
                                if ti == 1:
                                    self.kv_free[p] = [(st, n)]
                        self.T(tav)

            self.kv_free[(NKC - 1) % 2] = [
                (self.WR[b], self.WR[b].n) for b in range(2 * NDH)]

            for ti in range(2):
                for dh in range(NDH):
                    avbk = ti * NDH + dh

                    def vav(v, ti=ti, dh=dh, avbk=avbk, t0=t0):
                        self.WR[avbk].wait(v)
                        tile = t0 + ti
                        dst = _ap(self.xr, self.NT * D, tile * D + dh * DH, [(1, DH)])
                        last = v.tensor_tensor(dst, dst,
                                               _ap(self.ps[avbk], 512, 0, [(1, DH)]),
                                               op=ALU.add)
                        self.RD[avbk].inc(last)
                        v.drain()
                        sv.inc(v.nop())
                    self.V(vav)

    def store_out(self, tb):
        sv, dg = self.sv, self.dg
        n_ready = sv.n

        def gst(g, tb=tb, n_ready=n_ready):
            sv.wait_at(g, n_ready)
            g.wait_ge(dg.h, dg.n)
            for h2_ in range(2):
                tile = 2 * tb + h2_
                dg.inc(g.dma_start(
                    out=self.out_d[tile * 128:(tile + 1) * 128, :],
                    in_=_ap(self.xr, self.NT * self.D, tile * self.D, [(1, self.D)])), 16)
        self.G(gst)

    def finish(self):
        dg = self.dg

        def gf(g):
            g.wait_ge(dg.h, dg.n)
        self.G(gf)


def build_nc():
    return Build().build()


def kernel(L0_attn_x, L0_attn_W, L0_ffn_w1_w, L0_ffn_w1_s, L0_ffn_w2_w,
           L0_ffn_w2_s, L0_ffn_h1_w, L0_ffn_h1_s, L0_ffn_h2_w, L0_ffn_h2_s,
           L0_ffn_w3_w, L0_ffn_w3_s):
    D, SEQ, TOK = CFG["D"], CFG["SEQ"], CFG["TOK"]
    NT = TOK // 128
    x = np.ascontiguousarray(np.asarray(L0_attn_x, dtype=np.float32))
    aw = np.ascontiguousarray(np.asarray(L0_attn_W, dtype=np.float32))
    w = {n: np.ascontiguousarray(np.asarray(a, dtype=np.float32)) for n, a in
         (("w1", L0_ffn_w1_w), ("w2", L0_ffn_w2_w), ("h1", L0_ffn_h1_w),
          ("h2", L0_ffn_h2_w), ("w3", L0_ffn_w3_w))}
    s = {n: np.ascontiguousarray(np.asarray(a, dtype=np.float32).reshape(-1, 128).T)
         for n, a in (("s1", L0_ffn_w1_s), ("s2", L0_ffn_w2_s), ("sh1", L0_ffn_h1_s),
                      ("sh2", L0_ffn_h2_s), ("s3", L0_ffn_w3_s))}
    ident = np.eye(128, dtype=np.float32)
    ident16 = np.eye(128, dtype=np.float16)
    nc = build_nc()
    in_maps = []
    for c in range(CFG["NCORE"]):
        b, h = c % 4, c // 4
        qpos = (h * TOK + np.arange(NT)[None, :] * 128
                + np.arange(128)[:, None]).astype(np.float32)
        m = {"x": np.ascontiguousarray(x[b, h * TOK:(h + 1) * TOK, :]), "aw": aw,
             "ident": ident, "idn16": ident16, "qpos": np.ascontiguousarray(qpos)}
        for n in w:
            m[n] = w[n]
        for n in s:
            m[n + "c"] = s[n]
        in_maps.append(m)
    res = run_bass_kernel_spmd(nc, in_maps, core_ids=list(range(CFG["NCORE"])))
    out = np.empty((4, SEQ, D), np.float32)
    for c, r in enumerate(res.results):
        b, h = c % 4, c // 4
        out[b, h * TOK:(h + 1) * TOK, :] = r["out"]
    return out
